# version 02
# baseline (speedup 1.0000x reference)
import math
import numpy as np
import ml_dtypes
from contextlib import ExitStack

import concourse.bass as bass
import concourse.mybir as mybir
from concourse.bass_utils import run_bass_kernel_spmd
from concourse.alu_op_type import AluOpType as ALU

F32 = mybir.dt.float32
BF16 = mybir.dt.bfloat16
I32 = mybir.dt.int32
AF = mybir.ActivationFunctionType
IOff = bass.IndirectOffsetOnAxis
BF = ml_dtypes.bfloat16
AXX = mybir.AxisListType.X

D = 1024
NH = 16
HD = 64
LN_EPS = 1e-5
RMS_EPS = 1e-6


class Task:
    pass


class DmaStream:
    def __init__(self, ctx, nc, name, R):
        self.sems = [ctx.enter_context(nc.semaphore(f"{name}{i}")) for i in range(R)]
        self.R = R
        self.counts = [0] * R
        self.ntasks = 0


class Prog:
    CQUEUES = ["tensor", "vector", "scalar", "gpsimd"]

    def __init__(self, nc, ctx, pfx=""):
        self.nc = nc
        self.ctx = ctx
        self.pfx = pfx
        self.tasks = []
        self.chans = {e: ctx.enter_context(nc.semaphore(f"{pfx}ch_{e}")) for e in Prog.CQUEUES}
        self.ccounts = {e: 0 for e in Prog.CQUEUES}
        self.hz = {e: ctx.enter_context(nc.semaphore(f"{pfx}hz_{e}")) for e in Prog.CQUEUES}
        self.hzc = {e: 0 for e in Prog.CQUEUES}
        self.streams = []

    def fence(self, ename, eng, inst):
        self.hzc[ename] += 1
        inst.then_inc(self.hz[ename], 1)
        eng.wait_ge(self.hz[ename], self.hzc[ename])

    def stream(self, name, R=2):
        s = DmaStream(self.ctx, self.nc, self.pfx + name, R)
        self.streams.append(s)
        return s

    def add(self, engine, fn, deps=(), ndma=None, stream=None):
        t = Task()
        t.engine = engine
        t.fn = fn
        t.deps = [d for d in deps if d is not None]
        if ndma is not None:
            assert stream is not None, "dma task needs a stream"
            t.kind = "d"
            t.stream = stream
            t.nev = ndma
            t.sem_idx = stream.ntasks % stream.R
            stream.ntasks += 1
            stream.counts[t.sem_idx] += ndma
            t.wait_count = stream.counts[t.sem_idx] * 16
        else:
            assert engine != "sync"
            t.kind = "c"
            t.stream = None
            self.ccounts[engine] += 1
            t.wait_count = self.ccounts[engine]
        self.tasks.append(t)
        return t

    def emit(self, blk, open_waits=()):
        queues = [("sync", None)] + [(e, None) for e in ["gpsimd", "tensor", "vector", "scalar"]]
        for ename, _ in queues:
            if ename == "sync":
                mine = [t for t in self.tasks if t.engine == "sync"]
            else:
                mine = [t for t in self.tasks if t.engine == ename]
            if not mine and not open_waits:
                continue
            section = getattr(blk, ename)

            def make(mine=mine, ename=ename):
                def body(eng):
                    for sem, cnt in open_waits:
                        eng.wait_ge(sem, cnt)
                    for t in mine:
                        waits = {}
                        for d in t.deps:
                            if d.kind == "c":
                                if d.engine == t.engine and t.kind == "c":
                                    continue
                                key = self.chans[d.engine]
                            else:
                                if (t.kind == "d" and d.stream is t.stream
                                        and d.sem_idx == t.sem_idx):
                                    pass
                                key = d.stream.sems[d.sem_idx]
                            waits[key] = max(waits.get(key, 0), d.wait_count)
                        for key, cnt in waits.items():
                            eng.wait_ge(key, cnt)
                        out = t.fn(eng)
                        insts = out if isinstance(out, (list, tuple)) else [out]
                        if t.kind == "d":
                            assert len(insts) == t.nev, (ename, len(insts), t.nev)
                            sem = t.stream.sems[t.sem_idx]
                            for inst in insts:
                                inst.then_inc(sem, 16)
                        else:
                            insts[-1].then_inc(self.chans[t.engine], 1)
                return body
            section(make())

    def barrier_deps(self):
        out = []
        last_c = {}
        for t in self.tasks:
            if t.kind == "c":
                last_c[t.engine] = t
        out.extend(last_c.values())
        for s in self.streams:
            seen = set()
            for t in reversed(self.tasks):
                if t.kind == "d" and t.stream is s and t.sem_idx not in seen:
                    seen.add(t.sem_idx)
                    out.append(t)
                if len(seen) == s.R:
                    break
        return out


def build_l0_attn(T=4096, debug=False, fu=None):
    NTT = T // 128
    NQC = T // 512
    pfx = fu["pfx"] if fu else ""
    nc = fu["nc"] if fu else bass.Bass("TRN2", target_bir_lowering=False)
    DT = lambda name, *a, **k: nc.dram_tensor(pfx + name, *a, **k)

    if fu:
        x_in, xout = fu["x_in"], fu["x_out"]
    else:
        x_in = DT("x", [T, D], F32, kind="ExternalInput").ap()
        xout = DT("xout", [T, D], F32, kind="ExternalOutput").ap()
    wq_in = DT("wq", [D, D], F32, kind="ExternalInput").ap()
    wk_in = DT("wk", [D, D], F32, kind="ExternalInput").ap()
    wv_in = DT("wv", [D, D], F32, kind="ExternalInput").ap()
    wo_in = DT("wo", [D, D], F32, kind="ExternalInput").ap()
    qnw_in = DT("qnw", [1, 512], F32, kind="ExternalInput").ap()
    knw_in = DT("knw", [1, 512], F32, kind="ExternalInput").ap()
    cos_in = DT("cosr", [T, 32], F32, kind="ExternalInput").ap()
    sin_in = DT("sinr", [T, 32], F32, kind="ExternalInput").ap()
    idb_in = DT("idb", [128, 128], BF16, kind="ExternalInput").ap()
    idf_in = DT("idf", [128, 128], F32, kind="ExternalInput").ap()
    wedge_in = DT("wedge", [128, 896], BF16, kind="ExternalInput").ap()
    cval_in = DT("cval", [1, 8], F32, kind="ExternalInput").ap()

    dbg = dict(kind="ExternalOutput") if debug else {}
    xlnT_h = DT("xlnT_h", [NTT, 128, D], BF16, **dbg).ap()
    avnf_h = DT("avnf_h", [8, NTT, 128, 128], BF16, **dbg).ap()
    negm_h = DT("negm_h", [32, 128], F32, **dbg).ap()

    with ExitStack() as ctx:
        S = lambda name, shape, dt: ctx.enter_context(nc.sbuf_tensor(pfx + name, shape, dt))
        idb = S("idb_s", [128, 128], BF16)
        idf = S("idf_s", [128, 128], F32)
        wedge = S("wedge_s", [128, 896], BF16)
        cvals = S("cvals", [128, 8], F32)
        wqrep = S("wqrep", [128, 512], F32)
        wkrep = S("wkrep", [128, 512], F32)
        cosr = S("cosr_s", [128, NTT * 32], F32)
        sinr = S("sinr_s", [128, NTT * 32], F32)
        xbuf = S("xbuf", [128, D], F32)
        stat = S("stat", [128, 16], F32)
        sqk = S("sqk", [128, 512], F32)
        xlnbf = S("xlnbf", [128, D], BF16)
        stg = S("stg", [128, D], BF16)
        wbuf = S("wbuf", [128, 4096], BF16)
        xts = S("xts", [128, D], BF16)
        qks = S("qks", [128, 2 * 512], F32)
        ropa = S("ropa", [128, 2 * 256], F32)
        qtok = S("qtok", [128, 512], BF16)
        ktok = S("ktok", [128, 512], BF16)
        rms = S("rms", [128, 32], F32)
        qf = S("qf", [128, 4 * T], BF16)
        kf = S("kf", [128, 4 * T], BF16)
        vg = S("vg", [128, NTT * 520], BF16)
        kst = S("kst", [80, T], BF16)
        qst = S("qst", [80, T], BF16)
        astat = S("astat", [128, 32], F32)
        astT = S("astT", [32, 128], F32)
        pt = S("pt", [128, 2 * 1024], BF16)
        av65 = S("av65", [80, 512], F32)
        avtok = S("avtok", [128, NTT * 512], BF16)
        ztok = S("ztok", [128, NTT * 8], F32)
        rzs = S("rzs", [128, 8], F32)
        avn = S("avn", [128, 512], BF16)
        avstg = S("avstg", [128, 512], BF16)
        avf = S("avf", [128, 2 * D], BF16)
        obuf = S("obuf", [128, 512], F32)

        P_ = lambda name, shape, dt: ctx.enter_context(nc.psum_tensor(pfx + name, shape, dt))
        ps_tr = P_("ps_tr", [128, 2048], BF16)
        ps_big = P_("ps_big", [128, 2048], F32)
        ps_av = P_("ps_av", [80, 512], F32)
        ps_ms = P_("ps_ms", [128, 128], F32)
        ps_q = ps_big[:, 0:512]
        ps_k = ps_big[:, 512:1024]
        ps_v = ps_big[:, 1024:1536]

        pr = Prog(nc, fu["sem_ctx"] if fu else ctx, pfx)
        s_const = pr.stream("s_const", 1)
        s_lx = pr.stream("s_lx", 2)
        s_stg = pr.stream("s_stg", 2)
        s_w = pr.stream("s_w", 1)
        s_xt = pr.stream("s_xt", 2)
        s_negm = pr.stream("s_negm", 2)
        s_gav = pr.stream("s_gav", 2)
        s_lwo = pr.stream("s_lwo", 1)
        s_lav = pr.stream("s_lav", 2)
        s_lxo = pr.stream("s_lxo", 2)
        s_go = pr.stream("s_go", 2)
        blk = fu["blk"] if fu else ctx.enter_context(nc.Block())

        def t_const(sy):
            out = []
            for a, b in [(idb, idb_in), (idf, idf_in), (wedge, wedge_in)]:
                out.append(sy.dma_start(a[:], b[:]))
            out.append(sy.dma_start(cvals[:], cval_in.to_broadcast([128, 8])))
            out.append(sy.dma_start(wqrep[:], qnw_in.to_broadcast([128, 512])))
            out.append(sy.dma_start(wkrep[:], knw_in.to_broadcast([128, 512])))
            for t_in, t_sb in [(cos_in, cosr), (sin_in, sinr)]:
                out.append(sy.dma_start(
                    t_sb[:].rearrange("p (tt i) -> p tt i", i=32),
                    t_in.rearrange("(tt p) i -> p tt i", p=128)))
            return out
        CONST = pr.add("sync", t_const, ndma=8, stream=s_const)
        vg4 = vg[:].rearrange("p (tt h c) -> p tt h c", h=8, c=65)
        ones_set = pr.add("vector", lambda v: [
            v.memset(kst[64:65, :], 1.0),
            v.memset(vg4[:, :, :, 64:65], 1.0)], deps=[])

        ln_done = [None] * NTT
        vr_done = [None] * NTT
        tp_done = [None] * NTT
        ev_done = [None] * NTT
        st_done = [None] * NTT
        sqs = qks[:, 0:D]
        for tt in range(NTT):
            xb = xbuf[:, 0:D]
            xlb = xlnbf[:, 0:D]
            ld = pr.add("sync", lambda sy, xb=xb, tt=tt: sy.dma_start(
                xb, x_in[tt * 128:(tt + 1) * 128, :]), ndma=1, stream=s_lx,
                deps=[ln_done[tt - 1] if tt >= 1 else None])
            sq_t = pr.add("scalar", lambda s, xb=xb: s.activation(
                sqs[:], xb, AF.Square),
                deps=[ld, vr_done[tt - 1] if tt >= 1 else None])
            def vr_fn(v, xb=xb):
                pr.fence("vector", v, v.reduce_sum(stat[:, 0:1], xb, axis=AXX))
                pr.fence("vector", v, v.reduce_sum(stat[:, 1:2], sqs[:], axis=AXX))
                pr.fence("vector", v, v.tensor_scalar(
                    stat[:, 2:3], stat[:, 0:1], 1.0 / D, None, op0=ALU.mult))
                pr.fence("vector", v, v.tensor_tensor(
                    stat[:, 3:4], stat[:, 2:3], stat[:, 2:3], ALU.mult))
                return v.scalar_tensor_tensor(stat[:, 4:5], stat[:, 1:2], 1.0 / D,
                                              stat[:, 3:4], op0=ALU.mult,
                                              op1=ALU.subtract)
            vr_done[tt] = pr.add("vector", vr_fn,
                deps=[ld, sq_t, ln_done[tt - 1] if tt >= 1 else None])
            def lnp_fn(s):
                pr.fence("scalar", s, s.activation(stat[:, 5:6], stat[:, 4:5],
                                                   AF.Ln, bias=cvals[:, 0:1]))
                pr.fence("scalar", s,
                         s.activation(stat[:, 6:7], stat[:, 5:6], AF.Exp, scale=-0.5))
                pr.fence("scalar", s, s.activation(stat[:, 7:8], stat[:, 2:3],
                                                   AF.Copy, scale=stat[:, 6:7]))
                return s.mul(stat[:, 7:8], stat[:, 7:8], -1.0)
            lnp = pr.add("scalar", lnp_fn, deps=[vr_done[tt], CONST])
            ln_done[tt] = pr.add("vector", lambda v, xb=xb, xlb=xlb: v.scalar_tensor_tensor(
                xlb, xb, stat[:, 6:7], stat[:, 7:8].to_broadcast([128, D]),
                op0=ALU.mult, op1=ALU.add),
                deps=[lnp, tp_done[tt - 1] if tt >= 1 else None])
            half = (tt % 2) * 1024
            tp_done[tt] = pr.add("tensor", lambda t, xlb=xlb, half=half: [
                t.transpose(ps_tr[:, half + dc * 128:half + (dc + 1) * 128],
                            xlb[:, dc * 128:(dc + 1) * 128], idb[:])
                for dc in range(8)],
                deps=[CONST, ln_done[tt], ev_done[tt - 2] if tt >= 2 else None])
            ev_done[tt] = pr.add("vector", lambda v, half=half: v.tensor_copy(
                stg[:], ps_tr[:, half:half + 1024]),
                deps=[tp_done[tt], st_done[tt - 1] if tt >= 1 else None])
            st_done[tt] = pr.add("gpsimd", lambda g, tt=tt: g.dma_start(
                xlnT_h[tt, :, :], stg[:]), ndma=1, stream=s_stg, deps=[ev_done[tt]])

        gav_done = []
        last_evA = None
        mm_all = []
        last_eBh = {0: None, 1: None}
        last_red = None
        for g in range(2):
            gofs = g * 512
            prev_grp = pr.barrier_deps() if g == 1 else []
            p1_tail = [ev_done[-1], ev_done[-2]] if g == 0 else []
            wld = [None] * 3
            mmv = {}
            psum_free = {}
            tqk_done = {}
            evq_done = {}
            for wi, w_in in enumerate([wq_in, wk_in, wv_in]):
                wld[wi] = pr.add("gpsimd", lambda gp, w_in=w_in, gofs=gofs: [
                    gp.dma_start(wbuf[:, dc * 512:(dc + 1) * 512],
                                 w_in[dc * 128:(dc + 1) * 128, gofs:gofs + 512])
                    for dc in range(8)], ndma=8, stream=s_w,
                    deps=([mmv[(wi - 1, NTT - 1)]] if wi > 0 else []) + prev_grp
                         + p1_tail)
                for tt in range(NTT):
                    git = len(mm_all)
                    lx = pr.add("sync", lambda sy, tt=tt: sy.dma_start(
                        xts[:, 0:D], xlnT_h[tt, :, :]), ndma=1,
                        stream=s_xt,
                        deps=[st_done[tt], mm_all[git - 1] if git >= 1 else None])
                    psd = [ps_q, ps_k, ps_v][wi]
                    mmv[(wi, tt)] = pr.add("tensor", lambda t, psd=psd: [
                        t.matmul(psd, xts[:, dc * 128:(dc + 1) * 128],
                                 wbuf[:, dc * 512:(dc + 1) * 512],
                                 start=(dc == 0), stop=(dc == 7), skip_group_check=True)
                        for dc in range(8)],
                        deps=[wld[wi], lx, psum_free.get((wi, tt - 1))])
                    mm_all.append(mmv[(wi, tt)])
                    if wi < 2:
                        wrep = wqrep if wi == 0 else wkrep
                        rcol, scol = (16, 0) if wi == 0 else (24, 8)
                        sq2 = pr.add("scalar", lambda s, psd=psd: s.activation(
                            sqk[:], psd, AF.Square),
                            deps=[mmv[(wi, tt)], last_red])
                        red = pr.add("vector", lambda v, scol=scol: v.tensor_reduce(
                            rms[:, scol:scol + 8],
                            sqk[:].rearrange("p (h d) -> p h d", d=HD),
                            axis=AXX, op=ALU.add), deps=[sq2])
                        last_red = red
                        def rst_fn(s, rcol=rcol, scol=scol, wi=wi):
                            pr.fence("scalar", s, s.activation(
                                rms[:, rcol:rcol + 8], rms[:, scol:scol + 8],
                                AF.Ln, scale=1.0 / HD, bias=cvals[:, 1:2]))
                            last = s.activation(rms[:, rcol:rcol + 8],
                                                rms[:, rcol:rcol + 8],
                                                AF.Exp, scale=-0.5)
                            if wi == 0:
                                pr.fence("scalar", s, last)
                                last = s.mul(rms[:, rcol:rcol + 8],
                                             rms[:, rcol:rcol + 8], 0.125)
                            return last
                        rst = pr.add("scalar", rst_fn, deps=[red, CONST])
                        tok = qtok if wi == 0 else ktok
                        cs = cosr[:, tt * 32:(tt + 1) * 32]
                        sn = sinr[:, tt * 32:(tt + 1) * 32]

                        def rope_fn(v, psd=psd, wrep=wrep, rcol=rcol, tok=tok,
                                    cs=cs, sn=sn):
                            rr = rms[:, rcol:rcol + 8]
                            qh = qks[:, 0:512]
                            ii = []
                            ii.append(v.tensor_tensor(
                                qks[:, 512:1024].rearrange("p (h d) -> p h d", d=64),
                                psd.rearrange("p (h d) -> p h d", d=64),
                                rr.rearrange("p (h o) -> p h o", o=1)
                                  .to_broadcast([128, 8, 64]), ALU.mult))
                            ii.append(v.tensor_tensor(qh, qks[:, 512:1024], wrep[:],
                                                      ALU.mult))
                            q3 = qh.rearrange("p (h t i) -> p h t i", t=2, i=32)
                            o3 = tok.rearrange("p (h t i) -> p h t i", t=2, i=32)
                            cs3 = cs.rearrange("p (o i) -> p o i", o=1).to_broadcast([128, 8, 32])
                            sn3 = sn.rearrange("p (o i) -> p o i", o=1).to_broadcast([128, 8, 32])
                            a = ropa[:, 0:256].rearrange("p (h i) -> p h i", i=32)
                            b = ropa[:, 256:512].rearrange("p (h i) -> p h i", i=32)
                            ii.append(v.tensor_tensor(a, q3[:, :, 0, :], cs3, ALU.mult))
                            ii.append(v.tensor_tensor(b, q3[:, :, 1, :], sn3, ALU.mult))
                            ii.append(v.tensor_tensor(o3[:, :, 0, :], a, b, ALU.subtract))
                            ii.append(v.tensor_tensor(a, q3[:, :, 1, :], cs3, ALU.mult))
                            ii.append(v.tensor_tensor(b, q3[:, :, 0, :], sn3, ALU.mult))
                            ii.append(v.tensor_tensor(o3[:, :, 1, :], a, b, ALU.add))
                            return ii
                        rp = pr.add("vector", rope_fn,
                                    deps=[rst, mmv[(wi, tt)], CONST,
                                          tqk_done.get((wi, tt - 1))])
                        psum_free[(wi, tt)] = rp
                        half2 = (tt % 2) * 1024
                        tqk_done[(wi, tt)] = pr.add("tensor", lambda t, tok=tok,
                                                    half2=half2: [
                            t.transpose(ps_tr[:, half2 + s4 * 128:half2 + (s4 + 1) * 128],
                                        tok[:, s4 * 128:(s4 + 1) * 128], idb[:])
                            for s4 in range(4)],
                            deps=[rp, evq_done.get((wi, tt - 2))])
                        dstf = qf if wi == 0 else kf
                        evq_done[(wi, tt)] = pr.add("vector", lambda v, dstf=dstf,
                                                    half2=half2, tt=tt: [
                            v.tensor_copy(dstf[:, rt * T + tt * 128:rt * T + (tt + 1) * 128],
                                          ps_tr[:, half2 + rt * 128:half2 + (rt + 1) * 128])
                            for rt in range(4)],
                            deps=[tqk_done[(wi, tt)]])
                    else:
                        psum_free[(wi, tt)] = pr.add(
                            "vector", lambda v, tt=tt: v.tensor_copy(
                                vg4[:, tt, :, 0:64],
                                ps_v.rearrange("p (h d) -> p h d", d=64)),
                            deps=[mmv[(wi, tt)], ones_set])

            qkv_done = pr.barrier_deps()

            last_mB = None
            last_psms = None
            last_stats = None
            for hg in range(8):
                rt, sub = hg // 2, hg % 2
                p0 = sub * 64
                stg_k = pr.add("vector", lambda v, rt=rt, p0=p0: v.tensor_copy(
                    kst[0:64, :], kf[p0:p0 + 64, rt * T:(rt + 1) * T]),
                    deps=qkv_done + [ones_set, last_mB])
                stg_q = pr.add("vector", lambda v, rt=rt, p0=p0: v.tensor_copy(
                    qst[0:64, :], qf[p0:p0 + 64, rt * T:(rt + 1) * T]),
                    deps=qkv_done + [last_mB])
                rmx = [None] * NTT
                for qt in range(NTT):
                    qtend = (qt + 1) * 128
                    w0 = max(0, qtend - 512)
                    wid = qtend - w0
                    mA = pr.add("tensor", lambda t, qt=qt, w0=w0, wid=wid: t.matmul(
                        ps_big[:, 1536:1536 + wid],
                        qst[0:64, qt * 128:(qt + 1) * 128],
                        kst[0:64, w0:w0 + wid], start=True, stop=True,
                        skip_group_check=True),
                        deps=[stg_k, stg_q, rmx[qt - 1] if qt >= 1 else None,
                              last_eBh[1]])
                    rmx[qt] = pr.add("vector", lambda v, qt=qt, wid=wid: v.reduce_max(
                        astat[:, qt:qt + 1], ps_big[:, 1536:1536 + wid],
                        axis=AXX, negate=True),
                        deps=[mA, last_stats])
                tst = pr.add("tensor", lambda t: t.transpose(
                    ps_ms[0:NTT, 0:128] if NTT < 128 else ps_ms[:, 0:128],
                    astat[:, 0:NTT], idf[:]),
                    deps=[CONST, last_psms] + rmx)
                est = pr.add("vector", lambda v: v.tensor_copy(
                    astT[0:NTT, :], ps_ms[0:NTT, 0:128]), deps=[tst])
                last_psms = est
                gss = pr.add("gpsimd", lambda g_: g_.dma_start(
                    negm_h[0:NTT, :], astT[0:NTT, :]), ndma=1, stream=s_negm,
                    deps=[est])
                gmr = pr.add("gpsimd", lambda g_: g_.dma_start(
                    qst[64:65, 0:T].rearrange("p (a b) -> p a b", b=128),
                    negm_h[0:NTT, :].rearrange("(o a) b -> o a b", o=1)),
                    ndma=1, stream=s_negm, deps=[gss, stg_q])
                last_stats = gmr

                for qc in range(NQC):
                    nkt = (qc + 1) * 4
                    npair = (nkt + 1) // 2
                    eB = {}
                    mAV = {}
                    for pi in range(npair):
                        kts = [k2 for k2 in (2 * pi, 2 * pi + 1) if k2 < nkt]
                        pb = pi % 2
                        pso = 1024 * pb

                        def mb_fn(t, kts=kts, pso=pso, qc=qc):
                            return [t.matmul(
                                ps_big[:, pso + j * 512:pso + (j + 1) * 512],
                                kst[0:65, kt * 128:(kt + 1) * 128],
                                qst[0:65, qc * 512:(qc + 1) * 512],
                                start=True, stop=True, skip_group_check=True)
                                for j, kt in enumerate(kts)]
                        mB = pr.add("tensor", mb_fn,
                                    deps=[gmr, last_eBh[pb], mAV.get(pi - 1)])
                        last_mB = mB

                        def eb_fn(s, n=len(kts), pso=pso, pb=pb):
                            return s.activation(
                                pt[:, pb * 1024:pb * 1024 + n * 512],
                                ps_big[:, pso:pso + n * 512], AF.Exp)
                        eB[pi] = pr.add("scalar", eb_fn,
                                        deps=[mB, mAV.get(pi - 2)])
                        last_eBh[pb] = eB[pi]
                        wtasks = []
                        for j, kt in enumerate(kts):
                            o = kt * 128 - qc * 512
                            if 0 <= o <= 384:
                                def w_fn(v, j=j, pb=pb, o=o):
                                    sl = pt[:, pb * 1024 + j * 512:pb * 1024 + (j + 1) * 512]
                                    return v.tensor_tensor(
                                        sl, sl, wedge[:, 384 - o:896 - o], ALU.mult)
                                wtasks.append(pr.add("vector", w_fn, deps=[eB[pi]]))

                        def av_fn(t, kts=kts, pb=pb, hg=hg, pi=pi, npair=npair):
                            return [t.matmul(
                                ps_av[0:65, :],
                                vg4[:, kt, hg, :],
                                pt[:, pb * 1024 + j * 512:pb * 1024 + (j + 1) * 512],
                                start=(pi == 0 and j == 0),
                                stop=(pi == npair - 1 and j == len(kts) - 1),
                                skip_group_check=True)
                                for j, kt in enumerate(kts)]
                        mAV[pi] = pr.add("tensor", av_fn,
                                         deps=[eB[pi], last_evA] + wtasks)
                    evA = pr.add("vector", lambda v: v.tensor_copy(
                        av65[0:65, :], ps_av[0:65, :]), deps=[mAV[npair - 1]])
                    last_evA = evA
                    for s4 in range(4):
                        tA2 = pr.add("tensor", lambda t, s4=s4: t.transpose(
                            ps_ms[:, 0:65], av65[0:65, s4 * 128:(s4 + 1) * 128],
                            idf[0:65, 0:65]), deps=[evA, last_psms])
                        tt_abs = qc * 4 + s4
                        evA2 = pr.add("vector", lambda v, tt_abs=tt_abs, hg=hg: [
                            v.tensor_copy(
                                avtok[:, tt_abs * 512 + hg * 64:tt_abs * 512 + (hg + 1) * 64],
                                ps_ms[:, 0:64]),
                            v.tensor_copy(ztok[:, tt_abs * 8 + hg:tt_abs * 8 + hg + 1],
                                          ps_ms[:, 64:65])],
                            deps=[tA2])
                        last_psms = evA2

            grp_done = pr.barrier_deps()
            prev_t3 = None
            prev_ev3 = None
            prev_gav = None
            for tt in range(NTT):
                def rz_fn(v, tt=tt):
                    pr.fence("vector", v,
                             v.reciprocal(rzs[:, 0:8], ztok[:, tt * 8:(tt + 1) * 8]))
                    return v.tensor_tensor(
                        avn[:].rearrange("p (h d) -> p h d", d=64),
                        avtok[:, tt * 512:(tt + 1) * 512]
                          .rearrange("p (h d) -> p h d", d=64),
                        rzs[:, 0:8].rearrange("p (h o) -> p h o", o=1)
                           .to_broadcast([128, 8, 64]),
                        ALU.mult)
                rz = pr.add("vector", rz_fn, deps=grp_done + [prev_t3])
                t3 = pr.add("tensor", lambda t: [
                    t.transpose(ps_tr[:, s4 * 128:(s4 + 1) * 128],
                                avn[:, s4 * 128:(s4 + 1) * 128], idb[:])
                    for s4 in range(4)], deps=[rz, CONST, prev_ev3])
                prev_t3 = t3
                ev3 = pr.add("vector", lambda v: v.tensor_copy(
                    avstg[:], ps_tr[:, 0:512]), deps=[t3, prev_gav])
                prev_ev3 = ev3
                gav = pr.add("gpsimd", lambda g_, tt=tt, g=g: [
                    g_.dma_start(avnf_h[g * 4 + b4, tt, :, :],
                                 avstg[:, b4 * 128:(b4 + 1) * 128])
                    for b4 in range(4)], ndma=4, stream=s_gav, deps=[ev3])
                prev_gav = gav
                gav_done.append(gav)

        p4prev = [None] * (2 * NTT)
        for h2 in range(2):
            lwo = pr.add("gpsimd", lambda gp, h2=h2: [
                gp.dma_start(wbuf[:, dc * 512:(dc + 1) * 512],
                             wo_in[dc * 128:(dc + 1) * 128, h2 * 512:(h2 + 1) * 512])
                for dc in range(8)], ndma=8, stream=s_w,
                deps=[mm_all[-1], gav_done[-1], gav_done[-2]] +
                     ([p4prev[NTT - 1]] if h2 == 1 else []))
            for tt in range(NTT):
                it = h2 * NTT + tt
                buf = it % 2
                lav = pr.add("sync", lambda sy, tt=tt, buf=buf: [
                    sy.dma_start(avf[:, buf * D + dc * 128:buf * D + (dc + 1) * 128],
                                 avnf_h[dc, tt, :, :]) for dc in range(8)], ndma=8,
                    stream=s_lav,
                    deps=[gav_done[-1], gav_done[-2],
                          p4prev[it - 2] if it >= 2 else None])
                mo = pr.add("tensor", lambda t, buf=buf: [
                    t.matmul(ps_big[:, 0:512],
                             avf[:, buf * D + dc * 128:buf * D + (dc + 1) * 128],
                             wbuf[:, dc * 512:(dc + 1) * 512],
                             start=(dc == 0), stop=(dc == 7), skip_group_check=True)
                    for dc in range(8)],
                    deps=[lwo, lav, last_eBh[0], last_eBh[1],
                          p4prev[it - 1] if it >= 1 else None])
                lxo = pr.add("sync", lambda sy, tt=tt, buf=buf, h2=h2: sy.dma_start(
                    xbuf[:, buf * 512:(buf + 1) * 512],
                    x_in[tt * 128:(tt + 1) * 128, h2 * 512:(h2 + 1) * 512]), ndma=1,
                    stream=s_lxo,
                    deps=[ln_done[-1], vr_done[-1],
                          p4prev[it - 2] if it >= 2 else None])
                ro = pr.add("vector", lambda v, buf=buf: v.tensor_tensor(
                    obuf[:, 0:512], xbuf[:, buf * 512:(buf + 1) * 512],
                    ps_big[:, 0:512], ALU.add),
                    deps=[mo, lxo, p4prev[it - 1] if it >= 1 else None])
                go = pr.add("sync", lambda sy, tt=tt, h2=h2: sy.dma_start(
                    xout[tt * 128:(tt + 1) * 128, h2 * 512:(h2 + 1) * 512],
                    obuf[:, 0:512]), ndma=1, stream=s_go, deps=[ro])
                p4prev[it] = go

        if fu:
            return _fuse_tail(nc, pr, blk, fu, pfx)
        pr.add("gpsimd", lambda g_: g_.nop(), deps=pr.barrier_deps())
        pr.emit(blk)

    return nc


def _fuse_tail(nc, pr, blk, fu, pfx, last=False):
    ow = [fu["bar_prev"]] if fu.get("bar_prev") else []
    if last:
        pr.add("gpsimd", lambda g_: g_.nop(), deps=pr.barrier_deps())
        pr.emit(blk, open_waits=ow)
        return None
    bar = fu["sem_ctx"].enter_context(nc.semaphore(pfx + "bar"))

    def jfn(g_):
        g_.nop().then_inc(bar, 1)
        return g_.nop()
    pr.add("gpsimd", jfn, deps=pr.barrier_deps())
    pr.emit(blk, open_waits=ow)
    return (bar, 1)


def host_attn_consts(T=4096):
    inv = 1.0 / (10000.0 ** (np.arange(0, HD, 2, dtype=np.float32) / HD))
    t = np.arange(T, dtype=np.float32)
    fr = np.outer(t, inv)
    cos, sin = np.cos(fr).astype(np.float32), np.sin(fr).astype(np.float32)
    idf = np.eye(128, dtype=np.float32)
    idb = np.eye(128, dtype=np.float32).astype(BF)
    wedge = np.zeros((128, 896), dtype=np.float32)
    k = np.arange(128)[:, None]
    u = np.arange(896)[None, :]
    wedge[(u - 384) >= k] = 1.0
    return cos, sin, idf, idb, wedge.astype(BF)


def attn_in_maps(x_full, Wq, Wk, Wv, Wo, qnw, knw, T=4096, pfx="", with_x=True):
    cos, sin, idf, idb, wedge = host_attn_consts(T)
    base = {
        "wq": np.ascontiguousarray(Wq), "wk": np.ascontiguousarray(Wk),
        "wv": np.ascontiguousarray(Wv), "wo": np.ascontiguousarray(Wo),
        "qnw": np.tile(np.asarray(qnw, np.float32), 8)[None, :],
        "knw": np.tile(np.asarray(knw, np.float32), 8)[None, :],
        "cosr": cos, "sinr": sin, "idf": idf, "idb": idb, "wedge": wedge,
        "cval": np.array([[LN_EPS, RMS_EPS, 1.0, 0, 0, 0, 0, 0]], dtype=np.float32),
    }
    base = {pfx + k: v for k, v in base.items()}
    in_maps = []
    for b in range(x_full.shape[0]):
        m = dict(base)
        if with_x:
            m[pfx + "x"] = np.ascontiguousarray(x_full[b])
        in_maps.append(m)
    return in_maps


def run_l0_attn(x_full, Wq, Wk, Wv, Wo, qnw, knw, T=4096, debug=False, raw=False):
    nc = build_l0_attn(T, debug=debug)
    in_maps = attn_in_maps(x_full, Wq, Wk, Wv, Wo, qnw, knw, T)
    res = run_bass_kernel_spmd(nc, in_maps, core_ids=list(range(len(in_maps))))
    if raw:
        return res
    return np.stack([np.asarray(r["xout"]) for r in res.results])



FI = 2816


def emit_quant(pr, src_ap, dst_ap, W, scr, deps, name=""):
    G = W // 16
    m_ = scr["m"][:, 0:G]
    s_ = scr["m"][:, G:2 * G]
    r_ = scr["m"][:, 2 * G:3 * G]
    s2_ = scr["m"][:, 3 * G:4 * G]
    ax = scr["ax"][:, 0:W]
    acc = scr["acc"][:, 0:W]
    ac2 = scr["ac2"][:, 0:W]
    sg = scr["sg"][:, 0:W]

    t_abs = pr.add("scalar", lambda s: s.activation(ax, src_ap, AF.Abs), deps=deps)
    t_sgn = pr.add("scalar", lambda s: s.activation(sg, src_ap, AF.Sign), deps=deps)

    def t_m_fn(v):
        pr.fence("vector", v, v.reduce_max(
            m_, src_ap.rearrange("p (g k) -> p g k", k=16), axis=AXX,
            apply_absolute_value=True))
        pr.fence("vector", v, v.tensor_scalar(s_, m_, 1.0 / 6.0, 1e-12,
                                              op0=ALU.mult, op1=ALU.max))
        pr.fence("vector", v, v.reciprocal(r_, s_))
        return v.tensor_copy(s2_, s_)
    t_m = pr.add("vector", t_m_fn, deps=deps)

    def t_q_fn(v):
        ii = []
        r3 = r_.rearrange("p (g o) -> p g o", o=1).to_broadcast([128, G, 16])
        ax3 = ax.rearrange("p (g k) -> p g k", k=16)
        ii.append(v.tensor_tensor(ax3, ax3, r3, ALU.mult))
        u = ax
        ii.append(v.tensor_scalar(acc, u, 0.25, None, op0=ALU.is_gt))
        ii.append(v.scalar_tensor_tensor(acc, u, 0.75, acc, op0=ALU.is_gt, op1=ALU.add))
        ii.append(v.scalar_tensor_tensor(acc, u, 1.25, acc, op0=ALU.is_gt, op1=ALU.add))
        ii.append(v.scalar_tensor_tensor(acc, u, 1.75, acc, op0=ALU.is_gt, op1=ALU.add))
        ii.append(v.tensor_scalar(ac2, u, 2.5, None, op0=ALU.is_gt))
        ii.append(v.scalar_tensor_tensor(ac2, u, 3.5, ac2, op0=ALU.is_gt, op1=ALU.add))
        ii.append(v.scalar_tensor_tensor(acc, ac2, 2.0, acc, op0=ALU.mult, op1=ALU.add))
        ii.append(v.tensor_scalar(ac2, u, 5.0, 4.0, op0=ALU.is_gt, op1=ALU.mult))
        ii.append(v.tensor_tensor(acc, acc, ac2, ALU.add))
        ii.append(v.tensor_scalar(acc, acc, 0.5, None, op0=ALU.mult))
        ii.append(v.tensor_tensor(acc, acc, sg, ALU.mult))
        s23 = s2_.rearrange("p (g o) -> p g o", o=1).to_broadcast([128, G, 16])
        acc3 = acc.rearrange("p (g k) -> p g k", k=16)
        ii.append(v.tensor_tensor(dst_ap.rearrange("p (g k) -> p g k", k=16),
                                  acc3, s23, ALU.mult))
        return ii
    return pr.add("vector", t_q_fn, deps=[t_abs, t_sgn, t_m])


def build_l0_ffn(T=4096, debug=False, fu=None):
    NTT = T // 128
    FCH = [(i * 512, min(512, FI - i * 512)) for i in range((FI + 511) // 512)]
    pfx = fu["pfx"] if fu else ""
    nc = fu["nc"] if fu else bass.Bass("TRN2", target_bir_lowering=False)
    DT = lambda name, *a, **k: nc.dram_tensor(pfx + name, *a, **k)

    if fu:
        x_in, xout = fu["x_in"], fu["x_out"]
    else:
        x_in = DT("x", [T, D], F32, kind="ExternalInput").ap()
        xout = DT("xout", [T, D], F32, kind="ExternalOutput").ap()
    wg_in = DT("wg", [FI, D], F32, kind="ExternalInput").ap()
    wu_in = DT("wu", [FI, D], F32, kind="ExternalInput").ap()
    wd_in = DT("wd", [D, FI], F32, kind="ExternalInput").ap()
    idb_in = DT("idb", [128, 128], BF16, kind="ExternalInput").ap()
    cval_in = DT("cval", [1, 8], F32, kind="ExternalInput").ap()
    dbg = dict(kind="ExternalOutput") if debug else {}
    xln_h = DT("xln_h", [T, D], F32, **dbg).ap()
    xq_h = DT("xq_h", [T, D], BF16, **dbg).ap() if debug else None
    wgu_h = DT("wgu_h", [128, 8 * 2 * FI], BF16, **dbg).ap() if debug else None

    with ExitStack() as ctx:
        S = lambda name, shape, dt: ctx.enter_context(nc.sbuf_tensor(pfx + name, shape, dt))
        idb = S("idb_s", [128, 128], BF16)
        cvals = S("cvals", [128, 8], F32)
        wguT = S("wguT", [128, 8 * 2 * FI], BF16)
        wdT = S("wdT", [128, 22 * D], BF16)
        wraw = S("wraw", [128, 1408], F32)
        wqrow = S("wqrow", [128, 1408], BF16)
        xbuf = S("xbuf", [128, D], F32)
        xln = S("xln", [128, D], F32)
        xln2 = S("xln2", [128, D], F32)
        sq1 = S("sq1", [128, D], F32)
        stat = S("stat", [128, 16], F32)
        xq = S("xq", [128, D], BF16)
        xqT = S("xqT", [128, D], BF16)
        hbuf = S("hbuf", [128, 512], F32)
        hq = S("hq", [128, 512], BF16)
        hqT = S("hqT", [128, 512], BF16)
        obuf = S("obuf", [128, D], F32)
        scr = {
            "ax": S("q_ax", [128, 1408], F32),
            "acc": S("q_acc", [128, 1408], F32),
            "ac2": S("q_ac2", [128, 1408], F32),
            "sg": S("q_sg", [128, 1408], F32),
            "m": S("q_m", [128, 4 * 88], F32),
        }

        P_ = lambda name, shape, dt: ctx.enter_context(nc.psum_tensor(pfx + name, shape, dt))
        ps_tr = P_("ps_tr", [128, 2048], BF16)
        ps_gu = P_("ps_gu", [128, 1024], F32)
        ps_dn = P_("ps_dn", [128, 1024], F32)

        pr = Prog(nc, fu["sem_ctx"] if fu else ctx, pfx)
        s_const = pr.stream("s_const", 1)
        s_wr = pr.stream("s_wr", 2)
        s_lx = pr.stream("s_lx", 2)
        s_sx = pr.stream("s_sx", 2)
        s_go = pr.stream("s_go", 2)
        blk = fu["blk"] if fu else ctx.enter_context(nc.Block())

        CONST = pr.add("sync", lambda sy: [
            sy.dma_start(idb[:], idb_in[:]),
            sy.dma_start(cvals[:], cval_in.to_broadcast([128, 8]))],
            ndma=2, stream=s_const)

        prev_q = None
        prev_ev = [None, None]
        pi = 0

        def wchunk(w_in, r0, nr, w_cols0, w_ncols, dests):
            nonlocal prev_q, prev_ev, pi
            half = (pi % 2) * 1024
            ldw = pr.add("sync", lambda sy: sy.dma_start(
                wraw[:, 0:w_ncols], w_in[r0:r0 + nr, w_cols0:w_cols0 + w_ncols]),
                ndma=1, stream=s_wr, deps=[prev_q])
            tq = emit_quant(pr, wraw[:, 0:w_ncols], wqrow[:, 0:w_ncols], w_ncols,
                            scr, deps=[ldw, prev_ev[0], prev_ev[1]])
            prev_q = tq
            nb = w_ncols // 128
            done = []
            for fb in range(0, nb, 8):
                nfs = min(8, nb - fb)
                ttr = pr.add("tensor", lambda t, half=half, fb=fb, nfs=nfs: [
                    t.transpose(ps_tr[:, half + i * 128:half + (i + 1) * 128],
                                wqrow[:, (fb + i) * 128:(fb + i + 1) * 128], idb[:])
                    for i in range(nfs)],
                    deps=[tq, CONST, done[-1] if done else prev_ev[pi % 2]])
                ev = pr.add("vector", lambda v, half=half, fb=fb, nfs=nfs,
                            dests=dests: [
                    v.tensor_copy(dests[fb + i][0][:, dests[fb + i][1]:
                                                   dests[fb + i][1] + 128],
                                  ps_tr[:, half + i * 128:half + (i + 1) * 128])
                    for i in range(nfs)], deps=[ttr])
                done.append(ev)
            prev_ev[pi % 2] = done[-1]
            pi += 1

        for which, w_in in [(0, wg_in), (1, wu_in)]:
            for rt in range(FI // 128):
                r0 = rt * 128
                fc, off = r0 // 512, r0 % 512
                fcw = min(512, FI - fc * 512)
                dests = [(wguT, dc * 2 * FI + fc * 1024 + which * fcw + off)
                         for dc in range(8)]
                wchunk(w_in, r0, 128, 0, D, dests)
        for rt in range(8):
            for ch in range(2):
                dests = [(wdT, (ch * 11 + i) * D + rt * 128) for i in range(11)]
                wchunk(wd_in, rt * 128, 128, ch * 1408, 1408, dests)
        if debug:
            pr.add("gpsimd", lambda g_: g_.dma_start(wgu_h[:, :], wguT[:, :]),
                   ndma=1, stream=s_sx, deps=pr.barrier_deps())
        wdone = pr.barrier_deps()

        p1_ln = [None] * NTT
        p1_vr = [None] * NTT
        p1_sx = [None] * NTT
        for tt in range(NTT):
            ld = pr.add("sync", lambda sy, tt=tt: sy.dma_start(
                xbuf[:], x_in[tt * 128:(tt + 1) * 128, :]), ndma=1, stream=s_lx,
                deps=[p1_ln[tt - 1] if tt >= 1 else None] + (wdone if tt == 0 else []))
            sq_t = pr.add("scalar", lambda s: s.activation(
                sq1[:], xbuf[:], AF.Square),
                deps=[ld, p1_vr[tt - 1] if tt >= 1 else None])

            def vr_fn(v):
                pr.fence("vector", v, v.reduce_sum(stat[:, 0:1], xbuf[:], axis=AXX))
                pr.fence("vector", v, v.reduce_sum(stat[:, 1:2], sq1[:],
                                                   axis=AXX))
                pr.fence("vector", v, v.tensor_scalar(stat[:, 2:3], stat[:, 0:1],
                                                      1.0 / D, None, op0=ALU.mult))
                pr.fence("vector", v, v.tensor_tensor(stat[:, 3:4], stat[:, 2:3],
                                                      stat[:, 2:3], ALU.mult))
                return v.scalar_tensor_tensor(stat[:, 4:5], stat[:, 1:2], 1.0 / D,
                                              stat[:, 3:4], op0=ALU.mult,
                                              op1=ALU.subtract)
            vr = pr.add("vector", vr_fn, deps=[ld, sq_t])
            p1_vr[tt] = vr

            def lnp_fn(s):
                pr.fence("scalar", s, s.activation(stat[:, 5:6], stat[:, 4:5],
                                                   AF.Ln, bias=cvals[:, 0:1]))
                pr.fence("scalar", s, s.activation(stat[:, 6:7], stat[:, 5:6],
                                                   AF.Exp, scale=-0.5))
                pr.fence("scalar", s, s.activation(stat[:, 7:8], stat[:, 2:3],
                                                   AF.Copy, scale=stat[:, 6:7]))
                return s.mul(stat[:, 7:8], stat[:, 7:8], -1.0)
            lnp = pr.add("scalar", lnp_fn, deps=[vr, CONST])
            ln = pr.add("vector", lambda v: v.scalar_tensor_tensor(
                xln[:], xbuf[:], stat[:, 6:7], stat[:, 7:8].to_broadcast([128, D]),
                op0=ALU.mult, op1=ALU.add),
                deps=[lnp, p1_sx[tt - 1] if tt >= 1 else None])
            p1_ln[tt] = ln
            p1_sx[tt] = pr.add("gpsimd", lambda g_, tt=tt: g_.dma_start(
                xln_h[tt * 128:(tt + 1) * 128, :], xln[:]), ndma=1, stream=s_sx,
                deps=[ln])

        prev = {"q": None, "mm": None, "sil": None, "hq": None, "tph": None,
                "evh": None, "dn": None, "ro": None, "go": None, "tpx": None,
                "evx": None, "lnld": None}
        for tt in range(NTT):
            lnld = pr.add("sync", lambda sy, tt=tt: sy.dma_start(
                xln2[:], xln_h[tt * 128:(tt + 1) * 128, :]), ndma=1, stream=s_lx,
                deps=[p1_sx[tt], prev["q"]])
            prev["lnld"] = lnld
            tq = emit_quant(pr, xln2[:], xq[:], D, scr,
                            deps=[lnld, prev["tpx"]] +
                                 ([p1_vr[-1], p1_ln[-1]] if tt == 0 else []))
            prev["q"] = tq
            tpx = pr.add("tensor", lambda t: [
                t.transpose(ps_tr[:, dc * 128:(dc + 1) * 128],
                            xq[:, dc * 128:(dc + 1) * 128], idb[:])
                for dc in range(8)], deps=[tq, CONST, prev["evx"]])
            prev["tpx"] = tpx
            evx = pr.add("vector", lambda v: v.tensor_copy(xqT[:], ps_tr[:, 0:1024]),
                         deps=[tpx, prev["mm"]])
            prev["evx"] = evx
            if debug:
                pr.add("gpsimd", lambda g_, tt=tt: g_.dma_start(
                    xq_h[tt * 128:(tt + 1) * 128, :], xq[:]), ndma=1, stream=s_sx,
                    deps=[tq, prev["q"]])

            for fci, (f0, fn_) in enumerate(FCH):
                mm = pr.add("tensor", lambda t, f0=f0, fn_=fn_: [
                    t.matmul(ps_gu[:, w * 512:w * 512 + fn_],
                             xqT[:, dc * 128:(dc + 1) * 128],
                             wguT[:, dc * 2 * FI + (f0 // 512) * 1024 + w * fn_:
                                  dc * 2 * FI + (f0 // 512) * 1024 + (w + 1) * fn_],
                             start=(dc == 0), stop=(dc == 7), skip_group_check=True)
                    for w in range(2) for dc in range(8)],
                    deps=[evx, prev["sil"], prev["hq"]])
                prev["mm"] = mm
                sil = pr.add("scalar", lambda s, fn_=fn_: s.activation(
                    hbuf[:, 0:fn_], ps_gu[:, 0:fn_], AF.Silu), deps=[mm, prev["hq"]])
                prev["sil"] = sil
                hm = pr.add("vector", lambda v, fn_=fn_: v.tensor_tensor(
                    hbuf[:, 0:fn_], hbuf[:, 0:fn_], ps_gu[:, 512:512 + fn_],
                    ALU.mult), deps=[sil, mm])
                thq = emit_quant(pr, hbuf[:, 0:fn_], hq[:, 0:fn_], fn_, scr,
                                 deps=[hm, prev["tph"]])
                prev["hq"] = thq
                nfs = fn_ // 128
                tph = pr.add("tensor", lambda t, nfs=nfs: [
                    t.transpose(ps_tr[:, 1024 + i * 128:1024 + (i + 1) * 128],
                                hq[:, i * 128:(i + 1) * 128], idb[:])
                    for i in range(nfs)], deps=[thq, prev["evh"]])
                prev["tph"] = tph
                evh = pr.add("vector", lambda v, nfs=nfs: v.tensor_copy(
                    hqT[:, 0:nfs * 128], ps_tr[:, 1024:1024 + nfs * 128]),
                    deps=[tph, prev["dn"]])
                prev["evh"] = evh
                dn = pr.add("tensor", lambda t, f0=f0, nfs=nfs, fci=fci: [
                    t.matmul(ps_dn[:, h2 * 512:(h2 + 1) * 512],
                             hqT[:, i * 128:(i + 1) * 128],
                             wdT[:, (f0 // 128 + i) * D + h2 * 512:
                                  (f0 // 128 + i) * D + (h2 + 1) * 512],
                             start=(fci == 0 and i == 0),
                             stop=(fci == len(FCH) - 1 and i == nfs - 1
                                   and h2 == 1),
                             skip_group_check=True)
                    for i in range(nfs) for h2 in range(2)],
                    deps=[evh, prev["ro"]])
                prev["dn"] = dn
            lxr = pr.add("sync", lambda sy, tt=tt: sy.dma_start(
                xbuf[:], x_in[tt * 128:(tt + 1) * 128, :]), ndma=1, stream=s_go,
                deps=[prev["dn"]] + ([p1_ln[-1], p1_vr[-1],
                                      pr.tasks[pr.tasks.index(prev["dn"])]]
                                     if tt == 0 else []))
            ro = pr.add("vector", lambda v: v.tensor_tensor(
                obuf[:], xbuf[:], ps_dn[:, 0:1024], ALU.add),
                deps=[prev["dn"], lxr, prev["go"]])
            prev["ro"] = ro
            go = pr.add("sync", lambda sy, tt=tt: sy.dma_start(
                xout[tt * 128:(tt + 1) * 128, :], obuf[:]), ndma=1, stream=s_go,
                deps=[ro])
            prev["go"] = go

        if fu:
            return _fuse_tail(nc, pr, blk, fu, pfx)
        pr.add("gpsimd", lambda g_: g_.nop(), deps=pr.barrier_deps())
        pr.emit(blk)
    return nc


def ffn_in_maps(x_full, wg, wu, wd, pfx="", with_x=True):
    base = {
        "wg": np.ascontiguousarray(wg), "wu": np.ascontiguousarray(wu),
        "wd": np.ascontiguousarray(wd),
        "idb": np.eye(128, dtype=np.float32).astype(BF),
        "cval": np.array([[LN_EPS, RMS_EPS, 1.0, 0, 0, 0, 0, 0]], dtype=np.float32),
    }
    base = {pfx + k: v for k, v in base.items()}
    in_maps = []
    for b in range(x_full.shape[0]):
        m = dict(base)
        if with_x:
            m[pfx + "x"] = np.ascontiguousarray(x_full[b])
        in_maps.append(m)
    return in_maps


def run_l0_ffn(x_full, wg, wu, wd, T=4096, raw=False, debug=False):
    nc = build_l0_ffn(T, debug=debug)
    in_maps = ffn_in_maps(x_full, wg, wu, wd)
    res = run_bass_kernel_spmd(nc, in_maps, core_ids=list(range(len(in_maps))))
    if raw:
        return res
    return np.stack([np.asarray(r["xout"]) for r in res.results])



RN, NN = 64, 16


def build_l1_attn(T=4096, debug=False, fu=None):
    NTT = T // 128
    NCH = T // 512
    Q = 512
    pfx = fu["pfx"] if fu else ""
    nc = fu["nc"] if fu else bass.Bass("TRN2", target_bir_lowering=False)
    DT = lambda name, *a, **k: nc.dram_tensor(pfx + name, *a, **k)

    if fu:
        x_in, xout = fu["x_in"], fu["x_out"]
    else:
        x_in = DT("x", [T, D], F32, kind="ExternalInput").ap()
        xout = DT("xout", [T, D], F32, kind="ExternalOutput").ap()
    xpw_in = DT("xpw", [96, D], F32, kind="ExternalInput").ap()
    dtw_in = DT("dtw", [D, RN], F32, kind="ExternalInput").ap()
    dtb_in = DT("dtb", [D], F32, kind="ExternalInput").ap()
    alog_in = DT("alog", [D, NN], F32, kind="ExternalInput").ap()
    dpar_in = DT("dpar", [D], F32, kind="ExternalInput").ap()
    lna_in = DT("lna", [1, D], F32, kind="ExternalInput").ap()
    lnb_in = DT("lnb", [1, D], F32, kind="ExternalInput").ap()
    idb_in = DT("idb", [128, 128], BF16, kind="ExternalInput").ap()
    idf_in = DT("idf", [128, 128], F32, kind="ExternalInput").ap()
    cval_in = DT("cval", [1, 8], F32, kind="ExternalInput").ap()
    dbg = dict(kind="ExternalOutput") if debug else {}
    xln_h = DT("xln_h", [T, D], F32, **dbg).ap()
    dbc_h = DT("dbc_h", [32, T], F32, **dbg).ap()
    delta_h = DT("delta_h", [128, 8 * T], F32, **dbg).ap() if debug else None
    y_h = DT("y_h", [128, 8 * T], F32, **dbg).ap() if debug else None

    with ExitStack() as ctx:
        S = lambda name, shape, dt: ctx.enter_context(nc.sbuf_tensor(pfx + name, shape, dt))
        idb = S("idb_s", [128, 128], BF16)
        idf = S("idf_s", [128, 128], F32)
        cvals = S("cvals", [128, 8], F32)
        xpwT = S("xpwT", [128, 8 * 96], BF16)
        dtwT = S("dtwT", [64, 8 * 128], BF16)
        wstgb = S("wstgb", [128, 128], BF16)
        a_sb = S("a_sb", [128, 8 * NN], F32)
        db_sb = S("db_sb", [128, 8], F32)
        dp_sb = S("dp_sb", [128, 8], F32)
        lnar = S("lnar", [128, D], F32)
        lnbr = S("lnbr", [128, D], F32)
        xbuf = S("xbuf", [128, D], F32)
        xln = S("xln", [128, D], F32)
        sq1 = S("sq1", [128, D], F32)
        stat = S("stat", [128, 16], F32)
        xfT = S("xfT", [128, 8 * 512], F32)
        xbT = S("xbT", [128, 8 * 512], BF16)
        dbcB = S("dbcB", [64, 512], BF16)
        dbcS = S("dbcS", [32, 512], F32)
        delta = S("delta", [128, 8 * 512], F32)
        spl = S("spl", [128, 512], F32)
        spr = S("spr", [128, 512], F32)
        zc = S("zc", [128, 8 * 512], F32)
        dAb = S("dAb", [128, 8 * 512], F32)
        bxb = S("bxb", [128, 8 * 512], F32)
        hb = S("hb", [128, 8 * 512], F32)
        yac = S("yac", [128, 8 * 512], F32)
        yv = S("yv", [128, 512], F32)
        bmr = S("bmr", [128, 2 * 512], F32)
        cmr = S("cmr", [128, 2 * 512], F32)
        state = S("state", [128, 8 * NN], F32)
        ytok = S("ytok", [128, D], F32)
        obuf = S("obuf", [128, D], F32)

        P_ = lambda name, shape, dt: ctx.enter_context(nc.psum_tensor(pfx + name, shape, dt))
        ps_trf = P_("ps_trf", [128, 1024], F32)
        ps_trb = P_("ps_trb", [128, 128], BF16)
        ps_dbc = P_("ps_dbc", [96, 512], F32)
        ps_dt = P_("ps_dt", [128, 512], F32)

        pr = Prog(nc, fu["sem_ctx"] if fu else ctx, pfx)
        s_const = pr.stream("s_const", 1)
        s_w = pr.stream("s_w", 2)
        s_lx = pr.stream("s_lx", 2)
        s_sx = pr.stream("s_sx", 2)
        s_dbc = pr.stream("s_dbc", 2)
        s_bc = pr.stream("s_bc", 4)
        s_go = pr.stream("s_go", 2)
        blk = fu["blk"] if fu else ctx.enter_context(nc.Block())

        CONST = pr.add("sync", lambda sy: [
            sy.dma_start(idb[:], idb_in[:]),
            sy.dma_start(idf[:], idf_in[:]),
            sy.dma_start(cvals[:], cval_in.to_broadcast([128, 8])),
            sy.dma_start(a_sb[:].rearrange("p (dc n) -> p dc n", n=NN),
                         alog_in.rearrange("(dc p) n -> p dc n", p=128)),
            sy.dma_start(lnar[:], lna_in.to_broadcast([128, D])),
            sy.dma_start(lnbr[:], lnb_in.to_broadcast([128, D]))],
            ndma=6, stream=s_const)
        CONST2 = pr.add("gpsimd", lambda g_: [
            g_.dma_start(db_sb[:], dtb_in.rearrange("(dc p) -> p dc", p=128),
                         allow_slow_non_contiguous=True),
            g_.dma_start(dp_sb[:], dpar_in.rearrange("(dc p) -> p dc", p=128),
                         allow_slow_non_contiguous=True)],
            ndma=2, stream=s_bc)
        tA1 = pr.add("scalar", lambda s: s.activation(a_sb[:], a_sb[:], AF.Exp),
                     deps=[CONST])
        tA2 = pr.add("vector", lambda v: v.tensor_scalar(
            a_sb[:], a_sb[:], -1.0, None, op0=ALU.mult), deps=[tA1])

        prev_ev = None
        for dc in range(8):
            lw = pr.add("gpsimd", lambda g_, dc=dc: g_.dma_start(
                wstgb[0:96, :], xpw_in[:, dc * 128:(dc + 1) * 128]), ndma=1,
                stream=s_w, deps=[prev_ev])
            tt_ = pr.add("tensor", lambda t: t.transpose(
                ps_trb[:, 0:96], wstgb[0:96, :], idb[0:96, 0:96]),
                deps=[lw, CONST, prev_ev])
            ev = pr.add("vector", lambda v, dc=dc: v.tensor_copy(
                xpwT[:, dc * 96:(dc + 1) * 96], ps_trb[:, 0:96]), deps=[tt_])
            prev_ev = ev
        for dc in range(8):
            lw = pr.add("gpsimd", lambda g_, dc=dc: g_.dma_start(
                wstgb[:, 0:RN], dtw_in[dc * 128:(dc + 1) * 128, :]), ndma=1,
                stream=s_w, deps=[prev_ev])
            tt_ = pr.add("tensor", lambda t: t.transpose(
                ps_trb[0:RN, 0:128], wstgb[:, 0:RN], idb[:]),
                deps=[lw, CONST, prev_ev])
            ev = pr.add("vector", lambda v, dc=dc: v.tensor_copy(
                dtwT[0:RN, dc * 128:(dc + 1) * 128], ps_trb[0:RN, 0:128]),
                deps=[tt_])
            prev_ev = ev
        wdone = pr.barrier_deps()

        p1_ln = [None] * NTT
        p1_vr = [None] * NTT
        p1_sx = [None] * NTT
        for tt in range(NTT):
            ld = pr.add("sync", lambda sy, tt=tt: sy.dma_start(
                xbuf[:], x_in[tt * 128:(tt + 1) * 128, :]), ndma=1, stream=s_lx,
                deps=[p1_ln[tt - 1] if tt >= 1 else None])
            sq_t = pr.add("scalar", lambda s: s.activation(
                sq1[:], xbuf[:], AF.Square),
                deps=[ld, p1_vr[tt - 1] if tt >= 1 else None] +
                     (wdone if tt == 0 else []))

            def vr_fn(v):
                pr.fence("vector", v, v.reduce_sum(stat[:, 0:1], xbuf[:], axis=AXX))
                pr.fence("vector", v, v.reduce_sum(stat[:, 1:2], sq1[:], axis=AXX))
                pr.fence("vector", v, v.tensor_scalar(stat[:, 2:3], stat[:, 0:1],
                                                      1.0 / D, None, op0=ALU.mult))
                pr.fence("vector", v, v.tensor_tensor(stat[:, 3:4], stat[:, 2:3],
                                                      stat[:, 2:3], ALU.mult))
                return v.scalar_tensor_tensor(stat[:, 4:5], stat[:, 1:2], 1.0 / D,
                                              stat[:, 3:4], op0=ALU.mult,
                                              op1=ALU.subtract)
            vr = pr.add("vector", vr_fn, deps=[ld, sq_t])
            p1_vr[tt] = vr

            def lnp_fn(s):
                pr.fence("scalar", s, s.activation(stat[:, 5:6], stat[:, 4:5],
                                                   AF.Ln, bias=cvals[:, 0:1]))
                pr.fence("scalar", s, s.activation(stat[:, 6:7], stat[:, 5:6],
                                                   AF.Exp, scale=-0.5))
                pr.fence("scalar", s, s.activation(stat[:, 7:8], stat[:, 2:3],
                                                   AF.Copy, scale=stat[:, 6:7]))
                return s.mul(stat[:, 7:8], stat[:, 7:8], -1.0)
            lnp = pr.add("scalar", lnp_fn, deps=[vr, CONST])
            ln = pr.add("vector", lambda v: v.scalar_tensor_tensor(
                xln[:], xbuf[:], stat[:, 6:7], stat[:, 7:8].to_broadcast([128, D]),
                op0=ALU.mult, op1=ALU.add),
                deps=[lnp, p1_sx[tt - 1] if tt >= 1 else None])
            p1_ln[tt] = ln
            p1_sx[tt] = pr.add("gpsimd", lambda g_, tt=tt: g_.dma_start(
                xln_h[tt * 128:(tt + 1) * 128, :], xln[:]), ndma=1, stream=s_sx,
                deps=[ln])

        st0 = pr.add("vector", lambda v: v.memset(state[:], 0.0), deps=[])

        pv = {k: None for k in ["tpq", "evx", "mmdbc", "evb", "evs", "gdbc",
                                "mmdt", "spv", "zt", "dA", "bx", "scn", "yn",
                                "stu", "yD", "lxq", "ro", "go", "lnv", "sqy"]}
        pv["stu"] = st0
        for c in range(NCH):
            c0 = c * 512
            evx_list = []
            for s4 in range(4):
                tt = c * 4 + s4
                lxq = pr.add("sync", lambda sy, tt=tt: sy.dma_start(
                    xln[:], xln_h[tt * 128:(tt + 1) * 128, :]), ndma=1, stream=s_lx,
                    deps=[p1_sx[tt], pv["tpq"]] +
                         ([p1_ln[-1], p1_vr[-1], p1_sx[-1], p1_sx[-2]]
                          if c == 0 and s4 == 0 else []))
                pv["lxq"] = lxq
                tpq = pr.add("tensor", lambda t: [
                    t.transpose(ps_trf[:, i * 128:(i + 1) * 128],
                                xln[:, i * 128:(i + 1) * 128], idf[:])
                    for i in range(8)],
                    deps=[lxq, CONST, pv["evx"], pv.get("evy")])
                pv["tpq"] = tpq

                def evx_fn(v, s4=s4):
                    ii = []
                    for i in range(8):
                        ii.append(v.tensor_copy(
                            xfT[:, i * 512 + s4 * 128:i * 512 + (s4 + 1) * 128],
                            ps_trf[:, i * 128:(i + 1) * 128]))
                        ii.append(v.tensor_copy(
                            xbT[:, i * 512 + s4 * 128:i * 512 + (s4 + 1) * 128],
                            ps_trf[:, i * 128:(i + 1) * 128]))
                    return ii
                evx = pr.add("vector", evx_fn,
                             deps=[tpq, pv["zt"], pv["yD"], pv["mmdbc"]])
                pv["evx"] = evx
                evx_list.append(evx)
            mmdbc = pr.add("tensor", lambda t: [
                t.matmul(ps_dbc[0:96, :], xpwT[:, dc * 96:(dc + 1) * 96],
                         xbT[:, dc * 512:(dc + 1) * 512],
                         start=(dc == 0), stop=(dc == 7), skip_group_check=True)
                for dc in range(8)],
                deps=evx_list + [pv["evb"], pv["evs"]] + ([] if c else pr.barrier_deps()[:1]))
            pv["mmdbc"] = mmdbc
            evb = pr.add("vector", lambda v: v.tensor_copy(
                dbcB[0:64, :], ps_dbc[0:64, :]), deps=[mmdbc, pv["mmdt"]])
            pv["evb"] = evb
            evs = pr.add("vector", lambda v: v.tensor_copy(
                dbcS[0:32, :], ps_dbc[64:96, :]), deps=[mmdbc, pv["gdbc"]])
            pv["evs"] = evs
            gdbc = pr.add("gpsimd", lambda g_, c0=c0: g_.dma_start(
                dbc_h[:, c0:c0 + 512], dbcS[0:32, :]), ndma=1, stream=s_dbc,
                deps=[evs])
            pv["gdbc"] = gdbc
            spv_l = None
            for dc in range(8):
                mmdt = pr.add("tensor", lambda t, dc=dc: t.matmul(
                    ps_dt[:], dtwT[0:RN, dc * 128:(dc + 1) * 128], dbcB[0:RN, :],
                    start=True, stop=True, skip_group_check=True),
                    deps=[evb, spv_l])
                pv["mmdt"] = mmdt

                def sp_fn(s, dc=dc):
                    s.activation(spl[:], ps_dt[:], AF.Abs, bias=db_sb[:, dc:dc + 1])
                    s.activation(spl[:], spl[:], AF.Exp, scale=-1.0)
                    s.activation(spl[:], spl[:], AF.Ln, bias=cvals[:, 2:3])
                    return s.activation(spr[:], ps_dt[:], AF.Relu,
                                        bias=db_sb[:, dc:dc + 1])
                sp = pr.add("scalar", sp_fn, deps=[mmdt, CONST2, spv_l])
                spv = pr.add("vector", lambda v, dc=dc: v.tensor_tensor(
                    delta[:, dc * 512:(dc + 1) * 512], spl[:], spr[:], ALU.add),
                    deps=[sp])
                spv_l = spv
            pv["spv"] = spv_l
            zt = pr.add("vector", lambda v: v.tensor_tensor(
                zc[:], delta[:], xfT[:], ALU.mult), deps=[spv_l, pv["bx"]])
            pv["zt"] = zt
            for n in range(NN):
                nb = n % 2
                bm = pr.add("gpsimd", lambda g_, n=n, c0=c0, nb=nb: [
                    g_.dma_start(bmr[:, nb * 512:(nb + 1) * 512],
                                 dbc_h[n:n + 1, c0:c0 + 512].to_broadcast(
                                     [128, 512])),
                    g_.dma_start(cmr[:, nb * 512:(nb + 1) * 512],
                                 dbc_h[16 + n:17 + n, c0:c0 + 512].to_broadcast(
                                     [128, 512]))],
                    ndma=2, stream=s_bc,
                    deps=[gdbc, pv["bx"], pv["yn"]])

                def dA_fn(s, n=n):
                    return [s.activation(dAb[:, dc * 512:(dc + 1) * 512],
                                         delta[:, dc * 512:(dc + 1) * 512],
                                         AF.Exp, scale=a_sb[:, dc * NN + n:
                                                           dc * NN + n + 1])
                            for dc in range(8)]
                dA = pr.add("scalar", dA_fn, deps=[pv["spv"], pv["scn"], tA2])
                pv["dA"] = dA

                def bx_fn(v, nb=nb):
                    b3 = bmr[:, nb * 512:(nb + 1) * 512]
                    return [v.tensor_tensor(bxb[:, dc * 512:(dc + 1) * 512],
                                            zc[:, dc * 512:(dc + 1) * 512],
                                            b3, ALU.mult)
                            for dc in range(8)]
                bx = pr.add("vector", bx_fn, deps=[zt, bm, pv["scn"]])
                pv["bx"] = bx

                def scn_fn(v, n=n):
                    for dc in range(7):
                        v.tensor_tensor_scan(
                            hb[:, dc * 512:(dc + 1) * 512],
                            dAb[:, dc * 512:(dc + 1) * 512],
                            bxb[:, dc * 512:(dc + 1) * 512],
                            state[:, dc * NN + n:dc * NN + n + 1],
                            ALU.mult, ALU.add)
                    pr.fence("vector", v, v.tensor_tensor_scan(
                        hb[:, 7 * 512:8 * 512], dAb[:, 7 * 512:8 * 512],
                        bxb[:, 7 * 512:8 * 512],
                        state[:, 7 * NN + n:7 * NN + n + 1],
                        ALU.mult, ALU.add))
                    return v.nop()
                scn = pr.add("vector", scn_fn,
                             deps=[dA, bx, pv["yn"], pv["stu"]])
                pv["scn"] = scn

                def stu_fn(v, n=n):
                    for dc in range(7):
                        v.tensor_copy(state[:, dc * NN + n:dc * NN + n + 1],
                                      hb[:, dc * 512 + 511:dc * 512 + 512])
                    pr.fence("vector", v, v.tensor_copy(
                        state[:, 7 * NN + n:7 * NN + n + 1],
                        hb[:, 7 * 512 + 511:7 * 512 + 512]))
                    return v.engine_nop() if hasattr(v, "engine_nop") else v.nop()
                stu = pr.add("vector", stu_fn, deps=[scn])
                pv["stu"] = stu

                def yn_fn(v, n=n, nb=nb):
                    c3 = cmr[:, nb * 512:(nb + 1) * 512]
                    ii = []
                    for dc in range(8):
                        sl = slice(dc * 512, (dc + 1) * 512)
                        if n == 0:
                            ii.append(v.tensor_tensor(yac[:, sl], hb[:, sl], c3,
                                                      ALU.mult))
                        else:
                            ii.append(v.tensor_tensor(yv[:], hb[:, sl], c3,
                                                      ALU.mult))
                            ii.append(v.tensor_tensor(yac[:, sl], yac[:, sl],
                                                      yv[:], ALU.add))
                    return ii
                yn = pr.add("vector", yn_fn, deps=[scn, bm])
                pv["yn"] = yn
            yD = pr.add("vector", lambda v: [
                v.scalar_tensor_tensor(yac[:, dc * 512:(dc + 1) * 512],
                                       xfT[:, dc * 512:(dc + 1) * 512],
                                       dp_sb[:, dc:dc + 1],
                                       yac[:, dc * 512:(dc + 1) * 512],
                                       op0=ALU.mult, op1=ALU.add)
                for dc in range(8)], deps=[pv["yn"], CONST2])
            pv["yD"] = yD
            if debug:
                pr.add("gpsimd", lambda g_, c0=c0: g_.dma_start(
                    delta_h[:, :].rearrange("p (dc t) -> p dc t", t=T)[:, :, c0:c0 + 512],
                    delta[:].rearrange("p (dc t) -> p dc t", t=512)),
                    ndma=1, stream=s_sx, deps=[pv["spv"]])
                pr.add("gpsimd", lambda g_, c0=c0: g_.dma_start(
                    y_h[:, :].rearrange("p (dc t) -> p dc t", t=T)[:, :, c0:c0 + 512],
                    yac[:].rearrange("p (dc t) -> p dc t", t=512)),
                    ndma=1, stream=s_sx, deps=[yD])
            for s4 in range(4):
                tt = c * 4 + s4
                tpy = pr.add("tensor", lambda t, s4=s4: [
                    t.transpose(ps_trf[:, i * 128:(i + 1) * 128],
                                yac[:, i * 512 + s4 * 128:i * 512 + (s4 + 1) * 128],
                                idf[:]) for i in range(8)],
                    deps=[yD, pv["lnv"], pv["evx"], pv.get("evy")])
                pv["tpq"] = tpy
                evy = pr.add("vector", lambda v: v.tensor_copy(
                    ytok[:], ps_trf[:, 0:1024]), deps=[tpy, pv["ro"]])
                pv["evy"] = evy
                sqy = pr.add("scalar", lambda s: s.activation(
                    sq1[:], ytok[:], AF.Square), deps=[evy, pv["lnv"]])

                def yst_fn(v):
                    pr.fence("vector", v, v.reduce_sum(stat[:, 0:1], ytok[:],
                                                       axis=AXX))
                    pr.fence("vector", v, v.reduce_sum(stat[:, 1:2], sq1[:],
                                                       axis=AXX))
                    pr.fence("vector", v, v.tensor_scalar(
                        stat[:, 2:3], stat[:, 0:1], 1.0 / D, None, op0=ALU.mult))
                    pr.fence("vector", v, v.tensor_tensor(
                        stat[:, 3:4], stat[:, 2:3], stat[:, 0:1], ALU.mult))
                    pr.fence("vector", v, v.scalar_tensor_tensor(
                        stat[:, 4:5], stat[:, 1:2], 1.0, stat[:, 3:4],
                        op0=ALU.mult, op1=ALU.subtract))
                    return v.tensor_scalar(stat[:, 4:5], stat[:, 4:5],
                                           1.0 / (D - 1), None, op0=ALU.mult)
                yst = pr.add("vector", yst_fn, deps=[evy, sqy])

                def ylnp_fn(s):
                    pr.fence("scalar", s, s.activation(stat[:, 5:6], stat[:, 4:5],
                                                       AF.Ln))
                    return s.activation(stat[:, 6:7], stat[:, 5:6],
                                        AF.Exp, scale=0.5)
                ylnp = pr.add("scalar", ylnp_fn, deps=[yst])

                def yrec_fn(v):
                    pr.fence("vector", v, v.tensor_scalar(
                        stat[:, 6:7], stat[:, 6:7], 1e-12, None, op0=ALU.add))
                    pr.fence("vector", v, v.reciprocal(stat[:, 7:8], stat[:, 6:7]))
                    pr.fence("vector", v, v.tensor_tensor(
                        stat[:, 8:9], stat[:, 2:3], stat[:, 7:8], ALU.mult))
                    pr.fence("vector", v, v.tensor_scalar(
                        stat[:, 8:9], stat[:, 8:9], -1.0, None, op0=ALU.mult))
                    return v.nop()
                yrec = pr.add("vector", yrec_fn, deps=[ylnp])
                lnv = pr.add("vector", lambda v: v.scalar_tensor_tensor(
                    ytok[:], ytok[:], stat[:, 7:8],
                    stat[:, 8:9].to_broadcast([128, D]),
                    op0=ALU.mult, op1=ALU.add), deps=[yrec])
                pv["lnv"] = lnv
                lxr = pr.add("sync", lambda sy, tt=tt: sy.dma_start(
                    xbuf[:], x_in[tt * 128:(tt + 1) * 128, :]), ndma=1,
                    stream=s_go, deps=[pv["ro"], lnv] +
                    ([p1_ln[-1], p1_vr[-1]] if c == 0 and s4 == 0 else []))

                def out_fn(v):
                    ii = []
                    ii.append(v.tensor_tensor(obuf[:], ytok[:], lnar[:], ALU.mult))
                    ii.append(v.tensor_tensor(obuf[:], obuf[:], lnbr[:], ALU.add))
                    ii.append(v.tensor_tensor(obuf[:], obuf[:], xbuf[:], ALU.add))
                    return ii
                ro = pr.add("vector", out_fn, deps=[lnv, lxr, pv["go"], CONST])
                pv["ro"] = ro
                go = pr.add("sync", lambda sy, tt=tt: sy.dma_start(
                    xout[tt * 128:(tt + 1) * 128, :], obuf[:]), ndma=1,
                    stream=s_go, deps=[ro])
                pv["go"] = go

        if fu:
            return _fuse_tail(nc, pr, blk, fu, pfx)
        pr.add("gpsimd", lambda g_: g_.nop(), deps=pr.barrier_deps())
        pr.emit(blk)
    return nc


def ssm_in_maps(x_full, xpw, dtw, dtb, alog, dpar, lna, lnb, pfx="", with_x=True):
    base = {
        "xpw": np.ascontiguousarray(xpw), "dtw": np.ascontiguousarray(dtw),
        "dtb": np.ascontiguousarray(dtb), "alog": np.ascontiguousarray(alog),
        "dpar": np.ascontiguousarray(dpar),
        "lna": np.ascontiguousarray(lna)[None, :],
        "lnb": np.ascontiguousarray(lnb)[None, :],
        "idb": np.eye(128, dtype=np.float32).astype(BF),
        "idf": np.eye(128, dtype=np.float32),
        "cval": np.array([[LN_EPS, RMS_EPS, 1.0, 0, 0, 0, 0, 0]], dtype=np.float32),
    }
    base = {pfx + k: v for k, v in base.items()}
    in_maps = []
    for b in range(x_full.shape[0]):
        m = dict(base)
        if with_x:
            m[pfx + "x"] = np.ascontiguousarray(x_full[b])
        in_maps.append(m)
    return in_maps


def run_l1_attn(x_full, xpw, dtw, dtb, alog, dpar, lna, lnb, T=4096, raw=False, debug=False):
    nc = build_l1_attn(T, debug=debug)
    in_maps = ssm_in_maps(x_full, xpw, dtw, dtb, alog, dpar, lna, lnb)
    res = run_bass_kernel_spmd(nc, in_maps, core_ids=list(range(len(in_maps))))
    if raw:
        return res
    return np.stack([np.asarray(r["xout"]) for r in res.results])


NEXP, KTOP, DFF = 8, 2048, 4096


def build_l1_ffn(T=4096, NC=8, fu=None):
    NTT = T // 128
    TOT = T * NC
    GPP = TOT // 128
    NJT = KTOP // 128
    pfx = fu["pfx"] if fu else ""
    nc = fu["nc"] if fu else bass.Bass("TRN2", target_bir_lowering=False)
    DT = lambda name, *a, **k: nc.dram_tensor(pfx + name, *a, **k)

    if fu:
        x_in, xout = fu["x_in"], fu["x_out"]
    else:
        x_in = DT("x", [T, D], F32, kind="ExternalInput").ap()
        xout = DT("xout", [T, D], F32, kind="ExternalOutput").ap()
    rw_in = DT("rw", [D, NEXP], F32, kind="ExternalInput").ap()
    rb_in = DT("rb", [1, NEXP], F32, kind="ExternalInput").ap()
    w1_in = DT("w1", [D, DFF], F32, kind="ExternalInput").ap()
    b1_in = DT("b1", [1, DFF], F32, kind="ExternalInput").ap()
    w2_in = DT("w2", [DFF, D], F32, kind="ExternalInput").ap()
    b2_in = DT("b2", [1, D], F32, kind="ExternalInput").ap()
    idb_in = DT("idb", [128, 128], BF16, kind="ExternalInput").ap()
    ltri_in = DT("ltri", [128, 128], F32, kind="ExternalInput").ap()
    siota_in = DT("siota", [1, 64], F32, kind="ExternalInput").ap()
    piota_in = DT("piota", [128, 1], F32, kind="ExternalInput").ap()
    cval_in = DT("cval", [1, 8], F32, kind="ExternalInput").ap()
    coreid_in = DT("coreid", [1, 1], F32, kind="ExternalInput").ap()

    xlnb_loc = DT("xlnb_loc", [T, D], BF16).ap()
    xlnb_full = DT("xlnb_full", [TOT, D], BF16).ap()
    lg_loc = DT("lg_loc", [T, NEXP], F32).ap()
    lg_full = DT("lg_full", [TOT, NEXP], F32).ap()
    sc_h = DT("sc_h", [TOT, NEXP], F32).ap()
    idx_h = DT("idx_h", [KTOP + 128], I32).ap()
    cont_loc = DT("cont_loc", [KTOP + 1, D], F32).ap()
    cont_full = DT("cont_full", [NC * (KTOP + 1), D], F32).ap()

    with ExitStack() as ctx:
        S = lambda name, shape, dt: ctx.enter_context(nc.sbuf_tensor(pfx + name, shape, dt))
        idb = S("idb_s", [128, 128], BF16)
        ltri = S("ltri_s", [128, 128], F32)
        siota = S("siota_s", [128, 64], F32)
        piota = S("piota_s", [128, 1], F32)
        cvals = S("cvals", [128, 8], F32)
        brep = S("brep", [128, NEXP], F32)
        coreid = S("coreid_s", [128, 1], F32)
        rwb = S("rwb", [128, 8 * NEXP], BF16)
        inde = S("inde", [128, NEXP], F32)
        xbuf = S("xbuf", [128, D], F32)
        xlnbf = S("xlnbf", [128, D], BF16)
        stat = S("stat", [128, 16], F32)
        lgt = S("lgt", [128, NEXP], F32)
        sc_sb = S("sc_sb", [128, GPP * NEXP], F32)
        junk = S("junk", [128, GPP], F32)
        bst = S("bst", [128, 64], F32)
        csml = S("csml", [1, NEXP], F32)
        rc = S("rc", [128, GPP], F32)
        onesb = S("onesb", [128, max(GPP, 128)], F32)
        mask = S("mask", [128, GPP], F32)
        scown = S("scown", [128, GPP], F32)
        hiown = S("hiown", [128, 1], F32)
        pos = S("pos", [128, 64], F32)
        gid = S("gid", [128, 64], F32)
        slot = S("slot", [128, 64], F32)
        gidi = S("gidi", [128, 64], I32)
        sloti = S("sloti", [128, 64], I32)
        bsti = S("bsti", [128, 16], I32)
        ocol = S("ocol", [128, 1], F32)
        idxpre = S("idxpre", [128, NJT], F32)
        idxprei = S("idxprei", [128, NJT], I32)
        glist = S("glist", [128, NJT], F32)
        g8 = S("g8", [128, NEXP], F32)
        xg = S("xg", [128, 2 * D], BF16)
        xgT = S("xgT", [128, D], BF16)
        w1b = S("w1b", [128, 8 * DFF], BF16)
        w2b = S("w2b", [128, 32 * D], BF16)
        b1rep = S("b1rep", [128, 2048], F32)
        b2rep = S("b2rep", [128, D], F32)
        h1b = S("h1b", [128, DFF], BF16)
        h1T = S("h1T", [128, DFF], BF16)
        cont = S("cont", [128, D], F32)
        srcr = S("srcr", [128, GPP], F32)
        srciw = S("srciw", [128, NEXP * GPP], F32)
        vcol8 = S("vcol8", [128, NEXP], F32)
        icol = S("icol", [128, 1], I32)
        gbuf = S("gbuf", [128, D], F32)
        obuf = S("obuf", [128, D], F32)
        vcol = S("vcol", [128, 2], F32)

        P_ = lambda name, shape, dt: ctx.enter_context(nc.psum_tensor(pfx + name, shape, dt))
        ps_tr = P_("ps_tr", [128, 2048], BF16)
        ps_a = P_("ps_a", [128, 512], F32)
        ps_b = P_("ps_b", [128, 1024], F32)
        ps_sm = P_("ps_sm", [128, 8], F32)

        pr = Prog(nc, fu["sem_ctx"] if fu else ctx, pfx)
        s_const = pr.stream("s_const", 1)
        s_lx = pr.stream("s_lx", 2)
        s_sx = pr.stream("s_sx", 2)
        s_sl = pr.stream("s_sl", 2)
        s_w = pr.stream("s_w", 1)
        s_ix = pr.stream("s_ix", 4)
        s_go = pr.stream("s_go", 2)
        blk = fu["blk"] if fu else ctx.enter_context(nc.Block())

        CONST = pr.add("sync", lambda sy: [
            sy.dma_start(idb[:], idb_in[:]),
            sy.dma_start(ltri[:], ltri_in[:]),
            sy.dma_start(siota[:], siota_in.to_broadcast([128, 64])),
            sy.dma_start(piota[:], piota_in[:]),
            sy.dma_start(cvals[:], cval_in.to_broadcast([128, 8])),
            sy.dma_start(brep[:], rb_in.to_broadcast([128, NEXP])),
            sy.dma_start(coreid[:], coreid_in.to_broadcast([128, 1])),
            sy.dma_start(b2rep[:], b2_in.to_broadcast([128, D]))],
            ndma=8, stream=s_const)
        WLOAD = pr.add("gpsimd", lambda g_: [
            g_.dma_start(rwb[:].rearrange("p (dc e) -> p dc e", e=NEXP),
                         rw_in.rearrange("(dc p) e -> p dc e", p=128))] + [
            g_.dma_start(w1b[:, dc * DFF:(dc + 1) * DFF],
                         w1_in[dc * 128:(dc + 1) * 128, :]) for dc in range(8)] + [
            g_.dma_start(w2b[:, fs * D:(fs + 1) * D],
                         w2_in[fs * 128:(fs + 1) * 128, :]) for fs in range(32)],
            ndma=41, stream=s_w)
        indt = pr.add("vector", lambda v: v.scalar_tensor_tensor(
            inde[:], siota[:, 0:8], 1.0, coreid[:].to_broadcast([128, 8]),
            op0=ALU.mult, op1=ALU.is_equal), deps=[CONST])

        p1_ln = [None] * NTT
        p1_vr = [None] * NTT
        p1_sx = [None] * NTT
        for tt in range(NTT):
            ld = pr.add("sync", lambda sy, tt=tt: sy.dma_start(
                xbuf[:], x_in[tt * 128:(tt + 1) * 128, :]), ndma=1, stream=s_lx,
                deps=[p1_ln[tt - 1] if tt >= 1 else None])
            sq_t = pr.add("scalar", lambda s: s.activation(
                gbuf[:], xbuf[:], AF.Square),
                deps=[ld, p1_vr[tt - 1] if tt >= 1 else None])

            def vr_fn(v):
                pr.fence("vector", v, v.reduce_sum(stat[:, 0:1], xbuf[:], axis=AXX))
                pr.fence("vector", v, v.reduce_sum(stat[:, 1:2], gbuf[:], axis=AXX))
                pr.fence("vector", v, v.tensor_scalar(stat[:, 2:3], stat[:, 0:1],
                                                      1.0 / D, None, op0=ALU.mult))
                pr.fence("vector", v, v.tensor_tensor(stat[:, 3:4], stat[:, 2:3],
                                                      stat[:, 2:3], ALU.mult))
                return v.scalar_tensor_tensor(stat[:, 4:5], stat[:, 1:2], 1.0 / D,
                                              stat[:, 3:4], op0=ALU.mult,
                                              op1=ALU.subtract)
            vr = pr.add("vector", vr_fn, deps=[ld, sq_t])
            p1_vr[tt] = vr

            def lnp_fn(s):
                pr.fence("scalar", s, s.activation(stat[:, 5:6], stat[:, 4:5],
                                                   AF.Ln, bias=cvals[:, 0:1]))
                pr.fence("scalar", s, s.activation(stat[:, 6:7], stat[:, 5:6],
                                                   AF.Exp, scale=-0.5))
                pr.fence("scalar", s, s.activation(stat[:, 7:8], stat[:, 2:3],
                                                   AF.Copy, scale=stat[:, 6:7]))
                return s.mul(stat[:, 7:8], stat[:, 7:8], -1.0)
            lnp = pr.add("scalar", lnp_fn, deps=[vr, CONST])
            ln = pr.add("vector", lambda v: v.scalar_tensor_tensor(
                xlnbf[:], xbuf[:], stat[:, 6:7], stat[:, 7:8].to_broadcast([128, D]),
                op0=ALU.mult, op1=ALU.add),
                deps=[lnp, p1_sx[tt - 1] if tt >= 1 else None])
            p1_ln[tt] = ln
            p1_sx[tt] = pr.add("sync", lambda g_, tt=tt: g_.dma_start(
                xlnb_loc[tt * 128:(tt + 1) * 128, :], xlnbf[:]), ndma=1, stream=s_sx,
                deps=[ln])

        prev_tp = None
        prev_ev = None
        prev_sl = None
        prev_mm = None
        for tt in range(NTT):
            llx = pr.add("sync", lambda sy, tt=tt: sy.dma_start(
                xlnbf[:], xlnb_loc[tt * 128:(tt + 1) * 128, :]), ndma=1, stream=s_lx,
                deps=[p1_sx[tt], prev_tp] +
                     ([p1_ln[-1], p1_vr[-1], p1_sx[-1], p1_sx[-2]] if tt == 0 else []))
            tp = pr.add("tensor", lambda t: [
                t.transpose(ps_tr[:, dc * 128:(dc + 1) * 128],
                            xlnbf[:, dc * 128:(dc + 1) * 128], idb[:])
                for dc in range(8)], deps=[llx, CONST, prev_ev])
            prev_tp = tp
            ev = pr.add("vector", lambda v: v.tensor_copy(h1T[:, 0:1024],
                                                          ps_tr[:, 0:1024]),
                        deps=[tp, prev_mm])
            prev_ev = ev
            mm = pr.add("tensor", lambda t: [
                t.matmul(ps_sm[:], h1T[:, dc * 128:(dc + 1) * 128],
                         rwb[:, dc * NEXP:(dc + 1) * NEXP],
                         start=(dc == 0), stop=(dc == 7), skip_group_check=True)
                for dc in range(8)], deps=[ev, WLOAD, prev_sl])
            prev_mm = mm
            evl = pr.add("vector", lambda v: v.tensor_copy(lgt[:], ps_sm[:]),
                         deps=[mm])
            sl = pr.add("sync", lambda g_, tt=tt: g_.dma_start(
                lg_loc[tt * 128:(tt + 1) * 128, :], lgt[:]), ndma=1, stream=s_sl,
                deps=[evl])
            prev_sl = sl

        cc1 = pr.add("gpsimd", lambda g_: g_.collective_compute(
            "AllGather", ALU.bypass, replica_groups=[list(range(NC))],
            ins=[xlnb_loc[:, :].opt()], outs=[xlnb_full[:, :].opt()]),
            deps=[p1_sx[-1], p1_sx[-2], p1_ln[-1]])
        cc2 = pr.add("gpsimd", lambda g_: g_.collective_compute(
            "AllGather", ALU.bypass, replica_groups=[list(range(NC))],
            ins=[lg_loc[:, :].opt()], outs=[lg_full[:, :].opt()]),
            deps=[prev_sl, cc1])

        def lsc_fn(sy):
            out = []
            gq = GPP // 4
            for q4 in range(4):
                out.append(sy.dma_start(
                    sc_sb[:, q4 * gq * NEXP:(q4 + 1) * gq * NEXP]
                    .rearrange("p (g e) -> p g e", e=NEXP),
                    lg_full.rearrange("(g p) e -> p g e", p=128)
                    [:, q4 * gq:(q4 + 1) * gq, :]))
            return out
        lsc = pr.add("sync", lsc_fn, ndma=4, stream=s_lx, deps=[cc2])
        sc3 = sc_sb[:].rearrange("p (g e) -> p g e", e=NEXP)

        def sm1_fn(v):
            brep3 = brep[:].rearrange("p (o e) -> p o e", o=1).to_broadcast(
                [128, GPP, NEXP])
            pr.fence("vector", v, v.tensor_tensor(sc3, sc3, brep3, ALU.add))
            pr.fence("vector", v, v.tensor_reduce(
                rc[:, 0:GPP], sc3, axis=AXX, op=ALU.max))
            rcb = rc[:, 0:GPP].rearrange("p (g o) -> p g o", o=1).to_broadcast(
                [128, GPP, NEXP])
            return v.tensor_tensor(sc3, sc3, rcb, ALU.subtract)
        sm1 = pr.add("vector", sm1_fn, deps=[lsc, CONST])
        sm2 = pr.add("scalar", lambda s: s.activation(sc_sb[:], sc_sb[:], AF.Exp),
                     deps=[sm1])

        def sm3_fn(v):
            pr.fence("vector", v, v.tensor_reduce(
                rc[:, 0:GPP], sc3, axis=AXX, op=ALU.add))
            pr.fence("vector", v, v.reciprocal(rc[:, 0:GPP], rc[:, 0:GPP]))
            rcb = rc[:, 0:GPP].rearrange("p (g o) -> p g o", o=1).to_broadcast(
                [128, GPP, NEXP])
            return v.tensor_tensor(sc3, sc3, rcb, ALU.mult)
        sm3 = pr.add("vector", sm3_fn, deps=[sm2])
        def ssc_fn(g_):
            out = []
            gq = GPP // 4
            for q4 in range(4):
                out.append(g_.dma_start(
                    sc_h.rearrange("(g p) e -> p g e", p=128)
                    [:, q4 * gq:(q4 + 1) * gq, :],
                    sc_sb[:, q4 * gq * NEXP:(q4 + 1) * gq * NEXP]
                    .rearrange("p (g e) -> p g e", e=NEXP)))
            return out
        ssc = pr.add("sync", ssc_fn, ndma=4, stream=s_sl, deps=[sm3])

        blo, bhi = bst[:, 0:8], bst[:, 8:16]
        bm_, bsel = bst[:, 16:24], bst[:, 24:32]
        bcnt, bprt = bst[:, 32:40], bst[:, 40:48]
        bseli, bselni = bsti[:, 0:8], bsti[:, 8:16]
        init = pr.add("vector", lambda v: [
            v.memset(blo, 0.0), v.memset(bhi, 1.0), v.memset(onesb[:], 1.0)],
            deps=[sm3])
        prev_it = init
        for it in range(42):
            def itm_fn(v):
                pr.fence("vector", v, v.tensor_tensor(bm_, blo, bhi, ALU.add))
                pr.fence("vector", v, v.tensor_scalar(bm_, bm_, 0.5, None,
                                                      op0=ALU.mult))
                for e in range(NEXP):
                    sce = sc3[:, :, e:e + 1].rearrange("p g o -> p (g o)")
                    pr.fence("vector", v, v.tensor_scalar(
                        junk[:], sce, bm_[:, e:e + 1], 0.0, op0=ALU.is_gt,
                        op1=ALU.add, accum_out=bprt[:, e:e + 1]))
                return v.nop()
            itm = pr.add("vector", itm_fn, deps=[prev_it])
            summ = pr.add("tensor", lambda t: t.matmul(
                ps_sm[0:1, 0:8], onesb[:, 0:1], bprt,
                start=True, stop=True, skip_group_check=True), deps=[itm, init])
            evc = pr.add("vector", lambda v: v.tensor_copy(
                csml[0:1, :], ps_sm[0:1, 0:8]), deps=[summ])
            rep = pr.add("tensor", lambda t: t.matmul(
                ps_sm[:, 0:8], onesb[0:1, 0:128], csml[0:1, :],
                start=True, stop=True, skip_group_check=True), deps=[evc])

            def upd_fn(v):
                pr.fence("vector", v, v.tensor_copy(bcnt, ps_sm[:, 0:8]))
                pr.fence("vector", v, v.tensor_scalar(bseli, bcnt, float(KTOP),
                                                      None, op0=ALU.is_le))
                pr.fence("vector", v, v.tensor_scalar(bselni, bcnt, float(KTOP),
                                                      None, op0=ALU.is_gt))
                pr.fence("vector", v, v.select(bhi, bseli, bm_, bhi))
                pr.fence("vector", v, v.select(blo, bselni, bm_, blo))
                return v.nop()
            upd = pr.add("vector", upd_fn, deps=[rep])
            prev_it = upd

        def own_fn(v):
            for e in range(NEXP):
                sce = sc3[:, :, e:e + 1].rearrange("p g o -> p (g o)")
                if e == 0:
                    pr.fence("vector", v, v.tensor_scalar(
                        scown[:], sce, inde[:, 0:1], None, op0=ALU.mult))
                else:
                    pr.fence("vector", v, v.scalar_tensor_tensor(
                        scown[:], sce, inde[:, e:e + 1], scown[:],
                        op0=ALU.mult, op1=ALU.add))
            pr.fence("vector", v, v.tensor_tensor(bm_, bhi, inde[:], ALU.mult))
            pr.fence("vector", v, v.tensor_reduce(
                hiown[:], bm_.rearrange("p (o e) -> p o e", o=1),
                axis=AXX, op=ALU.add))
            pr.fence("vector", v, v.tensor_scalar(
                mask[:], scown[:], hiown[:, 0:1], None, op0=ALU.is_gt))
            pr.fence("vector", v, v.tensor_tensor_scan(
                rc[:, 0:GPP], onesb[:, 0:GPP], mask[:], 0.0, ALU.mult, ALU.add))
            return v.nop()
        own = pr.add("vector", own_fn, deps=[prev_it, indt])
        ocm = pr.add("tensor", lambda t: t.matmul(
            ps_sm[:, 0:1], ltri[:], rc[:, GPP - 1:GPP],
            start=True, stop=True, skip_group_check=True), deps=[own])

        def pos_fn(v):
            pr.fence("vector", v, v.tensor_copy(ocol[:], ps_sm[:, 0:1]))
            for s5 in range(64):
                pr.fence("vector", v, v.tensor_scalar(
                    junk[:], rc[:, 0:GPP], float(s5), 0.0, op0=ALU.is_le,
                    op1=ALU.add, accum_out=pos[:, s5:s5 + 1]))
            pr.fence("vector", v, v.scalar_tensor_tensor(
                gid[:], pos[:], 128.0, piota[:].to_broadcast([128, 64]),
                op0=ALU.mult, op1=ALU.add))
            pr.fence("vector", v, v.tensor_scalar(
                slot[:], pos[:], float(GPP) - 0.5, 1e9, op0=ALU.is_gt,
                op1=ALU.mult))
            pr.fence("vector", v, v.tensor_tensor(gid[:], gid[:], slot[:],
                                                  ALU.add))
            pr.fence("vector", v, v.scalar_tensor_tensor(
                slot[:], slot[:], 1.0, siota[:, 0:64], op0=ALU.mult, op1=ALU.add))
            pr.fence("vector", v, v.tensor_tensor(
                slot[:], slot[:], ocol[:].to_broadcast([128, 64]), ALU.add))
            pr.fence("vector", v, v.tensor_scalar(
                gid[:], gid[:], float(TOT - 1), None, op0=ALU.min))
            pr.fence("vector", v, v.tensor_scalar(
                slot[:], slot[:], float(KTOP), None, op0=ALU.min))
            pr.fence("vector", v, v.tensor_copy(gidi[:], gid[:]))
            pr.fence("vector", v, v.tensor_copy(sloti[:], slot[:]))
            pr.fence("vector", v, v.memset(idxpre[:], 0.0))
            pr.fence("vector", v, v.tensor_copy(idxprei[:], idxpre[:]))
            return v.nop()
        post = pr.add("vector", pos_fn, deps=[ocm, CONST])
        pre = pr.add("sync", lambda g_: g_.dma_start(
            idx_h[0:KTOP].rearrange("(jt p) -> p jt", p=128), idxprei[:],
            allow_slow_non_contiguous=True), ndma=1, stream=s_sl, deps=[post])
        scts = []
        for s5 in range(64):
            sct = pr.add("gpsimd", lambda g_, s5=s5: g_.indirect_dma_start(
                idx_h.rearrange("(a b) -> a b", b=1),
                IOff(sloti[:, s5:s5 + 1], 0), gidi[:, s5:s5 + 1], None),
                ndma=1, stream=s_ix, deps=[post, pre])
            scts.append(sct)
        lidx = pr.add("sync", lambda sy: sy.dma_start(
            idxprei[:], idx_h[0:KTOP].rearrange("(jt p) -> p jt", p=128),
            allow_slow_non_contiguous=True), ndma=1, stream=s_lx, deps=scts)

        xgz = pr.add("vector", lambda v: v.memset(xg[:], 0.0), deps=[])
        gzz = pr.add("vector", lambda v: v.memset(g8[:], 0.0), deps=[])
        pv7 = {"tp": None, "ev": None, "mm1": None, "hr": None, "tph": None,
               "tph2": None, "evh": None, "mm2": None, "co": None, "sc": None,
               "gx": [None] * NJT, "tps": [None] * NJT, "gsel": None}
        for jt in range(NJT):
            xslot = jt % 2
            gg = pr.add("gpsimd", lambda g_, jt=jt: g_.indirect_dma_start(
                g8[:], None, sc_h, IOff(idxprei[:, jt:jt + 1], 0)),
                ndma=1, stream=s_ix, deps=[lidx, ssc, gzz, pv7["gsel"]])

            def gsel_fn(v, jt=jt):
                pr.fence("vector", v, v.tensor_tensor(
                    g8[:].rearrange("p (o e) -> p o e", o=1),
                    g8[:].rearrange("p (o e) -> p o e", o=1),
                    inde[:].rearrange("p (o e) -> p o e", o=1), ALU.mult))
                return v.tensor_reduce(
                    glist[:, jt:jt + 1],
                    g8[:].rearrange("p (o e) -> p o e", o=1), axis=AXX,
                    op=ALU.add)
            gsel = pr.add("vector", gsel_fn, deps=[gg, indt])
            pv7["gsel"] = gsel
            gx = pr.add("gpsimd", lambda g_, jt=jt, xslot=xslot: g_.indirect_dma_start(
                xg[:, xslot * D:(xslot + 1) * D], None, xlnb_full,
                IOff(idxprei[:, jt:jt + 1], 0)),
                ndma=1, stream=s_ix,
                deps=[lidx, cc1, xgz, pv7["tps"][jt - 2] if jt >= 2 else None,
                      pv7["gx"][jt - 2] if jt >= 2 else None])
            pv7["gx"][jt] = gx
            tp = pr.add("tensor", lambda t, xslot=xslot: [
                t.transpose(ps_tr[:, dc * 128:(dc + 1) * 128],
                            xg[:, xslot * D + dc * 128:xslot * D + (dc + 1) * 128],
                            idb[:])
                for dc in range(8)], deps=[gx, CONST, pv7["ev"]])
            pv7["tp"] = tp
            pv7["tps"][jt] = tp
            ev = pr.add("vector", lambda v: v.tensor_copy(xgT[:], ps_tr[:, 0:1024]),
                        deps=[tp, pv7["mm1"]])
            pv7["ev"] = ev
            for fc in range(8):
                if fc % 4 == 0:
                    lb1 = pr.add("sync", lambda sy, fc=fc: sy.dma_start(
                        b1rep[:], b1_in[0:1, fc * 512:fc * 512 + 2048]
                        .to_broadcast([128, 2048])), ndma=1, stream=s_lx,
                        deps=[pv7["hr"]])
                mm1 = pr.add("tensor", lambda t, fc=fc: [
                    t.matmul(ps_a[:], xgT[:, dc * 128:(dc + 1) * 128],
                             w1b[:, dc * DFF + fc * 512:dc * DFF + (fc + 1) * 512],
                             start=(dc == 0), stop=(dc == 7),
                             skip_group_check=True)
                    for dc in range(8)],
                    deps=[ev, WLOAD, pv7["hr"]])
                pv7["mm1"] = mm1

                def hr_fn(v, fc=fc):
                    pr.fence("vector", v, v.scalar_tensor_tensor(
                        gbuf[:, 0:512], ps_a[:], 1.0,
                        b1rep[:, (fc % 4) * 512:(fc % 4 + 1) * 512],
                        op0=ALU.mult, op1=ALU.add))
                    return v.tensor_scalar(h1b[:, fc * 512:(fc + 1) * 512],
                                           gbuf[:, 0:512], 0.0, None, op0=ALU.max)
                hr = pr.add("vector", hr_fn, deps=[mm1, CONST, lb1, pv7["tph2"]])
                pv7["hr"] = hr
            tph = pr.add("tensor", lambda t: [
                t.transpose(ps_tr[:, i * 128:(i + 1) * 128],
                            h1b[:, i * 128:(i + 1) * 128], idb[:])
                for i in range(8)], deps=[pv7["hr"], pv7["evh"]])
            pv7["tph"] = tph
            evh = pr.add("vector", lambda v: v.tensor_copy(
                h1T[:, 0:1024], ps_tr[:, 0:1024]), deps=[tph, pv7["mm2"]])
            tph2 = pr.add("tensor", lambda t: [
                t.transpose(ps_tr[:, 1024 + i * 128:1024 + (i + 1) * 128],
                            h1b[:, 1024 + i * 128:1024 + (i + 1) * 128], idb[:])
                for i in range(8)] + [
                t.transpose(ps_tr[:, i * 128:(i + 1) * 128],
                            h1b[:, 2048 + i * 128:2048 + (i + 1) * 128], idb[:])
                for i in range(8)], deps=[pv7["hr"], evh, pv7["evh"]])
            evh2 = pr.add("vector", lambda v: [
                v.tensor_copy(h1T[:, 1024:2048], ps_tr[:, 1024:2048]),
                v.tensor_copy(h1T[:, 2048:3072], ps_tr[:, 0:1024])],
                deps=[tph2, pv7["mm2"]])
            tph3 = pr.add("tensor", lambda t: [
                t.transpose(ps_tr[:, 1024 + i * 128:1024 + (i + 1) * 128],
                            h1b[:, 3072 + i * 128:3072 + (i + 1) * 128], idb[:])
                for i in range(8)], deps=[pv7["hr"], evh2])
            pv7["tph2"] = tph3
            evh3 = pr.add("vector", lambda v: v.tensor_copy(
                h1T[:, 3072:4096], ps_tr[:, 1024:2048]), deps=[tph3, pv7["mm2"]])
            pv7["evh"] = evh3
            mm2 = pr.add("tensor", lambda t: [
                t.matmul(ps_b[:, h2 * 512:(h2 + 1) * 512],
                         h1T[:, fs * 128:(fs + 1) * 128],
                         w2b[:, fs * D + h2 * 512:fs * D + (h2 + 1) * 512],
                         start=(fs == 0), stop=(fs == 31), skip_group_check=True)
                for fs in range(32) for h2 in range(2)],
                deps=[evh, evh2, evh3, WLOAD, pv7["co"]])
            pv7["mm2"] = mm2

            def co_fn(v, jt=jt):
                pr.fence("vector", v, v.tensor_tensor(cont[:], ps_b[:, 0:1024],
                                                      b2rep[:], ALU.add))
                return v.tensor_scalar(cont[:], cont[:], glist[:, jt:jt + 1],
                                       None, op0=ALU.mult)
            co = pr.add("vector", co_fn, deps=[mm2, gsel, CONST])
            pv7["co"] = co
            sc_ = pr.add("sync", lambda g_, jt=jt: g_.dma_start(
                cont_loc[jt * 128:(jt + 1) * 128, :], cont[:]), ndma=1,
                stream=s_sl, deps=[co])
            pv7["sc"] = sc_

        zr = pr.add("vector", lambda v: v.memset(gbuf[:], 0.0), deps=[])
        zrs = pr.add("sync", lambda sy: sy.dma_start(
            cont_loc[KTOP:KTOP + 1, :], gbuf[0:1, :]), ndma=1, stream=s_sl,
            deps=[zr])

        cc3 = pr.add("gpsimd", lambda g_: g_.collective_compute(
            "AllGather", ALU.bypass, replica_groups=[list(range(NC))],
            ins=[cont_loc[:, :].opt()], outs=[cont_full[:, :].opt()]),
            deps=[pv7["sc"], pv7["co"], zrs])

        prev_e = None
        for e in range(NEXP):
            def msk_fn(v, e=e):
                sce = sc3[:, :, e:e + 1].rearrange("p g o -> p (g o)")
                pr.fence("vector", v, v.tensor_scalar(
                    mask[:], sce, bst[:, 8 + e:9 + e], None, op0=ALU.is_gt))
                pr.fence("vector", v, v.tensor_tensor_scan(
                    rc[:, 0:GPP], onesb[:, 0:GPP], mask[:], 0.0,
                    ALU.mult, ALU.add))
                return v.nop()
            msk = pr.add("vector", msk_fn, deps=[prev_e or post])
            ocm2 = pr.add("tensor", lambda t: t.matmul(
                ps_sm[:, 0:1], ltri[:], rc[:, GPP - 1:GPP],
                start=True, stop=True, skip_group_check=True), deps=[msk])

            def src_fn(v, e=e):
                pr.fence("vector", v, v.tensor_copy(ocol[:], ps_sm[:, 0:1]))
                pr.fence("vector", v, v.tensor_scalar(
                    srcr[:], rc[:, 0:GPP], 1.0, float(e * (KTOP + 1)) - 1.0,
                    op0=ALU.mult, op1=ALU.add))
                pr.fence("vector", v, v.tensor_tensor(
                    srcr[:], srcr[:], ocol[:].to_broadcast([128, GPP]), ALU.add))
                pr.fence("vector", v, v.tensor_scalar(
                    scown[:], mask[:], 0.5, 1e9, op0=ALU.is_lt, op1=ALU.mult))
                pr.fence("vector", v, v.tensor_tensor(
                    srcr[:], srcr[:], scown[:], ALU.add))
                pr.fence("vector", v, v.tensor_scalar(
                    srcr[:], srcr[:], float(e * (KTOP + 1) + KTOP), None,
                    op0=ALU.min))
                pr.fence("vector", v, v.tensor_copy(
                    srciw[:, e * GPP:(e + 1) * GPP], srcr[:]))
                return v.nop()
            st_ = pr.add("vector", src_fn, deps=[ocm2])
            prev_e = st_
        gbz = pr.add("vector", lambda v: v.memset(gbuf[:], 0.0), deps=[])
        pv9 = {"ro": None, "go": None, "acc": None}
        for tt in range(NTT):
            lxr = pr.add("sync", lambda sy, tt=tt: sy.dma_start(
                obuf[:], x_in[tt * 128:(tt + 1) * 128, :]), ndma=1, stream=s_go,
                deps=[pv9["go"], p1_ln[-1], p1_vr[-1]])
            pv9["acc"] = lxr
            for e in range(NEXP):
                def sel_fn(v, e=e, tt=tt):
                    view = srciw[:, e * GPP:(e + 1) * GPP].rearrange(
                        "p (b t) -> p b t", t=NTT)[:, :, tt:tt + 1].rearrange(
                        "p b o -> p (b o)")
                    pr.fence("vector", v, v.tensor_tensor(
                        vcol8[:], view, inde[:], ALU.mult))
                    pr.fence("vector", v, v.tensor_reduce(
                        vcol[:, 0:1], vcol8[:].rearrange("p (o b) -> p o b", o=1),
                        axis=AXX, op=ALU.add))
                    pr.fence("vector", v, v.tensor_copy(icol[:], vcol[:, 0:1]))
                    return v.nop()
                sel = pr.add("vector", sel_fn, deps=[prev_e, indt, pv9["gei"]
                                                     if "gei" in pv9 else None])
                ge = pr.add("gpsimd", lambda g_: g_.indirect_dma_start(
                    gbuf[:], None, cont_full, IOff(icol[:], 0)), ndma=1,
                    stream=s_ix,
                    deps=[cc3, sel, gbz, pv9["acc"]])
                pv9["gei"] = ge
                ac = pr.add("vector", lambda v: v.tensor_tensor(
                    obuf[:], obuf[:], gbuf[:], ALU.add),
                    deps=[ge, lxr, pv9["ro"]])
                pv9["acc"] = ac
                pv9["ro"] = ac
            go = pr.add("sync", lambda sy, tt=tt: sy.dma_start(
                xout[tt * 128:(tt + 1) * 128, :], obuf[:]), ndma=1, stream=s_go,
                deps=[pv9["acc"]])
            pv9["go"] = go

        if fu:
            return _fuse_tail(nc, pr, blk, fu, pfx, last=True)
        pr.add("gpsimd", lambda g_: g_.nop(), deps=pr.barrier_deps())
        pr.emit(blk)
    return nc


def moe_in_maps(x_full, rw, rb, w1, b1, w2, b2, NC=8, pfx="", with_x=True):
    ltri = np.triu(np.ones((128, 128), np.float32), 1)
    base = {
        "rw": np.ascontiguousarray(rw), "rb": np.asarray(rb, np.float32)[None, :],
        "idb": np.eye(128, dtype=np.float32).astype(BF),
        "ltri": ltri,
        "siota": np.arange(64, dtype=np.float32)[None, :],
        "piota": np.arange(128, dtype=np.float32)[:, None],
        "cval": np.array([[LN_EPS, RMS_EPS, 1.0, 0, 0, 0, 0, 0]], dtype=np.float32),
    }
    base = {pfx + k: v for k, v in base.items()}
    in_maps = []
    for b in range(NC):
        m = dict(base)
        if with_x:
            m[pfx + "x"] = np.ascontiguousarray(x_full[b])
        m[pfx + "w1"] = np.ascontiguousarray(w1[b])
        m[pfx + "b1"] = np.ascontiguousarray(b1[b])[None, :]
        m[pfx + "w2"] = np.ascontiguousarray(w2[b])
        m[pfx + "b2"] = np.ascontiguousarray(b2[b])[None, :]
        m[pfx + "coreid"] = np.array([[b]], dtype=np.float32)
        in_maps.append(m)
    return in_maps


def run_l1_ffn(x_full, rw, rb, w1, b1, w2, b2, T=4096, raw=False):
    NC = x_full.shape[0]
    nc = build_l1_ffn(T, NC)
    in_maps = moe_in_maps(x_full, rw, rb, w1, b1, w2, b2, NC)
    res = run_bass_kernel_spmd(nc, in_maps, core_ids=list(range(NC)))
    if raw:
        return res
    return np.stack([np.asarray(r["xout"]) for r in res.results])



def build_fused(T=4096, NC=8):
    nc = bass.Bass("TRN2", target_bir_lowering=False)
    x0 = nc.dram_tensor("x0", [T, D], F32, kind="ExternalInput").ap()
    x1 = nc.dram_tensor("x1", [T, D], F32).ap()
    x2 = nc.dram_tensor("x2", [T, D], F32).ap()
    x3 = nc.dram_tensor("x3", [T, D], F32).ap()
    x4 = nc.dram_tensor("x4", [T, D], F32, kind="ExternalOutput").ap()
    with ExitStack() as sctx:
        with nc.Block() as blk:
            fu = dict(nc=nc, blk=blk, sem_ctx=sctx)
            b0 = build_l0_attn(T, fu=dict(fu, pfx="a_", x_in=x0, x_out=x1,
                                          bar_prev=None))
            b1 = build_l0_ffn(T, fu=dict(fu, pfx="f_", x_in=x1, x_out=x2,
                                         bar_prev=b0))
            b2 = build_l1_attn(T, fu=dict(fu, pfx="s_", x_in=x2, x_out=x3,
                                          bar_prev=b1))
            build_l1_ffn(T, NC, fu=dict(fu, pfx="m_", x_in=x3, x_out=x4,
                                        bar_prev=b2))
    return nc


def fused_in_maps(x_full, Wq, Wk, Wv, Wo, qnw, knw, wg, wu, wd,
                  xpw, dtw, dtb, alog, dpar, lna, lnb,
                  rw, rb, w1, b1, w2, b2, T=4096, NC=8):
    a = attn_in_maps(x_full, Wq, Wk, Wv, Wo, qnw, knw, T, pfx="a_",
                     with_x=False)
    f = ffn_in_maps(x_full, wg, wu, wd, pfx="f_", with_x=False)
    s = ssm_in_maps(x_full, xpw, dtw, dtb, alog, dpar, lna, lnb, pfx="s_",
                    with_x=False)
    m = moe_in_maps(x_full, rw, rb, w1, b1, w2, b2, NC, pfx="m_",
                    with_x=False)
    out = []
    for b in range(NC):
        d = {}
        d.update(a[b]); d.update(f[b]); d.update(s[b]); d.update(m[b])
        d["x0"] = np.ascontiguousarray(x_full[b])
        out.append(d)
    return out


def kernel_multi(**inputs):
    g = lambda k: np.asarray(inputs[k], dtype=np.float32)
    x = g("L0_attn_x")
    x = run_l0_attn(x, g("L0_attn_Wq"), g("L0_attn_Wk"), g("L0_attn_Wv"),
                    g("L0_attn_Wo"), g("L0_attn_q_norm_w"),
                    g("L0_attn_k_norm_w"), T=4096)
    x = run_l0_ffn(x, g("L0_ffn_w_gate"), g("L0_ffn_w_up"),
                   g("L0_ffn_w_down"), T=4096)
    x = run_l1_attn(x, g("L1_attn_x_proj_w"), g("L1_attn_dt_proj_w"),
                    g("L1_attn_dt_proj_b"), g("L1_attn_A_log"),
                    g("L1_attn_D_param"), g("L1_attn_ln_a"),
                    g("L1_attn_ln_b"), T=4096)
    x = run_l1_ffn(x, g("L1_ffn_router_w"), g("L1_ffn_router_b"),
                   g("L1_ffn_w1"), g("L1_ffn_b1"), g("L1_ffn_w2"),
                   g("L1_ffn_b2"), T=4096)
    return np.ascontiguousarray(x.astype(np.float32))


def kernel(**inputs):
    g = lambda k: np.asarray(inputs[k], dtype=np.float32)
    x = g("L0_attn_x")
    nc = build_fused(4096, 8)
    in_maps = fused_in_maps(
        x, g("L0_attn_Wq"), g("L0_attn_Wk"), g("L0_attn_Wv"),
        g("L0_attn_Wo"), g("L0_attn_q_norm_w"), g("L0_attn_k_norm_w"),
        g("L0_ffn_w_gate"), g("L0_ffn_w_up"), g("L0_ffn_w_down"),
        g("L1_attn_x_proj_w"), g("L1_attn_dt_proj_w"), g("L1_attn_dt_proj_b"),
        g("L1_attn_A_log"), g("L1_attn_D_param"), g("L1_attn_ln_a"),
        g("L1_attn_ln_b"), g("L1_ffn_router_w"), g("L1_ffn_router_b"),
        g("L1_ffn_w1"), g("L1_ffn_b1"), g("L1_ffn_w2"), g("L1_ffn_b2"),
        T=4096, NC=8)
    res = run_bass_kernel_spmd(nc, in_maps, core_ids=list(range(8)))
    out = np.stack([np.asarray(r["x4"]) for r in res.results])
    return np.ascontiguousarray(out.astype(np.float32))


# version 03
# speedup vs baseline: 1.0963x; 1.0963x over previous
import math
import numpy as np
import ml_dtypes
from contextlib import ExitStack

import concourse.bass as bass
import concourse.mybir as mybir
from concourse.bass_utils import run_bass_kernel_spmd
from concourse.alu_op_type import AluOpType as ALU

F32 = mybir.dt.float32
BF16 = mybir.dt.bfloat16
I32 = mybir.dt.int32
AF = mybir.ActivationFunctionType
IOff = bass.IndirectOffsetOnAxis
BF = ml_dtypes.bfloat16
AXX = mybir.AxisListType.X

D = 1024
NH = 16
HD = 64
LN_EPS = 1e-5
RMS_EPS = 1e-6


class Task:
    pass


class DmaStream:
    def __init__(self, ctx, nc, name, R):
        self.sems = [ctx.enter_context(nc.semaphore(f"{name}{i}")) for i in range(R)]
        self.R = R
        self.counts = [0] * R
        self.ntasks = 0


class Prog:
    CQUEUES = ["tensor", "vector", "scalar", "gpsimd"]

    def __init__(self, nc, ctx, pfx=""):
        self.nc = nc
        self.ctx = ctx
        self.pfx = pfx
        self.tasks = []
        self.chans = {e: ctx.enter_context(nc.semaphore(f"{pfx}ch_{e}")) for e in Prog.CQUEUES}
        self.ccounts = {e: 0 for e in Prog.CQUEUES}
        self.hz = {e: ctx.enter_context(nc.semaphore(f"{pfx}hz_{e}")) for e in Prog.CQUEUES}
        self.hzc = {e: 0 for e in Prog.CQUEUES}
        self.streams = []

    def fence(self, ename, eng, inst):
        self.hzc[ename] += 1
        inst.then_inc(self.hz[ename], 1)
        eng.wait_ge(self.hz[ename], self.hzc[ename])

    def stream(self, name, R=2):
        s = DmaStream(self.ctx, self.nc, self.pfx + name, R)
        self.streams.append(s)
        return s

    def add(self, engine, fn, deps=(), ndma=None, stream=None):
        t = Task()
        t.engine = engine
        t.fn = fn
        t.deps = [d for d in deps if d is not None]
        if ndma is not None:
            assert stream is not None, "dma task needs a stream"
            t.kind = "d"
            t.stream = stream
            t.nev = ndma
            t.sem_idx = stream.ntasks % stream.R
            stream.ntasks += 1
            stream.counts[t.sem_idx] += ndma
            t.wait_count = stream.counts[t.sem_idx] * 16
        else:
            assert engine != "sync"
            t.kind = "c"
            t.stream = None
            self.ccounts[engine] += 1
            t.wait_count = self.ccounts[engine]
        self.tasks.append(t)
        return t

    def emit(self, blk, open_waits=()):
        queues = [("sync", None)] + [(e, None) for e in ["gpsimd", "tensor", "vector", "scalar"]]
        for ename, _ in queues:
            if ename == "sync":
                mine = [t for t in self.tasks if t.engine == "sync"]
            else:
                mine = [t for t in self.tasks if t.engine == ename]
            if not mine and not open_waits:
                continue
            section = getattr(blk, ename)

            def make(mine=mine, ename=ename):
                def body(eng):
                    for sem, cnt in open_waits:
                        eng.wait_ge(sem, cnt)
                    for t in mine:
                        waits = {}
                        for d in t.deps:
                            if d.kind == "c":
                                if d.engine == t.engine and t.kind == "c":
                                    continue
                                key = self.chans[d.engine]
                            else:
                                if (t.kind == "d" and d.stream is t.stream
                                        and d.sem_idx == t.sem_idx):
                                    pass
                                key = d.stream.sems[d.sem_idx]
                            waits[key] = max(waits.get(key, 0), d.wait_count)
                        for key, cnt in waits.items():
                            eng.wait_ge(key, cnt)
                        out = t.fn(eng)
                        insts = out if isinstance(out, (list, tuple)) else [out]
                        if t.kind == "d":
                            assert len(insts) == t.nev, (ename, len(insts), t.nev)
                            sem = t.stream.sems[t.sem_idx]
                            for inst in insts:
                                inst.then_inc(sem, 16)
                        else:
                            insts[-1].then_inc(self.chans[t.engine], 1)
                return body
            section(make())

    def barrier_deps(self):
        out = []
        last_c = {}
        for t in self.tasks:
            if t.kind == "c":
                last_c[t.engine] = t
        out.extend(last_c.values())
        for s in self.streams:
            seen = set()
            for t in reversed(self.tasks):
                if t.kind == "d" and t.stream is s and t.sem_idx not in seen:
                    seen.add(t.sem_idx)
                    out.append(t)
                if len(seen) == s.R:
                    break
        return out


def build_l0_attn(T=4096, debug=False, fu=None):
    NTT = T // 128
    NQC = T // 512
    pfx = fu["pfx"] if fu else ""
    nc = fu["nc"] if fu else bass.Bass("TRN2", target_bir_lowering=False)
    DT = lambda name, *a, **k: nc.dram_tensor(pfx + name, *a, **k)

    if fu:
        x_in, xout = fu["x_in"], fu["x_out"]
    else:
        x_in = DT("x", [T, D], F32, kind="ExternalInput").ap()
        xout = DT("xout", [T, D], F32, kind="ExternalOutput").ap()
    wq_in = DT("wq", [D, D], F32, kind="ExternalInput").ap()
    wk_in = DT("wk", [D, D], F32, kind="ExternalInput").ap()
    wv_in = DT("wv", [D, D], F32, kind="ExternalInput").ap()
    wo_in = DT("wo", [D, D], F32, kind="ExternalInput").ap()
    qnw_in = DT("qnw", [1, 512], F32, kind="ExternalInput").ap()
    knw_in = DT("knw", [1, 512], F32, kind="ExternalInput").ap()
    cos_in = DT("cosr", [T, 32], BF16, kind="ExternalInput").ap()
    sin_in = DT("sinr", [T, 32], BF16, kind="ExternalInput").ap()
    idb_in = DT("idb", [128, 128], BF16, kind="ExternalInput").ap()
    idf_in = DT("idf", [128, 128], F32, kind="ExternalInput").ap()
    wedge_in = DT("wedge", [128, 896], BF16, kind="ExternalInput").ap()
    cval_in = DT("cval", [1, 8], F32, kind="ExternalInput").ap()

    dbg = dict(kind="ExternalOutput") if debug else {}
    xlnT_h = DT("xlnT_h", [NTT, 128, D], BF16, **dbg).ap()
    avnf_h = DT("avnf_h", [8, NTT, 128, 128], BF16, **dbg).ap()
    negm_h = DT("negm_h", [32, 128], F32, **dbg).ap()

    with ExitStack() as ctx:
        S = lambda name, shape, dt: ctx.enter_context(nc.sbuf_tensor(pfx + name, shape, dt))
        idb = S("idb_s", [128, 128], BF16)
        idf = S("idf_s", [128, 128], F32)
        wedge = S("wedge_s", [128, 896], BF16)
        cvals = S("cvals", [128, 8], F32)
        wqrep = S("wqrep", [128, 512], F32)
        wkrep = S("wkrep", [128, 512], F32)
        cosr = S("cosr_s", [128, NTT * 32], BF16)
        sinr = S("sinr_s", [128, NTT * 32], BF16)
        xbuf = S("xbuf", [128, D], F32)
        stat = S("stat", [128, 16], F32)
        sqk = S("sqk", [128, 512], F32)
        xlnbf = S("xlnbf", [128, D], BF16)
        stg = S("stg", [128, D], BF16)
        wbuf = S("wbuf", [128, 4096], BF16)
        xts = S("xts", [128, 2 * D], BF16)
        qks = S("qks", [128, 2 * 512], F32)
        ropa = S("ropa", [128, 2 * 256], F32)
        qtok = S("qtok", [128, 512], BF16)
        ktok = S("ktok", [128, 512], BF16)
        rms = S("rms", [128, 32], F32)
        qf = S("qf", [128, 4 * T], BF16)
        kf = S("kf", [128, 4 * T], BF16)
        vg = S("vg", [128, NTT * 520], BF16)
        kst = S("kst", [80, T], BF16)
        qst = S("qst", [80, T], BF16)
        astat = S("astat", [128, 32], F32)
        astT = S("astT", [32, 128], F32)
        pt = S("pt", [128, 2 * 1024], BF16)
        av65 = S("av65", [80, 512], F32)
        avtok = S("avtok", [128, NTT * 512], BF16)
        ztok = S("ztok", [128, NTT * 8], F32)
        rzs = S("rzs", [128, NTT * 8], F32)
        avn = S("avn", [128, 1024], BF16)
        avstg = S("avstg", [128, 1024], BF16)
        avf = S("avf", [128, 2 * D], BF16)
        obuf = S("obuf", [128, 1024], F32)

        P_ = lambda name, shape, dt: ctx.enter_context(nc.psum_tensor(pfx + name, shape, dt))
        ps_tr = P_("ps_tr", [128, 2048], BF16)
        ps_big = P_("ps_big", [128, 2048], F32)
        ps_av = P_("ps_av", [80, 512], F32)
        ps_ms = P_("ps_ms", [128, 512], F32)
        ps_q = ps_big[:, 0:512]
        ps_k = ps_big[:, 512:1024]
        ps_v = ps_big[:, 1024:1536]

        pr = Prog(nc, fu["sem_ctx"] if fu else ctx, pfx)
        s_const = pr.stream("s_const", 1)
        s_lx = pr.stream("s_lx", 2)
        s_stg = pr.stream("s_stg", 2)
        s_w = pr.stream("s_w", 1)
        s_xt = pr.stream("s_xt", 2)
        s_negm = pr.stream("s_negm", 2)
        s_gav = pr.stream("s_gav", 2)
        s_lwo = pr.stream("s_lwo", 1)
        s_lav = pr.stream("s_lav", 2)
        s_lxo = pr.stream("s_lxo", 2)
        s_go = pr.stream("s_go", 2)
        s_kq = pr.stream("s_kq", 2)
        blk = fu["blk"] if fu else ctx.enter_context(nc.Block())

        def t_const(sy):
            out = []
            for a, b in [(idb, idb_in), (idf, idf_in), (wedge, wedge_in)]:
                out.append(sy.dma_start(a[:], b[:]))
            out.append(sy.dma_start(cvals[:], cval_in.to_broadcast([128, 8])))
            out.append(sy.dma_start(wqrep[:], qnw_in.to_broadcast([128, 512])))
            out.append(sy.dma_start(wkrep[:], knw_in.to_broadcast([128, 512])))
            for t_in, t_sb in [(cos_in, cosr), (sin_in, sinr)]:
                out.append(sy.dma_start(
                    t_sb[:].rearrange("p (tt i) -> p tt i", i=32),
                    t_in.rearrange("(tt p) i -> p tt i", p=128)))
            return out
        CONST = pr.add("sync", t_const, ndma=8, stream=s_const)
        vg4 = vg[:].rearrange("p (tt h c) -> p tt h c", h=8, c=65)
        ones_set = pr.add("vector", lambda v: [
            v.memset(kst[64:65, :], 1.0),
            v.memset(vg4[:, :, :, 64:65], 1.0)], deps=[])

        ln_done = [None] * NTT
        vr_done = [None] * NTT
        tp_done = [None] * NTT
        ev_done = [None] * NTT
        st_done = [None] * NTT
        sqs = qks[:, 0:D]
        for tt in range(NTT):
            xb = xbuf[:, 0:D]
            xlb = xlnbf[:, 0:D]
            ld = pr.add("sync", lambda sy, xb=xb, tt=tt: sy.dma_start(
                xb, x_in[tt * 128:(tt + 1) * 128, :]), ndma=1, stream=s_lx,
                deps=[ln_done[tt - 1] if tt >= 1 else None])
            sq_t = pr.add("scalar", lambda s, xb=xb: s.activation(
                sqs[:], xb, AF.Square),
                deps=[ld, vr_done[tt - 1] if tt >= 1 else None])
            def vr_fn(v, xb=xb):
                pr.fence("vector", v, v.reduce_sum(stat[:, 0:1], xb, axis=AXX))
                pr.fence("vector", v, v.reduce_sum(stat[:, 1:2], sqs[:], axis=AXX))
                pr.fence("vector", v, v.tensor_scalar(
                    stat[:, 2:3], stat[:, 0:1], 1.0 / D, None, op0=ALU.mult))
                pr.fence("vector", v, v.tensor_tensor(
                    stat[:, 3:4], stat[:, 2:3], stat[:, 2:3], ALU.mult))
                return v.scalar_tensor_tensor(stat[:, 4:5], stat[:, 1:2], 1.0 / D,
                                              stat[:, 3:4], op0=ALU.mult,
                                              op1=ALU.subtract)
            vr_done[tt] = pr.add("vector", vr_fn,
                deps=[ld, sq_t, ln_done[tt - 1] if tt >= 1 else None])
            def lnp_fn(s):
                pr.fence("scalar", s, s.activation(stat[:, 5:6], stat[:, 4:5],
                                                   AF.Ln, bias=cvals[:, 0:1]))
                pr.fence("scalar", s,
                         s.activation(stat[:, 6:7], stat[:, 5:6], AF.Exp, scale=-0.5))
                pr.fence("scalar", s, s.activation(stat[:, 7:8], stat[:, 2:3],
                                                   AF.Copy, scale=stat[:, 6:7]))
                return s.mul(stat[:, 7:8], stat[:, 7:8], -1.0)
            lnp = pr.add("scalar", lnp_fn, deps=[vr_done[tt], CONST])
            ln_done[tt] = pr.add("vector", lambda v, xb=xb, xlb=xlb: v.scalar_tensor_tensor(
                xlb, xb, stat[:, 6:7], stat[:, 7:8].to_broadcast([128, D]),
                op0=ALU.mult, op1=ALU.add),
                deps=[lnp, tp_done[tt - 1] if tt >= 1 else None])
            half = (tt % 2) * 1024
            tp_done[tt] = pr.add("tensor", lambda t, xlb=xlb, half=half: [
                t.transpose(ps_tr[:, half + dc * 128:half + (dc + 1) * 128],
                            xlb[:, dc * 128:(dc + 1) * 128], idb[:])
                for dc in range(8)],
                deps=[CONST, ln_done[tt], ev_done[tt - 2] if tt >= 2 else None])
            ev_done[tt] = pr.add("vector", lambda v, half=half: v.tensor_copy(
                stg[:], ps_tr[:, half:half + 1024]),
                deps=[tp_done[tt], st_done[tt - 1] if tt >= 1 else None])
            st_done[tt] = pr.add("gpsimd", lambda g, tt=tt: g.dma_start(
                xlnT_h[tt, :, :], stg[:]), ndma=1, stream=s_stg, deps=[ev_done[tt]])

        gav_done = []
        last_evA = None
        mm_all = []
        eBslot = [None] * 4
        AVslot = [None] * 4
        last_red = None
        for g in range(2):
            gofs = g * 512
            prev_grp = pr.barrier_deps() if g == 1 else []
            p1_tail = [ev_done[-1], ev_done[-2]] if g == 0 else []
            wld = [None] * 3
            mmv = {}
            psum_free = {}
            tqk_done = {}
            evq_done = {}
            for wi, w_in in enumerate([wq_in, wk_in, wv_in]):
                wld[wi] = pr.add("gpsimd", lambda gp, w_in=w_in, gofs=gofs: [
                    gp.dma_start(wbuf[:, dc * 512:(dc + 1) * 512],
                                 w_in[dc * 128:(dc + 1) * 128, gofs:gofs + 512])
                    for dc in range(8)], ndma=8, stream=s_w,
                    deps=([mmv[(wi - 1, NTT - 1)]] if wi > 0 else []) + prev_grp
                         + p1_tail)
                for tt in range(NTT):
                    git = len(mm_all)
                    xo = (git % 2) * D
                    lx = pr.add("sync", lambda sy, tt=tt, xo=xo: sy.dma_start(
                        xts[:, xo:xo + D], xlnT_h[tt, :, :]), ndma=1,
                        stream=s_xt,
                        deps=[st_done[tt], mm_all[git - 2] if git >= 2 else None])
                    po = (tt % 2) * 512
                    psd = ps_big[:, po:po + 512]
                    war = [psum_free.get((wi, tt - 2))]
                    if tt < 2:
                        war.append(psum_free.get((wi - 1, NTT - 2 + tt)))
                    mmv[(wi, tt)] = pr.add("tensor", lambda t, psd=psd, xo=xo: [
                        t.matmul(psd, xts[:, xo + dc * 128:xo + (dc + 1) * 128],
                                 wbuf[:, dc * 512:(dc + 1) * 512],
                                 start=(dc == 0), stop=(dc == 7), skip_group_check=True)
                        for dc in range(8)],
                        deps=[wld[wi], lx] + war)
                    mm_all.append(mmv[(wi, tt)])
                    if wi < 2:
                        wrep = wqrep if wi == 0 else wkrep
                        rcol, scol = (16, 0) if wi == 0 else (24, 8)
                        sq2 = pr.add("scalar", lambda s, psd=psd: s.activation(
                            sqk[:], psd, AF.Square),
                            deps=[mmv[(wi, tt)], last_red])
                        red = pr.add("vector", lambda v, scol=scol: v.tensor_reduce(
                            rms[:, scol:scol + 8],
                            sqk[:].rearrange("p (h d) -> p h d", d=HD),
                            axis=AXX, op=ALU.add), deps=[sq2])
                        last_red = red
                        def rst_fn(s, rcol=rcol, scol=scol, wi=wi):
                            pr.fence("scalar", s, s.activation(
                                rms[:, rcol:rcol + 8], rms[:, scol:scol + 8],
                                AF.Ln, scale=1.0 / HD, bias=cvals[:, 1:2]))
                            last = s.activation(rms[:, rcol:rcol + 8],
                                                rms[:, rcol:rcol + 8],
                                                AF.Exp, scale=-0.5)
                            if wi == 0:
                                pr.fence("scalar", s, last)
                                last = s.mul(rms[:, rcol:rcol + 8],
                                             rms[:, rcol:rcol + 8], 0.125)
                            return last
                        rst = pr.add("scalar", rst_fn, deps=[red, CONST])
                        tok = qtok if wi == 0 else ktok
                        cs = cosr[:, tt * 32:(tt + 1) * 32]
                        sn = sinr[:, tt * 32:(tt + 1) * 32]

                        def rope_fn(v, psd=psd, wrep=wrep, rcol=rcol, tok=tok,
                                    cs=cs, sn=sn):
                            rr = rms[:, rcol:rcol + 8]
                            qh = qks[:, 0:512]
                            ii = []
                            ii.append(v.tensor_tensor(
                                qks[:, 512:1024].rearrange("p (h d) -> p h d", d=64),
                                psd.rearrange("p (h d) -> p h d", d=64),
                                rr.rearrange("p (h o) -> p h o", o=1)
                                  .to_broadcast([128, 8, 64]), ALU.mult))
                            ii.append(v.tensor_tensor(qh, qks[:, 512:1024], wrep[:],
                                                      ALU.mult))
                            q3 = qh.rearrange("p (h t i) -> p h t i", t=2, i=32)
                            o3 = tok.rearrange("p (h t i) -> p h t i", t=2, i=32)
                            cs3 = cs.rearrange("p (o i) -> p o i", o=1).to_broadcast([128, 8, 32])
                            sn3 = sn.rearrange("p (o i) -> p o i", o=1).to_broadcast([128, 8, 32])
                            a = ropa[:, 0:256].rearrange("p (h i) -> p h i", i=32)
                            b = ropa[:, 256:512].rearrange("p (h i) -> p h i", i=32)
                            ii.append(v.tensor_tensor(a, q3[:, :, 0, :], cs3, ALU.mult))
                            ii.append(v.tensor_tensor(b, q3[:, :, 1, :], sn3, ALU.mult))
                            ii.append(v.tensor_tensor(o3[:, :, 0, :], a, b, ALU.subtract))
                            ii.append(v.tensor_tensor(a, q3[:, :, 1, :], cs3, ALU.mult))
                            ii.append(v.tensor_tensor(b, q3[:, :, 0, :], sn3, ALU.mult))
                            ii.append(v.tensor_tensor(o3[:, :, 1, :], a, b, ALU.add))
                            return ii
                        rp = pr.add("vector", rope_fn,
                                    deps=[rst, mmv[(wi, tt)], CONST,
                                          tqk_done.get((wi, tt - 1))])
                        psum_free[(wi, tt)] = rp
                        half2 = (tt % 2) * 1024
                        tqk_done[(wi, tt)] = pr.add("tensor", lambda t, tok=tok,
                                                    half2=half2: [
                            t.transpose(ps_tr[:, half2 + s4 * 128:half2 + (s4 + 1) * 128],
                                        tok[:, s4 * 128:(s4 + 1) * 128], idb[:])
                            for s4 in range(4)],
                            deps=[rp, evq_done.get((wi, tt - 2))])
                        dstf = qf if wi == 0 else kf
                        evq_done[(wi, tt)] = pr.add("vector", lambda v, dstf=dstf,
                                                    half2=half2, tt=tt: [
                            v.tensor_copy(dstf[:, rt * T + tt * 128:rt * T + (tt + 1) * 128],
                                          ps_tr[:, half2 + rt * 128:half2 + (rt + 1) * 128])
                            for rt in range(4)],
                            deps=[tqk_done[(wi, tt)]])
                    else:
                        psum_free[(wi, tt)] = pr.add(
                            "vector", lambda v, tt=tt, psd=psd: v.tensor_copy(
                                vg4[:, tt, :, 0:64],
                                psd.rearrange("p (h d) -> p h d", d=64)),
                            deps=[mmv[(wi, tt)], ones_set])

            qkv_done = pr.barrier_deps()

            last_mB = None
            psms_slot = [None] * 4
            last_stats = None
            for hg in range(8):
                rt, sub = hg // 2, hg % 2
                p0 = sub * 64
                stg_k = pr.add("sync", lambda sy, rt=rt, p0=p0: sy.dma_start(
                    kst[0:64, :], kf[p0:p0 + 64, rt * T:(rt + 1) * T]),
                    ndma=1, stream=s_kq,
                    deps=qkv_done + [ones_set, last_mB])
                stg_q = pr.add("sync", lambda sy, rt=rt, p0=p0: sy.dma_start(
                    qst[0:64, :], qf[p0:p0 + 64, rt * T:(rt + 1) * T]),
                    ndma=1, stream=s_kq,
                    deps=qkv_done + [last_mB])
                rmx = [None] * NTT
                for qt in range(NTT):
                    qtend = (qt + 1) * 128
                    w0 = max(0, qtend - 512)
                    wid = qtend - w0
                    sl = 2 + (qt % 2)
                    mA = pr.add("tensor", lambda t, qt=qt, w0=w0, wid=wid, sl=sl:
                        t.matmul(
                            ps_big[:, sl * 512:sl * 512 + wid],
                            qst[0:64, qt * 128:(qt + 1) * 128],
                            kst[0:64, w0:w0 + wid], start=True, stop=True,
                            skip_group_check=True),
                        deps=[stg_k, stg_q, rmx[qt - 2] if qt >= 2 else None,
                              eBslot[sl]])
                    rmx[qt] = pr.add("vector", lambda v, qt=qt, wid=wid, sl=sl:
                        v.reduce_max(
                            astat[:, qt:qt + 1], ps_big[:, sl * 512:sl * 512 + wid],
                            axis=AXX, negate=True),
                        deps=[mA, last_stats])
                eBslot[2] = rmx[NTT - 2]
                eBslot[3] = rmx[NTT - 1]
                tst = pr.add("tensor", lambda t: t.transpose(
                    ps_ms[0:NTT, 0:128] if NTT < 128 else ps_ms[:, 0:128],
                    astat[:, 0:NTT], idf[:]),
                    deps=[CONST, psms_slot[0]] + rmx)
                est = pr.add("vector", lambda v: v.tensor_copy(
                    astT[0:NTT, :], ps_ms[0:NTT, 0:128]), deps=[tst])
                psms_slot[0] = est
                gss = pr.add("gpsimd", lambda g_: g_.dma_start(
                    negm_h[0:NTT, :], astT[0:NTT, :]), ndma=1, stream=s_negm,
                    deps=[est])
                gmr = pr.add("gpsimd", lambda g_: g_.dma_start(
                    qst[64:65, 0:T].rearrange("p (a b) -> p a b", b=128),
                    negm_h[0:NTT, :].rearrange("(o a) b -> o a b", o=1)),
                    ndma=1, stream=s_negm, deps=[gss, stg_q])
                last_stats = gmr

                ktc = 0
                for qc in range(NQC):
                    nkt = (qc + 1) * 4
                    pend = []

                    def emit_av(ki, sl, eB_t, wt, nkt=nkt, hg=hg, qc=qc):
                        av = pr.add("tensor", lambda t, ki=ki, sl=sl: t.matmul(
                            ps_av[0:65, :], vg4[:, ki, hg, :],
                            pt[:, sl * 512:(sl + 1) * 512],
                            start=(ki == 0), stop=(ki == nkt - 1),
                            skip_group_check=True),
                            deps=[eB_t, wt] +
                                 ([last_evA] if ki == 0 else []))
                        AVslot[sl] = av
                        return av

                    for ki in range(nkt):
                        sl = ktc % 4
                        ktc += 1
                        mB = pr.add("tensor", lambda t, ki=ki, sl=sl, qc=qc:
                            t.matmul(
                                ps_big[:, sl * 512:(sl + 1) * 512],
                                kst[0:65, ki * 128:(ki + 1) * 128],
                                qst[0:65, qc * 512:(qc + 1) * 512],
                                start=True, stop=True, skip_group_check=True),
                            deps=[gmr, eBslot[sl]])
                        last_mB = mB
                        eB_t = pr.add("scalar", lambda s, sl=sl: s.activation(
                            pt[:, sl * 512:(sl + 1) * 512],
                            ps_big[:, sl * 512:(sl + 1) * 512], AF.Exp),
                            deps=[mB, AVslot[sl]])
                        eBslot[sl] = eB_t
                        o = ki * 128 - qc * 512
                        wt = None
                        if 0 <= o <= 384:
                            wt = pr.add("vector", lambda v, sl=sl, o=o:
                                v.tensor_tensor(
                                    pt[:, sl * 512:(sl + 1) * 512],
                                    pt[:, sl * 512:(sl + 1) * 512],
                                    wedge[:, 384 - o:896 - o], ALU.mult),
                                deps=[eB_t])
                        pend.append((ki, sl, eB_t, wt))
                        if len(pend) >= 2:
                            emit_av(*pend.pop(0))
                    while pend:
                        emit_av(*pend.pop(0))
                    evA4 = None
                    for s4 in range(4):
                        evA4 = pr.add("vector", lambda v, s4=s4: v.tensor_copy(
                            av65[0:65, s4 * 128:(s4 + 1) * 128],
                            ps_av[0:65, s4 * 128:(s4 + 1) * 128]),
                            deps=[AVslot[(ktc - 1) % 4]] if s4 == 0 else [])
                        tA2 = pr.add("tensor", lambda t, s4=s4: t.transpose(
                            ps_ms[:, s4 * 128:s4 * 128 + 65],
                            av65[0:65, s4 * 128:(s4 + 1) * 128],
                            idf[0:65, 0:65]), deps=[evA4, psms_slot[s4]])
                        tt_abs = qc * 4 + s4
                        evA2 = pr.add("vector", lambda v, tt_abs=tt_abs, hg=hg,
                                      s4=s4: [
                            v.tensor_copy(
                                avtok[:, tt_abs * 512 + hg * 64:tt_abs * 512 + (hg + 1) * 64],
                                ps_ms[:, s4 * 128:s4 * 128 + 64]),
                            v.tensor_copy(ztok[:, tt_abs * 8 + hg:tt_abs * 8 + hg + 1],
                                          ps_ms[:, s4 * 128 + 64:s4 * 128 + 65])],
                            deps=[tA2])
                        psms_slot[s4] = evA2

            grp_done = pr.barrier_deps()
            def rzall_fn(v):
                pr.fence("vector", v,
                         v.reciprocal(rzs[:, 0:NTT * 8], ztok[:, 0:NTT * 8]))
                return v.nop()
            rzall = pr.add("vector", rzall_fn, deps=grp_done)
            t3s = [None] * NTT
            ev3s = [None] * NTT
            gavs = [None] * NTT
            for tt in range(NTT):
                ao = (tt % 2) * 512
                rz = pr.add("vector", lambda v, tt=tt, ao=ao: v.tensor_tensor(
                    avn[:, ao:ao + 512].rearrange("p (h d) -> p h d", d=64),
                    avtok[:, tt * 512:(tt + 1) * 512]
                      .rearrange("p (h d) -> p h d", d=64),
                    rzs[:, tt * 8:(tt + 1) * 8]
                       .rearrange("p (h o) -> p h o", o=1)
                       .to_broadcast([128, 8, 64]),
                    ALU.mult),
                    deps=[rzall, t3s[tt - 2] if tt >= 2 else None])
                t3s[tt] = pr.add("tensor", lambda t, ao=ao: [
                    t.transpose(ps_tr[:, ao + s4 * 128:ao + (s4 + 1) * 128],
                                avn[:, ao + s4 * 128:ao + (s4 + 1) * 128], idb[:])
                    for s4 in range(4)],
                    deps=[rz, CONST, ev3s[tt - 2] if tt >= 2 else None])
                ev3s[tt] = pr.add("vector", lambda v, ao=ao: v.tensor_copy(
                    avstg[:, ao:ao + 512], ps_tr[:, ao:ao + 512]),
                    deps=[t3s[tt], gavs[tt - 2] if tt >= 2 else None])
                gavs[tt] = pr.add("gpsimd", lambda g_, tt=tt, g=g, ao=ao: [
                    g_.dma_start(avnf_h[g * 4 + b4, tt, :, :],
                                 avstg[:, ao + b4 * 128:ao + (b4 + 1) * 128])
                    for b4 in range(4)], ndma=4, stream=s_gav, deps=[ev3s[tt]])
                gav_done.append(gavs[tt])

        p4mo = [None] * (2 * NTT)
        p4ro = [None] * (2 * NTT)
        p4go = [None] * (2 * NTT)
        for h2 in range(2):
            lwo = pr.add("gpsimd", lambda gp, h2=h2: [
                gp.dma_start(wbuf[:, dc * 512:(dc + 1) * 512],
                             wo_in[dc * 128:(dc + 1) * 128, h2 * 512:(h2 + 1) * 512])
                for dc in range(8)], ndma=8, stream=s_w,
                deps=[mm_all[-1], gav_done[-1], gav_done[-2]] +
                     ([p4mo[NTT - 1]] if h2 == 1 else []))
            for tt in range(NTT):
                it = h2 * NTT + tt
                buf = it % 2
                lav = pr.add("sync", lambda sy, tt=tt, buf=buf: [
                    sy.dma_start(avf[:, buf * D + dc * 128:buf * D + (dc + 1) * 128],
                                 avnf_h[dc, tt, :, :]) for dc in range(8)], ndma=8,
                    stream=s_lav,
                    deps=[gav_done[-1], gav_done[-2],
                          p4mo[it - 2] if it >= 2 else None])
                mo = pr.add("tensor", lambda t, buf=buf: [
                    t.matmul(ps_big[:, buf * 512:buf * 512 + 512],
                             avf[:, buf * D + dc * 128:buf * D + (dc + 1) * 128],
                             wbuf[:, dc * 512:(dc + 1) * 512],
                             start=(dc == 0), stop=(dc == 7), skip_group_check=True)
                    for dc in range(8)],
                    deps=[lwo, lav, eBslot[buf],
                          p4ro[it - 2] if it >= 2 else None])
                p4mo[it] = mo
                lxo = pr.add("sync", lambda sy, tt=tt, buf=buf, h2=h2: sy.dma_start(
                    xbuf[:, buf * 512:(buf + 1) * 512],
                    x_in[tt * 128:(tt + 1) * 128, h2 * 512:(h2 + 1) * 512]), ndma=1,
                    stream=s_lxo,
                    deps=[ln_done[-1], vr_done[-1],
                          p4ro[it - 2] if it >= 2 else None])
                ro = pr.add("vector", lambda v, buf=buf: v.tensor_tensor(
                    obuf[:, buf * 512:buf * 512 + 512],
                    xbuf[:, buf * 512:(buf + 1) * 512],
                    ps_big[:, buf * 512:buf * 512 + 512], ALU.add),
                    deps=[mo, lxo, p4go[it - 2] if it >= 2 else None])
                p4ro[it] = ro
                go = pr.add("sync", lambda sy, tt=tt, h2=h2, buf=buf: sy.dma_start(
                    xout[tt * 128:(tt + 1) * 128, h2 * 512:(h2 + 1) * 512],
                    obuf[:, buf * 512:buf * 512 + 512]), ndma=1, stream=s_go,
                    deps=[ro])
                p4go[it] = go

        if fu:
            return _fuse_tail(nc, pr, blk, fu, pfx)
        pr.add("gpsimd", lambda g_: g_.nop(), deps=pr.barrier_deps())
        pr.emit(blk)

    return nc


def _fuse_tail(nc, pr, blk, fu, pfx, last=False):
    ow = [fu["bar_prev"]] if fu.get("bar_prev") else []
    if last:
        pr.add("gpsimd", lambda g_: g_.nop(), deps=pr.barrier_deps())
        pr.emit(blk, open_waits=ow)
        return None
    bar = fu["sem_ctx"].enter_context(nc.semaphore(pfx + "bar"))

    def jfn(g_):
        g_.nop().then_inc(bar, 1)
        return g_.nop()
    pr.add("gpsimd", jfn, deps=pr.barrier_deps())
    pr.emit(blk, open_waits=ow)
    return (bar, 1)


def host_attn_consts(T=4096):
    inv = 1.0 / (10000.0 ** (np.arange(0, HD, 2, dtype=np.float32) / HD))
    t = np.arange(T, dtype=np.float32)
    fr = np.outer(t, inv)
    cos, sin = np.cos(fr).astype(np.float32), np.sin(fr).astype(np.float32)
    idf = np.eye(128, dtype=np.float32)
    idb = np.eye(128, dtype=np.float32).astype(BF)
    wedge = np.zeros((128, 896), dtype=np.float32)
    k = np.arange(128)[:, None]
    u = np.arange(896)[None, :]
    wedge[(u - 384) >= k] = 1.0
    return cos, sin, idf, idb, wedge.astype(BF)


def attn_in_maps(x_full, Wq, Wk, Wv, Wo, qnw, knw, T=4096, pfx="", with_x=True):
    cos, sin, idf, idb, wedge = host_attn_consts(T)
    base = {
        "wq": np.ascontiguousarray(Wq), "wk": np.ascontiguousarray(Wk),
        "wv": np.ascontiguousarray(Wv), "wo": np.ascontiguousarray(Wo),
        "qnw": np.tile(np.asarray(qnw, np.float32), 8)[None, :],
        "knw": np.tile(np.asarray(knw, np.float32), 8)[None, :],
        "cosr": cos.astype(BF), "sinr": sin.astype(BF), "idf": idf, "idb": idb, "wedge": wedge,
        "cval": np.array([[LN_EPS, RMS_EPS, 1.0, 0, 0, 0, 0, 0]], dtype=np.float32),
    }
    base = {pfx + k: v for k, v in base.items()}
    in_maps = []
    for b in range(x_full.shape[0]):
        m = dict(base)
        if with_x:
            m[pfx + "x"] = np.ascontiguousarray(x_full[b])
        in_maps.append(m)
    return in_maps


def run_l0_attn(x_full, Wq, Wk, Wv, Wo, qnw, knw, T=4096, debug=False, raw=False):
    nc = build_l0_attn(T, debug=debug)
    in_maps = attn_in_maps(x_full, Wq, Wk, Wv, Wo, qnw, knw, T)
    res = run_bass_kernel_spmd(nc, in_maps, core_ids=list(range(len(in_maps))))
    if raw:
        return res
    return np.stack([np.asarray(r["xout"]) for r in res.results])



FI = 2816


def emit_quant(pr, src_ap, dst_ap, W, scr, deps, name=""):
    G = W // 16
    m_ = scr["m"][:, 0:G]
    s_ = scr["m"][:, G:2 * G]
    r_ = scr["m"][:, 2 * G:3 * G]
    s2_ = scr["m"][:, 3 * G:4 * G]
    ax = scr["ax"][:, 0:W]
    acc = scr["acc"][:, 0:W]
    ac2 = scr["ac2"][:, 0:W]
    sg = scr["sg"][:, 0:W]

    t_abs = pr.add("scalar", lambda s: s.activation(ax, src_ap, AF.Abs), deps=deps)
    t_sgn = pr.add("scalar", lambda s: s.activation(sg, src_ap, AF.Sign), deps=deps)

    def t_m_fn(v):
        pr.fence("vector", v, v.reduce_max(
            m_, src_ap.rearrange("p (g k) -> p g k", k=16), axis=AXX,
            apply_absolute_value=True))
        pr.fence("vector", v, v.tensor_scalar(s_, m_, 1.0 / 6.0, 1e-12,
                                              op0=ALU.mult, op1=ALU.max))
        pr.fence("vector", v, v.reciprocal(r_, s_))
        return v.tensor_copy(s2_, s_)
    t_m = pr.add("vector", t_m_fn, deps=deps)

    def t_q_fn(v):
        ii = []
        r3 = r_.rearrange("p (g o) -> p g o", o=1).to_broadcast([128, G, 16])
        ax3 = ax.rearrange("p (g k) -> p g k", k=16)
        ii.append(v.tensor_tensor(ax3, ax3, r3, ALU.mult))
        u = ax
        ii.append(v.tensor_scalar(acc, u, 0.25, None, op0=ALU.is_gt))
        ii.append(v.scalar_tensor_tensor(acc, u, 0.75, acc, op0=ALU.is_gt, op1=ALU.add))
        ii.append(v.scalar_tensor_tensor(acc, u, 1.25, acc, op0=ALU.is_gt, op1=ALU.add))
        ii.append(v.scalar_tensor_tensor(acc, u, 1.75, acc, op0=ALU.is_gt, op1=ALU.add))
        ii.append(v.tensor_scalar(ac2, u, 2.5, None, op0=ALU.is_gt))
        ii.append(v.scalar_tensor_tensor(ac2, u, 3.5, ac2, op0=ALU.is_gt, op1=ALU.add))
        ii.append(v.scalar_tensor_tensor(acc, ac2, 2.0, acc, op0=ALU.mult, op1=ALU.add))
        ii.append(v.tensor_scalar(ac2, u, 5.0, 4.0, op0=ALU.is_gt, op1=ALU.mult))
        ii.append(v.tensor_tensor(acc, acc, ac2, ALU.add))
        ii.append(v.tensor_scalar(acc, acc, 0.5, None, op0=ALU.mult))
        ii.append(v.tensor_tensor(acc, acc, sg, ALU.mult))
        s23 = s2_.rearrange("p (g o) -> p g o", o=1).to_broadcast([128, G, 16])
        acc3 = acc.rearrange("p (g k) -> p g k", k=16)
        ii.append(v.tensor_tensor(dst_ap.rearrange("p (g k) -> p g k", k=16),
                                  acc3, s23, ALU.mult))
        return ii
    return pr.add("vector", t_q_fn, deps=[t_abs, t_sgn, t_m])


def build_l0_ffn(T=4096, debug=False, fu=None):
    NTT = T // 128
    FCH = [(i * 512, min(512, FI - i * 512)) for i in range((FI + 511) // 512)]
    pfx = fu["pfx"] if fu else ""
    nc = fu["nc"] if fu else bass.Bass("TRN2", target_bir_lowering=False)
    DT = lambda name, *a, **k: nc.dram_tensor(pfx + name, *a, **k)

    if fu:
        x_in, xout = fu["x_in"], fu["x_out"]
    else:
        x_in = DT("x", [T, D], F32, kind="ExternalInput").ap()
        xout = DT("xout", [T, D], F32, kind="ExternalOutput").ap()
    wg_in = DT("wg", [FI, D], F32, kind="ExternalInput").ap()
    wu_in = DT("wu", [FI, D], F32, kind="ExternalInput").ap()
    wd_in = DT("wd", [D, FI], F32, kind="ExternalInput").ap()
    idb_in = DT("idb", [128, 128], BF16, kind="ExternalInput").ap()
    cval_in = DT("cval", [1, 8], F32, kind="ExternalInput").ap()
    dbg = dict(kind="ExternalOutput") if debug else {}
    xln_h = DT("xln_h", [T, D], F32, **dbg).ap()
    xq_h = DT("xq_h", [T, D], BF16, **dbg).ap() if debug else None
    wgu_h = DT("wgu_h", [128, 8 * 2 * FI], BF16, **dbg).ap() if debug else None

    with ExitStack() as ctx:
        S = lambda name, shape, dt: ctx.enter_context(nc.sbuf_tensor(pfx + name, shape, dt))
        idb = S("idb_s", [128, 128], BF16)
        cvals = S("cvals", [128, 8], F32)
        wguT = S("wguT", [128, 8 * 2 * FI], BF16)
        wdT = S("wdT", [128, 22 * D], BF16)
        wraw = S("wraw", [128, 1408], F32)
        wqrow = S("wqrow", [128, 1408], BF16)
        xbuf = S("xbuf", [128, D], F32)
        xln = S("xln", [128, D], F32)
        xln2 = S("xln2", [128, D], F32)
        sq1 = S("sq1", [128, D], F32)
        stat = S("stat", [128, 16], F32)
        xq = S("xq", [128, D], BF16)
        xqT = S("xqT", [128, D], BF16)
        hbuf = S("hbuf", [128, 512], F32)
        hq = S("hq", [128, 512], BF16)
        hqT = S("hqT", [128, 512], BF16)
        obuf = S("obuf", [128, D], F32)
        scr = {
            "ax": S("q_ax", [128, 1408], F32),
            "acc": S("q_acc", [128, 1408], F32),
            "ac2": S("q_ac2", [128, 1408], F32),
            "sg": S("q_sg", [128, 1408], F32),
            "m": S("q_m", [128, 4 * 88], F32),
        }

        P_ = lambda name, shape, dt: ctx.enter_context(nc.psum_tensor(pfx + name, shape, dt))
        ps_tr = P_("ps_tr", [128, 2048], BF16)
        ps_gu = P_("ps_gu", [128, 1024], F32)
        ps_dn = P_("ps_dn", [128, 1024], F32)

        pr = Prog(nc, fu["sem_ctx"] if fu else ctx, pfx)
        s_const = pr.stream("s_const", 1)
        s_wr = pr.stream("s_wr", 2)
        s_lx = pr.stream("s_lx", 2)
        s_sx = pr.stream("s_sx", 2)
        s_go = pr.stream("s_go", 2)
        blk = fu["blk"] if fu else ctx.enter_context(nc.Block())

        CONST = pr.add("sync", lambda sy: [
            sy.dma_start(idb[:], idb_in[:]),
            sy.dma_start(cvals[:], cval_in.to_broadcast([128, 8]))],
            ndma=2, stream=s_const)

        prev_q = None
        prev_ev = [None, None]
        pi = 0

        def wchunk(w_in, r0, nr, w_cols0, w_ncols, dests):
            nonlocal prev_q, prev_ev, pi
            half = (pi % 2) * 1024
            ldw = pr.add("sync", lambda sy: sy.dma_start(
                wraw[:, 0:w_ncols], w_in[r0:r0 + nr, w_cols0:w_cols0 + w_ncols]),
                ndma=1, stream=s_wr, deps=[prev_q])
            tq = emit_quant(pr, wraw[:, 0:w_ncols], wqrow[:, 0:w_ncols], w_ncols,
                            scr, deps=[ldw, prev_ev[0], prev_ev[1]])
            prev_q = tq
            nb = w_ncols // 128
            done = []
            for fb in range(0, nb, 8):
                nfs = min(8, nb - fb)
                ttr = pr.add("tensor", lambda t, half=half, fb=fb, nfs=nfs: [
                    t.transpose(ps_tr[:, half + i * 128:half + (i + 1) * 128],
                                wqrow[:, (fb + i) * 128:(fb + i + 1) * 128], idb[:])
                    for i in range(nfs)],
                    deps=[tq, CONST, done[-1] if done else prev_ev[pi % 2]])
                ev = pr.add("vector", lambda v, half=half, fb=fb, nfs=nfs,
                            dests=dests: [
                    v.tensor_copy(dests[fb + i][0][:, dests[fb + i][1]:
                                                   dests[fb + i][1] + 128],
                                  ps_tr[:, half + i * 128:half + (i + 1) * 128])
                    for i in range(nfs)], deps=[ttr])
                done.append(ev)
            prev_ev[pi % 2] = done[-1]
            pi += 1

        for which, w_in in [(0, wg_in), (1, wu_in)]:
            for rt in range(FI // 128):
                r0 = rt * 128
                fc, off = r0 // 512, r0 % 512
                fcw = min(512, FI - fc * 512)
                dests = [(wguT, dc * 2 * FI + fc * 1024 + which * fcw + off)
                         for dc in range(8)]
                wchunk(w_in, r0, 128, 0, D, dests)
        for rt in range(8):
            for ch in range(2):
                dests = [(wdT, (ch * 11 + i) * D + rt * 128) for i in range(11)]
                wchunk(wd_in, rt * 128, 128, ch * 1408, 1408, dests)
        if debug:
            pr.add("gpsimd", lambda g_: g_.dma_start(wgu_h[:, :], wguT[:, :]),
                   ndma=1, stream=s_sx, deps=pr.barrier_deps())
        wdone = pr.barrier_deps()

        p1_ln = [None] * NTT
        p1_vr = [None] * NTT
        p1_sx = [None] * NTT
        for tt in range(NTT):
            ld = pr.add("sync", lambda sy, tt=tt: sy.dma_start(
                xbuf[:], x_in[tt * 128:(tt + 1) * 128, :]), ndma=1, stream=s_lx,
                deps=[p1_ln[tt - 1] if tt >= 1 else None] + (wdone if tt == 0 else []))
            sq_t = pr.add("scalar", lambda s: s.activation(
                sq1[:], xbuf[:], AF.Square),
                deps=[ld, p1_vr[tt - 1] if tt >= 1 else None])

            def vr_fn(v):
                pr.fence("vector", v, v.reduce_sum(stat[:, 0:1], xbuf[:], axis=AXX))
                pr.fence("vector", v, v.reduce_sum(stat[:, 1:2], sq1[:],
                                                   axis=AXX))
                pr.fence("vector", v, v.tensor_scalar(stat[:, 2:3], stat[:, 0:1],
                                                      1.0 / D, None, op0=ALU.mult))
                pr.fence("vector", v, v.tensor_tensor(stat[:, 3:4], stat[:, 2:3],
                                                      stat[:, 2:3], ALU.mult))
                return v.scalar_tensor_tensor(stat[:, 4:5], stat[:, 1:2], 1.0 / D,
                                              stat[:, 3:4], op0=ALU.mult,
                                              op1=ALU.subtract)
            vr = pr.add("vector", vr_fn, deps=[ld, sq_t])
            p1_vr[tt] = vr

            def lnp_fn(s):
                pr.fence("scalar", s, s.activation(stat[:, 5:6], stat[:, 4:5],
                                                   AF.Ln, bias=cvals[:, 0:1]))
                pr.fence("scalar", s, s.activation(stat[:, 6:7], stat[:, 5:6],
                                                   AF.Exp, scale=-0.5))
                pr.fence("scalar", s, s.activation(stat[:, 7:8], stat[:, 2:3],
                                                   AF.Copy, scale=stat[:, 6:7]))
                return s.mul(stat[:, 7:8], stat[:, 7:8], -1.0)
            lnp = pr.add("scalar", lnp_fn, deps=[vr, CONST])
            ln = pr.add("vector", lambda v: v.scalar_tensor_tensor(
                xln[:], xbuf[:], stat[:, 6:7], stat[:, 7:8].to_broadcast([128, D]),
                op0=ALU.mult, op1=ALU.add),
                deps=[lnp, p1_sx[tt - 1] if tt >= 1 else None])
            p1_ln[tt] = ln
            p1_sx[tt] = pr.add("gpsimd", lambda g_, tt=tt: g_.dma_start(
                xln_h[tt * 128:(tt + 1) * 128, :], xln[:]), ndma=1, stream=s_sx,
                deps=[ln])

        prev = {"q": None, "mm": None, "sil": None, "hq": None, "tph": None,
                "evh": None, "dn": None, "ro": None, "go": None, "tpx": None,
                "evx": None, "lnld": None}
        for tt in range(NTT):
            lnld = pr.add("sync", lambda sy, tt=tt: sy.dma_start(
                xln2[:], xln_h[tt * 128:(tt + 1) * 128, :]), ndma=1, stream=s_lx,
                deps=[p1_sx[tt], prev["q"]])
            prev["lnld"] = lnld
            tq = emit_quant(pr, xln2[:], xq[:], D, scr,
                            deps=[lnld, prev["tpx"]] +
                                 ([p1_vr[-1], p1_ln[-1]] if tt == 0 else []))
            prev["q"] = tq
            tpx = pr.add("tensor", lambda t: [
                t.transpose(ps_tr[:, dc * 128:(dc + 1) * 128],
                            xq[:, dc * 128:(dc + 1) * 128], idb[:])
                for dc in range(8)], deps=[tq, CONST, prev["evx"]])
            prev["tpx"] = tpx
            evx = pr.add("vector", lambda v: v.tensor_copy(xqT[:], ps_tr[:, 0:1024]),
                         deps=[tpx, prev["mm"]])
            prev["evx"] = evx
            if debug:
                pr.add("gpsimd", lambda g_, tt=tt: g_.dma_start(
                    xq_h[tt * 128:(tt + 1) * 128, :], xq[:]), ndma=1, stream=s_sx,
                    deps=[tq, prev["q"]])

            for fci, (f0, fn_) in enumerate(FCH):
                mm = pr.add("tensor", lambda t, f0=f0, fn_=fn_: [
                    t.matmul(ps_gu[:, w * 512:w * 512 + fn_],
                             xqT[:, dc * 128:(dc + 1) * 128],
                             wguT[:, dc * 2 * FI + (f0 // 512) * 1024 + w * fn_:
                                  dc * 2 * FI + (f0 // 512) * 1024 + (w + 1) * fn_],
                             start=(dc == 0), stop=(dc == 7), skip_group_check=True)
                    for w in range(2) for dc in range(8)],
                    deps=[evx, prev["sil"], prev["hq"]])
                prev["mm"] = mm
                sil = pr.add("scalar", lambda s, fn_=fn_: s.activation(
                    hbuf[:, 0:fn_], ps_gu[:, 0:fn_], AF.Silu), deps=[mm, prev["hq"]])
                prev["sil"] = sil
                hm = pr.add("vector", lambda v, fn_=fn_: v.tensor_tensor(
                    hbuf[:, 0:fn_], hbuf[:, 0:fn_], ps_gu[:, 512:512 + fn_],
                    ALU.mult), deps=[sil, mm])
                thq = emit_quant(pr, hbuf[:, 0:fn_], hq[:, 0:fn_], fn_, scr,
                                 deps=[hm, prev["tph"]])
                prev["hq"] = thq
                nfs = fn_ // 128
                tph = pr.add("tensor", lambda t, nfs=nfs: [
                    t.transpose(ps_tr[:, 1024 + i * 128:1024 + (i + 1) * 128],
                                hq[:, i * 128:(i + 1) * 128], idb[:])
                    for i in range(nfs)], deps=[thq, prev["evh"]])
                prev["tph"] = tph
                evh = pr.add("vector", lambda v, nfs=nfs: v.tensor_copy(
                    hqT[:, 0:nfs * 128], ps_tr[:, 1024:1024 + nfs * 128]),
                    deps=[tph, prev["dn"]])
                prev["evh"] = evh
                dn = pr.add("tensor", lambda t, f0=f0, nfs=nfs, fci=fci: [
                    t.matmul(ps_dn[:, h2 * 512:(h2 + 1) * 512],
                             hqT[:, i * 128:(i + 1) * 128],
                             wdT[:, (f0 // 128 + i) * D + h2 * 512:
                                  (f0 // 128 + i) * D + (h2 + 1) * 512],
                             start=(fci == 0 and i == 0),
                             stop=(fci == len(FCH) - 1 and i == nfs - 1
                                   and h2 == 1),
                             skip_group_check=True)
                    for i in range(nfs) for h2 in range(2)],
                    deps=[evh, prev["ro"]])
                prev["dn"] = dn
            lxr = pr.add("sync", lambda sy, tt=tt: sy.dma_start(
                xbuf[:], x_in[tt * 128:(tt + 1) * 128, :]), ndma=1, stream=s_go,
                deps=[prev["dn"]] + ([p1_ln[-1], p1_vr[-1],
                                      pr.tasks[pr.tasks.index(prev["dn"])]]
                                     if tt == 0 else []))
            ro = pr.add("vector", lambda v: v.tensor_tensor(
                obuf[:], xbuf[:], ps_dn[:, 0:1024], ALU.add),
                deps=[prev["dn"], lxr, prev["go"]])
            prev["ro"] = ro
            go = pr.add("sync", lambda sy, tt=tt: sy.dma_start(
                xout[tt * 128:(tt + 1) * 128, :], obuf[:]), ndma=1, stream=s_go,
                deps=[ro])
            prev["go"] = go

        if fu:
            return _fuse_tail(nc, pr, blk, fu, pfx)
        pr.add("gpsimd", lambda g_: g_.nop(), deps=pr.barrier_deps())
        pr.emit(blk)
    return nc


def ffn_in_maps(x_full, wg, wu, wd, pfx="", with_x=True):
    base = {
        "wg": np.ascontiguousarray(wg), "wu": np.ascontiguousarray(wu),
        "wd": np.ascontiguousarray(wd),
        "idb": np.eye(128, dtype=np.float32).astype(BF),
        "cval": np.array([[LN_EPS, RMS_EPS, 1.0, 0, 0, 0, 0, 0]], dtype=np.float32),
    }
    base = {pfx + k: v for k, v in base.items()}
    in_maps = []
    for b in range(x_full.shape[0]):
        m = dict(base)
        if with_x:
            m[pfx + "x"] = np.ascontiguousarray(x_full[b])
        in_maps.append(m)
    return in_maps


def run_l0_ffn(x_full, wg, wu, wd, T=4096, raw=False, debug=False):
    nc = build_l0_ffn(T, debug=debug)
    in_maps = ffn_in_maps(x_full, wg, wu, wd)
    res = run_bass_kernel_spmd(nc, in_maps, core_ids=list(range(len(in_maps))))
    if raw:
        return res
    return np.stack([np.asarray(r["xout"]) for r in res.results])



RN, NN = 64, 16


def build_l1_attn(T=4096, debug=False, fu=None):
    NTT = T // 128
    NCH = T // 512
    Q = 512
    pfx = fu["pfx"] if fu else ""
    nc = fu["nc"] if fu else bass.Bass("TRN2", target_bir_lowering=False)
    DT = lambda name, *a, **k: nc.dram_tensor(pfx + name, *a, **k)

    if fu:
        x_in, xout = fu["x_in"], fu["x_out"]
    else:
        x_in = DT("x", [T, D], F32, kind="ExternalInput").ap()
        xout = DT("xout", [T, D], F32, kind="ExternalOutput").ap()
    xpw_in = DT("xpw", [96, D], F32, kind="ExternalInput").ap()
    dtw_in = DT("dtw", [D, RN], F32, kind="ExternalInput").ap()
    dtb_in = DT("dtb", [D], F32, kind="ExternalInput").ap()
    alog_in = DT("alog", [D, NN], F32, kind="ExternalInput").ap()
    dpar_in = DT("dpar", [D], F32, kind="ExternalInput").ap()
    lna_in = DT("lna", [1, D], F32, kind="ExternalInput").ap()
    lnb_in = DT("lnb", [1, D], F32, kind="ExternalInput").ap()
    idb_in = DT("idb", [128, 128], BF16, kind="ExternalInput").ap()
    idf_in = DT("idf", [128, 128], F32, kind="ExternalInput").ap()
    cval_in = DT("cval", [1, 8], F32, kind="ExternalInput").ap()
    dbg = dict(kind="ExternalOutput") if debug else {}
    xln_h = DT("xln_h", [T, D], F32, **dbg).ap()
    dbc_h = DT("dbc_h", [32, T], F32, **dbg).ap()
    delta_h = DT("delta_h", [128, 8 * T], F32, **dbg).ap() if debug else None
    y_h = DT("y_h", [128, 8 * T], F32, **dbg).ap() if debug else None

    with ExitStack() as ctx:
        S = lambda name, shape, dt: ctx.enter_context(nc.sbuf_tensor(pfx + name, shape, dt))
        idb = S("idb_s", [128, 128], BF16)
        idf = S("idf_s", [128, 128], F32)
        cvals = S("cvals", [128, 8], F32)
        xpwT = S("xpwT", [128, 8 * 96], BF16)
        dtwT = S("dtwT", [64, 8 * 128], BF16)
        wstgb = S("wstgb", [128, 128], BF16)
        a_sb = S("a_sb", [128, 8 * NN], F32)
        db_sb = S("db_sb", [128, 8], F32)
        dp_sb = S("dp_sb", [128, 8], F32)
        lnar = S("lnar", [128, D], F32)
        lnbr = S("lnbr", [128, D], F32)
        xbuf = S("xbuf", [128, D], F32)
        xln = S("xln", [128, D], F32)
        sq1 = S("sq1", [128, D], F32)
        stat = S("stat", [128, 16], F32)
        xfT = S("xfT", [128, 8 * 512], F32)
        xbT = S("xbT", [128, 8 * 512], BF16)
        dbcB = S("dbcB", [64, 512], BF16)
        dbcS = S("dbcS", [32, 512], F32)
        delta = S("delta", [128, 8 * 512], F32)
        spl = S("spl", [128, 512], F32)
        spr = S("spr", [128, 512], F32)
        zc = S("zc", [128, 8 * 512], F32)
        dAb = S("dAb", [128, 8 * 512], F32)
        bxb = S("bxb", [128, 8 * 512], F32)
        hb = S("hb", [128, 8 * 512], F32)
        yac = S("yac", [128, 8 * 512], F32)
        yv = S("yv", [128, 512], F32)
        bmr = S("bmr", [128, 2 * 512], F32)
        cmr = S("cmr", [128, 2 * 512], F32)
        state = S("state", [128, 8 * NN], F32)
        ytok = S("ytok", [128, D], F32)
        obuf = S("obuf", [128, D], F32)

        P_ = lambda name, shape, dt: ctx.enter_context(nc.psum_tensor(pfx + name, shape, dt))
        ps_trf = P_("ps_trf", [128, 1024], F32)
        ps_trb = P_("ps_trb", [128, 128], BF16)
        ps_dbc = P_("ps_dbc", [96, 512], F32)
        ps_dt = P_("ps_dt", [128, 512], F32)

        pr = Prog(nc, fu["sem_ctx"] if fu else ctx, pfx)
        s_const = pr.stream("s_const", 1)
        s_w = pr.stream("s_w", 2)
        s_lx = pr.stream("s_lx", 2)
        s_sx = pr.stream("s_sx", 2)
        s_dbc = pr.stream("s_dbc", 2)
        s_bc = pr.stream("s_bc", 4)
        s_go = pr.stream("s_go", 2)
        blk = fu["blk"] if fu else ctx.enter_context(nc.Block())

        CONST = pr.add("sync", lambda sy: [
            sy.dma_start(idb[:], idb_in[:]),
            sy.dma_start(idf[:], idf_in[:]),
            sy.dma_start(cvals[:], cval_in.to_broadcast([128, 8])),
            sy.dma_start(a_sb[:].rearrange("p (dc n) -> p dc n", n=NN),
                         alog_in.rearrange("(dc p) n -> p dc n", p=128)),
            sy.dma_start(lnar[:], lna_in.to_broadcast([128, D])),
            sy.dma_start(lnbr[:], lnb_in.to_broadcast([128, D]))],
            ndma=6, stream=s_const)
        CONST2 = pr.add("gpsimd", lambda g_: [
            g_.dma_start(db_sb[:], dtb_in.rearrange("(dc p) -> p dc", p=128),
                         allow_slow_non_contiguous=True),
            g_.dma_start(dp_sb[:], dpar_in.rearrange("(dc p) -> p dc", p=128),
                         allow_slow_non_contiguous=True)],
            ndma=2, stream=s_bc)
        tA1 = pr.add("scalar", lambda s: s.activation(a_sb[:], a_sb[:], AF.Exp),
                     deps=[CONST])
        tA2 = pr.add("vector", lambda v: v.tensor_scalar(
            a_sb[:], a_sb[:], -1.0, None, op0=ALU.mult), deps=[tA1])

        prev_ev = None
        for dc in range(8):
            lw = pr.add("gpsimd", lambda g_, dc=dc: g_.dma_start(
                wstgb[0:96, :], xpw_in[:, dc * 128:(dc + 1) * 128]), ndma=1,
                stream=s_w, deps=[prev_ev])
            tt_ = pr.add("tensor", lambda t: t.transpose(
                ps_trb[:, 0:96], wstgb[0:96, :], idb[0:96, 0:96]),
                deps=[lw, CONST, prev_ev])
            ev = pr.add("vector", lambda v, dc=dc: v.tensor_copy(
                xpwT[:, dc * 96:(dc + 1) * 96], ps_trb[:, 0:96]), deps=[tt_])
            prev_ev = ev
        for dc in range(8):
            lw = pr.add("gpsimd", lambda g_, dc=dc: g_.dma_start(
                wstgb[:, 0:RN], dtw_in[dc * 128:(dc + 1) * 128, :]), ndma=1,
                stream=s_w, deps=[prev_ev])
            tt_ = pr.add("tensor", lambda t: t.transpose(
                ps_trb[0:RN, 0:128], wstgb[:, 0:RN], idb[:]),
                deps=[lw, CONST, prev_ev])
            ev = pr.add("vector", lambda v, dc=dc: v.tensor_copy(
                dtwT[0:RN, dc * 128:(dc + 1) * 128], ps_trb[0:RN, 0:128]),
                deps=[tt_])
            prev_ev = ev
        wdone = pr.barrier_deps()

        p1_ln = [None] * NTT
        p1_vr = [None] * NTT
        p1_sx = [None] * NTT
        for tt in range(NTT):
            ld = pr.add("sync", lambda sy, tt=tt: sy.dma_start(
                xbuf[:], x_in[tt * 128:(tt + 1) * 128, :]), ndma=1, stream=s_lx,
                deps=[p1_ln[tt - 1] if tt >= 1 else None])
            sq_t = pr.add("scalar", lambda s: s.activation(
                sq1[:], xbuf[:], AF.Square),
                deps=[ld, p1_vr[tt - 1] if tt >= 1 else None] +
                     (wdone if tt == 0 else []))

            def vr_fn(v):
                pr.fence("vector", v, v.reduce_sum(stat[:, 0:1], xbuf[:], axis=AXX))
                pr.fence("vector", v, v.reduce_sum(stat[:, 1:2], sq1[:], axis=AXX))
                pr.fence("vector", v, v.tensor_scalar(stat[:, 2:3], stat[:, 0:1],
                                                      1.0 / D, None, op0=ALU.mult))
                pr.fence("vector", v, v.tensor_tensor(stat[:, 3:4], stat[:, 2:3],
                                                      stat[:, 2:3], ALU.mult))
                return v.scalar_tensor_tensor(stat[:, 4:5], stat[:, 1:2], 1.0 / D,
                                              stat[:, 3:4], op0=ALU.mult,
                                              op1=ALU.subtract)
            vr = pr.add("vector", vr_fn, deps=[ld, sq_t])
            p1_vr[tt] = vr

            def lnp_fn(s):
                pr.fence("scalar", s, s.activation(stat[:, 5:6], stat[:, 4:5],
                                                   AF.Ln, bias=cvals[:, 0:1]))
                pr.fence("scalar", s, s.activation(stat[:, 6:7], stat[:, 5:6],
                                                   AF.Exp, scale=-0.5))
                pr.fence("scalar", s, s.activation(stat[:, 7:8], stat[:, 2:3],
                                                   AF.Copy, scale=stat[:, 6:7]))
                return s.mul(stat[:, 7:8], stat[:, 7:8], -1.0)
            lnp = pr.add("scalar", lnp_fn, deps=[vr, CONST])
            ln = pr.add("vector", lambda v: v.scalar_tensor_tensor(
                xln[:], xbuf[:], stat[:, 6:7], stat[:, 7:8].to_broadcast([128, D]),
                op0=ALU.mult, op1=ALU.add),
                deps=[lnp, p1_sx[tt - 1] if tt >= 1 else None])
            p1_ln[tt] = ln
            p1_sx[tt] = pr.add("gpsimd", lambda g_, tt=tt: g_.dma_start(
                xln_h[tt * 128:(tt + 1) * 128, :], xln[:]), ndma=1, stream=s_sx,
                deps=[ln])

        st0 = pr.add("vector", lambda v: v.memset(state[:], 0.0), deps=[])

        pv = {k: None for k in ["tpq", "evx", "mmdbc", "evb", "evs", "gdbc",
                                "mmdt", "spv", "zt", "dA", "bx", "scn", "yn",
                                "stu", "yD", "lxq", "ro", "go", "lnv", "sqy"]}
        pv["stu"] = st0
        for c in range(NCH):
            c0 = c * 512
            evx_list = []
            for s4 in range(4):
                tt = c * 4 + s4
                lxq = pr.add("sync", lambda sy, tt=tt: sy.dma_start(
                    xln[:], xln_h[tt * 128:(tt + 1) * 128, :]), ndma=1, stream=s_lx,
                    deps=[p1_sx[tt], pv["tpq"]] +
                         ([p1_ln[-1], p1_vr[-1], p1_sx[-1], p1_sx[-2]]
                          if c == 0 and s4 == 0 else []))
                pv["lxq"] = lxq
                tpq = pr.add("tensor", lambda t: [
                    t.transpose(ps_trf[:, i * 128:(i + 1) * 128],
                                xln[:, i * 128:(i + 1) * 128], idf[:])
                    for i in range(8)],
                    deps=[lxq, CONST, pv["evx"], pv.get("evy")])
                pv["tpq"] = tpq

                def evx_fn(v, s4=s4):
                    ii = []
                    for i in range(8):
                        ii.append(v.tensor_copy(
                            xfT[:, i * 512 + s4 * 128:i * 512 + (s4 + 1) * 128],
                            ps_trf[:, i * 128:(i + 1) * 128]))
                        ii.append(v.tensor_copy(
                            xbT[:, i * 512 + s4 * 128:i * 512 + (s4 + 1) * 128],
                            ps_trf[:, i * 128:(i + 1) * 128]))
                    return ii
                evx = pr.add("vector", evx_fn,
                             deps=[tpq, pv["zt"], pv["yD"], pv["mmdbc"]])
                pv["evx"] = evx
                evx_list.append(evx)
            mmdbc = pr.add("tensor", lambda t: [
                t.matmul(ps_dbc[0:96, :], xpwT[:, dc * 96:(dc + 1) * 96],
                         xbT[:, dc * 512:(dc + 1) * 512],
                         start=(dc == 0), stop=(dc == 7), skip_group_check=True)
                for dc in range(8)],
                deps=evx_list + [pv["evb"], pv["evs"]] + ([] if c else pr.barrier_deps()[:1]))
            pv["mmdbc"] = mmdbc
            evb = pr.add("vector", lambda v: v.tensor_copy(
                dbcB[0:64, :], ps_dbc[0:64, :]), deps=[mmdbc, pv["mmdt"]])
            pv["evb"] = evb
            evs = pr.add("vector", lambda v: v.tensor_copy(
                dbcS[0:32, :], ps_dbc[64:96, :]), deps=[mmdbc, pv["gdbc"]])
            pv["evs"] = evs
            gdbc = pr.add("gpsimd", lambda g_, c0=c0: g_.dma_start(
                dbc_h[:, c0:c0 + 512], dbcS[0:32, :]), ndma=1, stream=s_dbc,
                deps=[evs])
            pv["gdbc"] = gdbc
            spv_l = None
            for dc in range(8):
                mmdt = pr.add("tensor", lambda t, dc=dc: t.matmul(
                    ps_dt[:], dtwT[0:RN, dc * 128:(dc + 1) * 128], dbcB[0:RN, :],
                    start=True, stop=True, skip_group_check=True),
                    deps=[evb, spv_l])
                pv["mmdt"] = mmdt

                def sp_fn(s, dc=dc):
                    s.activation(spl[:], ps_dt[:], AF.Abs, bias=db_sb[:, dc:dc + 1])
                    s.activation(spl[:], spl[:], AF.Exp, scale=-1.0)
                    s.activation(spl[:], spl[:], AF.Ln, bias=cvals[:, 2:3])
                    return s.activation(spr[:], ps_dt[:], AF.Relu,
                                        bias=db_sb[:, dc:dc + 1])
                sp = pr.add("scalar", sp_fn, deps=[mmdt, CONST2, spv_l])
                spv = pr.add("vector", lambda v, dc=dc: v.tensor_tensor(
                    delta[:, dc * 512:(dc + 1) * 512], spl[:], spr[:], ALU.add),
                    deps=[sp])
                spv_l = spv
            pv["spv"] = spv_l
            zt = pr.add("vector", lambda v: v.tensor_tensor(
                zc[:], delta[:], xfT[:], ALU.mult), deps=[spv_l, pv["bx"]])
            pv["zt"] = zt
            for n in range(NN):
                nb = n % 2
                bm = pr.add("gpsimd", lambda g_, n=n, c0=c0, nb=nb: [
                    g_.dma_start(bmr[:, nb * 512:(nb + 1) * 512],
                                 dbc_h[n:n + 1, c0:c0 + 512].to_broadcast(
                                     [128, 512])),
                    g_.dma_start(cmr[:, nb * 512:(nb + 1) * 512],
                                 dbc_h[16 + n:17 + n, c0:c0 + 512].to_broadcast(
                                     [128, 512]))],
                    ndma=2, stream=s_bc,
                    deps=[gdbc, pv["bx"], pv["yn"]])

                def dA_fn(s, n=n):
                    return [s.activation(dAb[:, dc * 512:(dc + 1) * 512],
                                         delta[:, dc * 512:(dc + 1) * 512],
                                         AF.Exp, scale=a_sb[:, dc * NN + n:
                                                           dc * NN + n + 1])
                            for dc in range(8)]
                dA = pr.add("scalar", dA_fn, deps=[pv["spv"], pv["scn"], tA2])
                pv["dA"] = dA

                def bx_fn(v, nb=nb):
                    b3 = bmr[:, nb * 512:(nb + 1) * 512]
                    return [v.tensor_tensor(bxb[:, dc * 512:(dc + 1) * 512],
                                            zc[:, dc * 512:(dc + 1) * 512],
                                            b3, ALU.mult)
                            for dc in range(8)]
                bx = pr.add("vector", bx_fn, deps=[zt, bm, pv["scn"]])
                pv["bx"] = bx

                def scn_fn(v, n=n):
                    for dc in range(7):
                        v.tensor_tensor_scan(
                            hb[:, dc * 512:(dc + 1) * 512],
                            dAb[:, dc * 512:(dc + 1) * 512],
                            bxb[:, dc * 512:(dc + 1) * 512],
                            state[:, dc * NN + n:dc * NN + n + 1],
                            ALU.mult, ALU.add)
                    pr.fence("vector", v, v.tensor_tensor_scan(
                        hb[:, 7 * 512:8 * 512], dAb[:, 7 * 512:8 * 512],
                        bxb[:, 7 * 512:8 * 512],
                        state[:, 7 * NN + n:7 * NN + n + 1],
                        ALU.mult, ALU.add))
                    return v.nop()
                scn = pr.add("vector", scn_fn,
                             deps=[dA, bx, pv["yn"], pv["stu"]])
                pv["scn"] = scn

                def stu_fn(v, n=n):
                    for dc in range(7):
                        v.tensor_copy(state[:, dc * NN + n:dc * NN + n + 1],
                                      hb[:, dc * 512 + 511:dc * 512 + 512])
                    pr.fence("vector", v, v.tensor_copy(
                        state[:, 7 * NN + n:7 * NN + n + 1],
                        hb[:, 7 * 512 + 511:7 * 512 + 512]))
                    return v.engine_nop() if hasattr(v, "engine_nop") else v.nop()
                stu = pr.add("vector", stu_fn, deps=[scn])
                pv["stu"] = stu

                def yn_fn(v, n=n, nb=nb):
                    c3 = cmr[:, nb * 512:(nb + 1) * 512]
                    ii = []
                    for dc in range(8):
                        sl = slice(dc * 512, (dc + 1) * 512)
                        if n == 0:
                            ii.append(v.tensor_tensor(yac[:, sl], hb[:, sl], c3,
                                                      ALU.mult))
                        else:
                            ii.append(v.tensor_tensor(yv[:], hb[:, sl], c3,
                                                      ALU.mult))
                            ii.append(v.tensor_tensor(yac[:, sl], yac[:, sl],
                                                      yv[:], ALU.add))
                    return ii
                yn = pr.add("vector", yn_fn, deps=[scn, bm])
                pv["yn"] = yn
            yD = pr.add("vector", lambda v: [
                v.scalar_tensor_tensor(yac[:, dc * 512:(dc + 1) * 512],
                                       xfT[:, dc * 512:(dc + 1) * 512],
                                       dp_sb[:, dc:dc + 1],
                                       yac[:, dc * 512:(dc + 1) * 512],
                                       op0=ALU.mult, op1=ALU.add)
                for dc in range(8)], deps=[pv["yn"], CONST2])
            pv["yD"] = yD
            if debug:
                pr.add("gpsimd", lambda g_, c0=c0: g_.dma_start(
                    delta_h[:, :].rearrange("p (dc t) -> p dc t", t=T)[:, :, c0:c0 + 512],
                    delta[:].rearrange("p (dc t) -> p dc t", t=512)),
                    ndma=1, stream=s_sx, deps=[pv["spv"]])
                pr.add("gpsimd", lambda g_, c0=c0: g_.dma_start(
                    y_h[:, :].rearrange("p (dc t) -> p dc t", t=T)[:, :, c0:c0 + 512],
                    yac[:].rearrange("p (dc t) -> p dc t", t=512)),
                    ndma=1, stream=s_sx, deps=[yD])
            for s4 in range(4):
                tt = c * 4 + s4
                tpy = pr.add("tensor", lambda t, s4=s4: [
                    t.transpose(ps_trf[:, i * 128:(i + 1) * 128],
                                yac[:, i * 512 + s4 * 128:i * 512 + (s4 + 1) * 128],
                                idf[:]) for i in range(8)],
                    deps=[yD, pv["lnv"], pv["evx"], pv.get("evy")])
                pv["tpq"] = tpy
                evy = pr.add("vector", lambda v: v.tensor_copy(
                    ytok[:], ps_trf[:, 0:1024]), deps=[tpy, pv["ro"]])
                pv["evy"] = evy
                sqy = pr.add("scalar", lambda s: s.activation(
                    sq1[:], ytok[:], AF.Square), deps=[evy, pv["lnv"]])

                def yst_fn(v):
                    pr.fence("vector", v, v.reduce_sum(stat[:, 0:1], ytok[:],
                                                       axis=AXX))
                    pr.fence("vector", v, v.reduce_sum(stat[:, 1:2], sq1[:],
                                                       axis=AXX))
                    pr.fence("vector", v, v.tensor_scalar(
                        stat[:, 2:3], stat[:, 0:1], 1.0 / D, None, op0=ALU.mult))
                    pr.fence("vector", v, v.tensor_tensor(
                        stat[:, 3:4], stat[:, 2:3], stat[:, 0:1], ALU.mult))
                    pr.fence("vector", v, v.scalar_tensor_tensor(
                        stat[:, 4:5], stat[:, 1:2], 1.0, stat[:, 3:4],
                        op0=ALU.mult, op1=ALU.subtract))
                    return v.tensor_scalar(stat[:, 4:5], stat[:, 4:5],
                                           1.0 / (D - 1), None, op0=ALU.mult)
                yst = pr.add("vector", yst_fn, deps=[evy, sqy])

                def ylnp_fn(s):
                    pr.fence("scalar", s, s.activation(stat[:, 5:6], stat[:, 4:5],
                                                       AF.Ln))
                    return s.activation(stat[:, 6:7], stat[:, 5:6],
                                        AF.Exp, scale=0.5)
                ylnp = pr.add("scalar", ylnp_fn, deps=[yst])

                def yrec_fn(v):
                    pr.fence("vector", v, v.tensor_scalar(
                        stat[:, 6:7], stat[:, 6:7], 1e-12, None, op0=ALU.add))
                    pr.fence("vector", v, v.reciprocal(stat[:, 7:8], stat[:, 6:7]))
                    pr.fence("vector", v, v.tensor_tensor(
                        stat[:, 8:9], stat[:, 2:3], stat[:, 7:8], ALU.mult))
                    pr.fence("vector", v, v.tensor_scalar(
                        stat[:, 8:9], stat[:, 8:9], -1.0, None, op0=ALU.mult))
                    return v.nop()
                yrec = pr.add("vector", yrec_fn, deps=[ylnp])
                lnv = pr.add("vector", lambda v: v.scalar_tensor_tensor(
                    ytok[:], ytok[:], stat[:, 7:8],
                    stat[:, 8:9].to_broadcast([128, D]),
                    op0=ALU.mult, op1=ALU.add), deps=[yrec])
                pv["lnv"] = lnv
                lxr = pr.add("sync", lambda sy, tt=tt: sy.dma_start(
                    xbuf[:], x_in[tt * 128:(tt + 1) * 128, :]), ndma=1,
                    stream=s_go, deps=[pv["ro"], lnv] +
                    ([p1_ln[-1], p1_vr[-1]] if c == 0 and s4 == 0 else []))

                def out_fn(v):
                    ii = []
                    ii.append(v.tensor_tensor(obuf[:], ytok[:], lnar[:], ALU.mult))
                    ii.append(v.tensor_tensor(obuf[:], obuf[:], lnbr[:], ALU.add))
                    ii.append(v.tensor_tensor(obuf[:], obuf[:], xbuf[:], ALU.add))
                    return ii
                ro = pr.add("vector", out_fn, deps=[lnv, lxr, pv["go"], CONST])
                pv["ro"] = ro
                go = pr.add("sync", lambda sy, tt=tt: sy.dma_start(
                    xout[tt * 128:(tt + 1) * 128, :], obuf[:]), ndma=1,
                    stream=s_go, deps=[ro])
                pv["go"] = go

        if fu:
            return _fuse_tail(nc, pr, blk, fu, pfx)
        pr.add("gpsimd", lambda g_: g_.nop(), deps=pr.barrier_deps())
        pr.emit(blk)
    return nc


def ssm_in_maps(x_full, xpw, dtw, dtb, alog, dpar, lna, lnb, pfx="", with_x=True):
    base = {
        "xpw": np.ascontiguousarray(xpw), "dtw": np.ascontiguousarray(dtw),
        "dtb": np.ascontiguousarray(dtb), "alog": np.ascontiguousarray(alog),
        "dpar": np.ascontiguousarray(dpar),
        "lna": np.ascontiguousarray(lna)[None, :],
        "lnb": np.ascontiguousarray(lnb)[None, :],
        "idb": np.eye(128, dtype=np.float32).astype(BF),
        "idf": np.eye(128, dtype=np.float32),
        "cval": np.array([[LN_EPS, RMS_EPS, 1.0, 0, 0, 0, 0, 0]], dtype=np.float32),
    }
    base = {pfx + k: v for k, v in base.items()}
    in_maps = []
    for b in range(x_full.shape[0]):
        m = dict(base)
        if with_x:
            m[pfx + "x"] = np.ascontiguousarray(x_full[b])
        in_maps.append(m)
    return in_maps


def run_l1_attn(x_full, xpw, dtw, dtb, alog, dpar, lna, lnb, T=4096, raw=False, debug=False):
    nc = build_l1_attn(T, debug=debug)
    in_maps = ssm_in_maps(x_full, xpw, dtw, dtb, alog, dpar, lna, lnb)
    res = run_bass_kernel_spmd(nc, in_maps, core_ids=list(range(len(in_maps))))
    if raw:
        return res
    return np.stack([np.asarray(r["xout"]) for r in res.results])


NEXP, KTOP, DFF = 8, 2048, 4096


def build_l1_ffn(T=4096, NC=8, fu=None):
    NTT = T // 128
    TOT = T * NC
    GPP = TOT // 128
    NJT = KTOP // 128
    pfx = fu["pfx"] if fu else ""
    nc = fu["nc"] if fu else bass.Bass("TRN2", target_bir_lowering=False)
    DT = lambda name, *a, **k: nc.dram_tensor(pfx + name, *a, **k)

    if fu:
        x_in, xout = fu["x_in"], fu["x_out"]
    else:
        x_in = DT("x", [T, D], F32, kind="ExternalInput").ap()
        xout = DT("xout", [T, D], F32, kind="ExternalOutput").ap()
    rw_in = DT("rw", [D, NEXP], F32, kind="ExternalInput").ap()
    rb_in = DT("rb", [1, NEXP], F32, kind="ExternalInput").ap()
    w1_in = DT("w1", [D, DFF], F32, kind="ExternalInput").ap()
    b1_in = DT("b1", [1, DFF], F32, kind="ExternalInput").ap()
    w2_in = DT("w2", [DFF, D], F32, kind="ExternalInput").ap()
    b2_in = DT("b2", [1, D], F32, kind="ExternalInput").ap()
    idb_in = DT("idb", [128, 128], BF16, kind="ExternalInput").ap()
    ltri_in = DT("ltri", [128, 128], F32, kind="ExternalInput").ap()
    siota_in = DT("siota", [1, 64], F32, kind="ExternalInput").ap()
    piota_in = DT("piota", [128, 1], F32, kind="ExternalInput").ap()
    cval_in = DT("cval", [1, 8], F32, kind="ExternalInput").ap()
    coreid_in = DT("coreid", [1, 1], F32, kind="ExternalInput").ap()

    xlnb_loc = DT("xlnb_loc", [T, D], BF16).ap()
    xlnb_full = DT("xlnb_full", [TOT, D], BF16).ap()
    lg_loc = DT("lg_loc", [T, NEXP], F32).ap()
    lg_full = DT("lg_full", [TOT, NEXP], F32).ap()
    sc_h = DT("sc_h", [TOT, NEXP], F32).ap()
    idx_h = DT("idx_h", [KTOP + 128], I32).ap()
    cont_loc = DT("cont_loc", [KTOP + 1, D], F32).ap()
    cont_full = DT("cont_full", [NC * (KTOP + 1), D], F32).ap()

    with ExitStack() as ctx:
        S = lambda name, shape, dt: ctx.enter_context(nc.sbuf_tensor(pfx + name, shape, dt))
        idb = S("idb_s", [128, 128], BF16)
        ltri = S("ltri_s", [128, 128], F32)
        siota = S("siota_s", [128, 64], F32)
        piota = S("piota_s", [128, 1], F32)
        cvals = S("cvals", [128, 8], F32)
        brep = S("brep", [128, NEXP], F32)
        coreid = S("coreid_s", [128, 1], F32)
        rwb = S("rwb", [128, 8 * NEXP], BF16)
        inde = S("inde", [128, NEXP], F32)
        xbuf = S("xbuf", [128, D], F32)
        xlnbf = S("xlnbf", [128, D], BF16)
        stat = S("stat", [128, 16], F32)
        lgt = S("lgt", [128, NEXP], F32)
        sc_sb = S("sc_sb", [128, GPP * NEXP], F32)
        junk = S("junk", [128, GPP], F32)
        bst = S("bst", [128, 64], F32)
        csml = S("csml", [1, NEXP], F32)
        rc = S("rc", [128, GPP], F32)
        onesb = S("onesb", [128, max(GPP, 128)], F32)
        mask = S("mask", [128, GPP], F32)
        scown = S("scown", [128, GPP], F32)
        hiown = S("hiown", [128, 1], F32)
        pos = S("pos", [128, 64], F32)
        gid = S("gid", [128, 64], F32)
        slot = S("slot", [128, 64], F32)
        gidi = S("gidi", [128, 64], I32)
        sloti = S("sloti", [128, 64], I32)
        bsti = S("bsti", [128, 16], I32)
        ocol = S("ocol", [128, 1], F32)
        idxpre = S("idxpre", [128, NJT], F32)
        idxprei = S("idxprei", [128, NJT], I32)
        glist = S("glist", [128, NJT], F32)
        g8 = S("g8", [128, NEXP], F32)
        xg = S("xg", [128, 2 * D], BF16)
        xgT = S("xgT", [128, D], BF16)
        w1b = S("w1b", [128, 8 * DFF], BF16)
        w2b = S("w2b", [128, 32 * D], BF16)
        b1rep = S("b1rep", [128, 2048], F32)
        b2rep = S("b2rep", [128, D], F32)
        h1b = S("h1b", [128, DFF], BF16)
        h1T = S("h1T", [128, DFF], BF16)
        cont = S("cont", [128, D], F32)
        srcr = S("srcr", [128, GPP], F32)
        srciw = S("srciw", [128, NEXP * GPP], F32)
        vcol8 = S("vcol8", [128, NEXP], F32)
        icol = S("icol", [128, 1], I32)
        gbuf = S("gbuf", [128, D], F32)
        obuf = S("obuf", [128, D], F32)
        vcol = S("vcol", [128, 2], F32)

        P_ = lambda name, shape, dt: ctx.enter_context(nc.psum_tensor(pfx + name, shape, dt))
        ps_tr = P_("ps_tr", [128, 2048], BF16)
        ps_a = P_("ps_a", [128, 512], F32)
        ps_b = P_("ps_b", [128, 1024], F32)
        ps_sm = P_("ps_sm", [128, 8], F32)

        pr = Prog(nc, fu["sem_ctx"] if fu else ctx, pfx)
        s_const = pr.stream("s_const", 1)
        s_lx = pr.stream("s_lx", 2)
        s_sx = pr.stream("s_sx", 2)
        s_sl = pr.stream("s_sl", 2)
        s_w = pr.stream("s_w", 1)
        s_ix = pr.stream("s_ix", 4)
        s_go = pr.stream("s_go", 2)
        blk = fu["blk"] if fu else ctx.enter_context(nc.Block())

        CONST = pr.add("sync", lambda sy: [
            sy.dma_start(idb[:], idb_in[:]),
            sy.dma_start(ltri[:], ltri_in[:]),
            sy.dma_start(siota[:], siota_in.to_broadcast([128, 64])),
            sy.dma_start(piota[:], piota_in[:]),
            sy.dma_start(cvals[:], cval_in.to_broadcast([128, 8])),
            sy.dma_start(brep[:], rb_in.to_broadcast([128, NEXP])),
            sy.dma_start(coreid[:], coreid_in.to_broadcast([128, 1])),
            sy.dma_start(b2rep[:], b2_in.to_broadcast([128, D]))],
            ndma=8, stream=s_const)
        WLOAD = pr.add("gpsimd", lambda g_: [
            g_.dma_start(rwb[:].rearrange("p (dc e) -> p dc e", e=NEXP),
                         rw_in.rearrange("(dc p) e -> p dc e", p=128))] + [
            g_.dma_start(w1b[:, dc * DFF:(dc + 1) * DFF],
                         w1_in[dc * 128:(dc + 1) * 128, :]) for dc in range(8)] + [
            g_.dma_start(w2b[:, fs * D:(fs + 1) * D],
                         w2_in[fs * 128:(fs + 1) * 128, :]) for fs in range(32)],
            ndma=41, stream=s_w)
        indt = pr.add("vector", lambda v: v.scalar_tensor_tensor(
            inde[:], siota[:, 0:8], 1.0, coreid[:].to_broadcast([128, 8]),
            op0=ALU.mult, op1=ALU.is_equal), deps=[CONST])

        p1_ln = [None] * NTT
        p1_vr = [None] * NTT
        p1_sx = [None] * NTT
        for tt in range(NTT):
            ld = pr.add("sync", lambda sy, tt=tt: sy.dma_start(
                xbuf[:], x_in[tt * 128:(tt + 1) * 128, :]), ndma=1, stream=s_lx,
                deps=[p1_ln[tt - 1] if tt >= 1 else None])
            sq_t = pr.add("scalar", lambda s: s.activation(
                gbuf[:], xbuf[:], AF.Square),
                deps=[ld, p1_vr[tt - 1] if tt >= 1 else None])

            def vr_fn(v):
                pr.fence("vector", v, v.reduce_sum(stat[:, 0:1], xbuf[:], axis=AXX))
                pr.fence("vector", v, v.reduce_sum(stat[:, 1:2], gbuf[:], axis=AXX))
                pr.fence("vector", v, v.tensor_scalar(stat[:, 2:3], stat[:, 0:1],
                                                      1.0 / D, None, op0=ALU.mult))
                pr.fence("vector", v, v.tensor_tensor(stat[:, 3:4], stat[:, 2:3],
                                                      stat[:, 2:3], ALU.mult))
                return v.scalar_tensor_tensor(stat[:, 4:5], stat[:, 1:2], 1.0 / D,
                                              stat[:, 3:4], op0=ALU.mult,
                                              op1=ALU.subtract)
            vr = pr.add("vector", vr_fn, deps=[ld, sq_t])
            p1_vr[tt] = vr

            def lnp_fn(s):
                pr.fence("scalar", s, s.activation(stat[:, 5:6], stat[:, 4:5],
                                                   AF.Ln, bias=cvals[:, 0:1]))
                pr.fence("scalar", s, s.activation(stat[:, 6:7], stat[:, 5:6],
                                                   AF.Exp, scale=-0.5))
                pr.fence("scalar", s, s.activation(stat[:, 7:8], stat[:, 2:3],
                                                   AF.Copy, scale=stat[:, 6:7]))
                return s.mul(stat[:, 7:8], stat[:, 7:8], -1.0)
            lnp = pr.add("scalar", lnp_fn, deps=[vr, CONST])
            ln = pr.add("vector", lambda v: v.scalar_tensor_tensor(
                xlnbf[:], xbuf[:], stat[:, 6:7], stat[:, 7:8].to_broadcast([128, D]),
                op0=ALU.mult, op1=ALU.add),
                deps=[lnp, p1_sx[tt - 1] if tt >= 1 else None])
            p1_ln[tt] = ln
            p1_sx[tt] = pr.add("sync", lambda g_, tt=tt: g_.dma_start(
                xlnb_loc[tt * 128:(tt + 1) * 128, :], xlnbf[:]), ndma=1, stream=s_sx,
                deps=[ln])

        prev_tp = None
        prev_ev = None
        prev_sl = None
        prev_mm = None
        for tt in range(NTT):
            llx = pr.add("sync", lambda sy, tt=tt: sy.dma_start(
                xlnbf[:], xlnb_loc[tt * 128:(tt + 1) * 128, :]), ndma=1, stream=s_lx,
                deps=[p1_sx[tt], prev_tp] +
                     ([p1_ln[-1], p1_vr[-1], p1_sx[-1], p1_sx[-2]] if tt == 0 else []))
            tp = pr.add("tensor", lambda t: [
                t.transpose(ps_tr[:, dc * 128:(dc + 1) * 128],
                            xlnbf[:, dc * 128:(dc + 1) * 128], idb[:])
                for dc in range(8)], deps=[llx, CONST, prev_ev])
            prev_tp = tp
            ev = pr.add("vector", lambda v: v.tensor_copy(h1T[:, 0:1024],
                                                          ps_tr[:, 0:1024]),
                        deps=[tp, prev_mm])
            prev_ev = ev
            mm = pr.add("tensor", lambda t: [
                t.matmul(ps_sm[:], h1T[:, dc * 128:(dc + 1) * 128],
                         rwb[:, dc * NEXP:(dc + 1) * NEXP],
                         start=(dc == 0), stop=(dc == 7), skip_group_check=True)
                for dc in range(8)], deps=[ev, WLOAD, prev_sl])
            prev_mm = mm
            evl = pr.add("vector", lambda v: v.tensor_copy(lgt[:], ps_sm[:]),
                         deps=[mm])
            sl = pr.add("sync", lambda g_, tt=tt: g_.dma_start(
                lg_loc[tt * 128:(tt + 1) * 128, :], lgt[:]), ndma=1, stream=s_sl,
                deps=[evl])
            prev_sl = sl

        cc1 = pr.add("gpsimd", lambda g_: g_.collective_compute(
            "AllGather", ALU.bypass, replica_groups=[list(range(NC))],
            ins=[xlnb_loc[:, :].opt()], outs=[xlnb_full[:, :].opt()]),
            deps=[p1_sx[-1], p1_sx[-2], p1_ln[-1]])
        cc2 = pr.add("gpsimd", lambda g_: g_.collective_compute(
            "AllGather", ALU.bypass, replica_groups=[list(range(NC))],
            ins=[lg_loc[:, :].opt()], outs=[lg_full[:, :].opt()]),
            deps=[prev_sl, cc1])

        def lsc_fn(sy):
            out = []
            gq = GPP // 4
            for q4 in range(4):
                out.append(sy.dma_start(
                    sc_sb[:, q4 * gq * NEXP:(q4 + 1) * gq * NEXP]
                    .rearrange("p (g e) -> p g e", e=NEXP),
                    lg_full.rearrange("(g p) e -> p g e", p=128)
                    [:, q4 * gq:(q4 + 1) * gq, :]))
            return out
        lsc = pr.add("sync", lsc_fn, ndma=4, stream=s_lx, deps=[cc2])
        sc3 = sc_sb[:].rearrange("p (g e) -> p g e", e=NEXP)

        def sm1_fn(v):
            brep3 = brep[:].rearrange("p (o e) -> p o e", o=1).to_broadcast(
                [128, GPP, NEXP])
            pr.fence("vector", v, v.tensor_tensor(sc3, sc3, brep3, ALU.add))
            pr.fence("vector", v, v.tensor_reduce(
                rc[:, 0:GPP], sc3, axis=AXX, op=ALU.max))
            rcb = rc[:, 0:GPP].rearrange("p (g o) -> p g o", o=1).to_broadcast(
                [128, GPP, NEXP])
            return v.tensor_tensor(sc3, sc3, rcb, ALU.subtract)
        sm1 = pr.add("vector", sm1_fn, deps=[lsc, CONST])
        sm2 = pr.add("scalar", lambda s: s.activation(sc_sb[:], sc_sb[:], AF.Exp),
                     deps=[sm1])

        def sm3_fn(v):
            pr.fence("vector", v, v.tensor_reduce(
                rc[:, 0:GPP], sc3, axis=AXX, op=ALU.add))
            pr.fence("vector", v, v.reciprocal(rc[:, 0:GPP], rc[:, 0:GPP]))
            rcb = rc[:, 0:GPP].rearrange("p (g o) -> p g o", o=1).to_broadcast(
                [128, GPP, NEXP])
            return v.tensor_tensor(sc3, sc3, rcb, ALU.mult)
        sm3 = pr.add("vector", sm3_fn, deps=[sm2])
        def ssc_fn(g_):
            out = []
            gq = GPP // 4
            for q4 in range(4):
                out.append(g_.dma_start(
                    sc_h.rearrange("(g p) e -> p g e", p=128)
                    [:, q4 * gq:(q4 + 1) * gq, :],
                    sc_sb[:, q4 * gq * NEXP:(q4 + 1) * gq * NEXP]
                    .rearrange("p (g e) -> p g e", e=NEXP)))
            return out
        ssc = pr.add("sync", ssc_fn, ndma=4, stream=s_sl, deps=[sm3])

        blo, bhi = bst[:, 0:8], bst[:, 8:16]
        bm_, bsel = bst[:, 16:24], bst[:, 24:32]
        bcnt, bprt = bst[:, 32:40], bst[:, 40:48]
        bseli, bselni = bsti[:, 0:8], bsti[:, 8:16]
        init = pr.add("vector", lambda v: [
            v.memset(blo, 0.0), v.memset(bhi, 1.0), v.memset(onesb[:], 1.0)],
            deps=[sm3])
        prev_it = init
        for it in range(42):
            def itm_fn(v):
                pr.fence("vector", v, v.tensor_tensor(bm_, blo, bhi, ALU.add))
                pr.fence("vector", v, v.tensor_scalar(bm_, bm_, 0.5, None,
                                                      op0=ALU.mult))
                for e in range(NEXP):
                    sce = sc3[:, :, e:e + 1].rearrange("p g o -> p (g o)")
                    pr.fence("vector", v, v.tensor_scalar(
                        junk[:], sce, bm_[:, e:e + 1], 0.0, op0=ALU.is_gt,
                        op1=ALU.add, accum_out=bprt[:, e:e + 1]))
                return v.nop()
            itm = pr.add("vector", itm_fn, deps=[prev_it])
            summ = pr.add("tensor", lambda t: t.matmul(
                ps_sm[0:1, 0:8], onesb[:, 0:1], bprt,
                start=True, stop=True, skip_group_check=True), deps=[itm, init])
            evc = pr.add("vector", lambda v: v.tensor_copy(
                csml[0:1, :], ps_sm[0:1, 0:8]), deps=[summ])
            rep = pr.add("tensor", lambda t: t.matmul(
                ps_sm[:, 0:8], onesb[0:1, 0:128], csml[0:1, :],
                start=True, stop=True, skip_group_check=True), deps=[evc])

            def upd_fn(v):
                pr.fence("vector", v, v.tensor_copy(bcnt, ps_sm[:, 0:8]))
                pr.fence("vector", v, v.tensor_scalar(bseli, bcnt, float(KTOP),
                                                      None, op0=ALU.is_le))
                pr.fence("vector", v, v.tensor_scalar(bselni, bcnt, float(KTOP),
                                                      None, op0=ALU.is_gt))
                pr.fence("vector", v, v.select(bhi, bseli, bm_, bhi))
                pr.fence("vector", v, v.select(blo, bselni, bm_, blo))
                return v.nop()
            upd = pr.add("vector", upd_fn, deps=[rep])
            prev_it = upd

        def own_fn(v):
            for e in range(NEXP):
                sce = sc3[:, :, e:e + 1].rearrange("p g o -> p (g o)")
                if e == 0:
                    pr.fence("vector", v, v.tensor_scalar(
                        scown[:], sce, inde[:, 0:1], None, op0=ALU.mult))
                else:
                    pr.fence("vector", v, v.scalar_tensor_tensor(
                        scown[:], sce, inde[:, e:e + 1], scown[:],
                        op0=ALU.mult, op1=ALU.add))
            pr.fence("vector", v, v.tensor_tensor(bm_, bhi, inde[:], ALU.mult))
            pr.fence("vector", v, v.tensor_reduce(
                hiown[:], bm_.rearrange("p (o e) -> p o e", o=1),
                axis=AXX, op=ALU.add))
            pr.fence("vector", v, v.tensor_scalar(
                mask[:], scown[:], hiown[:, 0:1], None, op0=ALU.is_gt))
            pr.fence("vector", v, v.tensor_tensor_scan(
                rc[:, 0:GPP], onesb[:, 0:GPP], mask[:], 0.0, ALU.mult, ALU.add))
            return v.nop()
        own = pr.add("vector", own_fn, deps=[prev_it, indt])
        ocm = pr.add("tensor", lambda t: t.matmul(
            ps_sm[:, 0:1], ltri[:], rc[:, GPP - 1:GPP],
            start=True, stop=True, skip_group_check=True), deps=[own])

        def pos_fn(v):
            pr.fence("vector", v, v.tensor_copy(ocol[:], ps_sm[:, 0:1]))
            for s5 in range(64):
                pr.fence("vector", v, v.tensor_scalar(
                    junk[:], rc[:, 0:GPP], float(s5), 0.0, op0=ALU.is_le,
                    op1=ALU.add, accum_out=pos[:, s5:s5 + 1]))
            pr.fence("vector", v, v.scalar_tensor_tensor(
                gid[:], pos[:], 128.0, piota[:].to_broadcast([128, 64]),
                op0=ALU.mult, op1=ALU.add))
            pr.fence("vector", v, v.tensor_scalar(
                slot[:], pos[:], float(GPP) - 0.5, 1e9, op0=ALU.is_gt,
                op1=ALU.mult))
            pr.fence("vector", v, v.tensor_tensor(gid[:], gid[:], slot[:],
                                                  ALU.add))
            pr.fence("vector", v, v.scalar_tensor_tensor(
                slot[:], slot[:], 1.0, siota[:, 0:64], op0=ALU.mult, op1=ALU.add))
            pr.fence("vector", v, v.tensor_tensor(
                slot[:], slot[:], ocol[:].to_broadcast([128, 64]), ALU.add))
            pr.fence("vector", v, v.tensor_scalar(
                gid[:], gid[:], float(TOT - 1), None, op0=ALU.min))
            pr.fence("vector", v, v.tensor_scalar(
                slot[:], slot[:], float(KTOP), None, op0=ALU.min))
            pr.fence("vector", v, v.tensor_copy(gidi[:], gid[:]))
            pr.fence("vector", v, v.tensor_copy(sloti[:], slot[:]))
            pr.fence("vector", v, v.memset(idxpre[:], 0.0))
            pr.fence("vector", v, v.tensor_copy(idxprei[:], idxpre[:]))
            return v.nop()
        post = pr.add("vector", pos_fn, deps=[ocm, CONST])
        pre = pr.add("sync", lambda g_: g_.dma_start(
            idx_h[0:KTOP].rearrange("(jt p) -> p jt", p=128), idxprei[:],
            allow_slow_non_contiguous=True), ndma=1, stream=s_sl, deps=[post])
        scts = []
        for s5 in range(64):
            sct = pr.add("gpsimd", lambda g_, s5=s5: g_.indirect_dma_start(
                idx_h.rearrange("(a b) -> a b", b=1),
                IOff(sloti[:, s5:s5 + 1], 0), gidi[:, s5:s5 + 1], None),
                ndma=1, stream=s_ix, deps=[post, pre])
            scts.append(sct)
        lidx = pr.add("sync", lambda sy: sy.dma_start(
            idxprei[:], idx_h[0:KTOP].rearrange("(jt p) -> p jt", p=128),
            allow_slow_non_contiguous=True), ndma=1, stream=s_lx, deps=scts)

        xgz = pr.add("vector", lambda v: v.memset(xg[:], 0.0), deps=[])
        gzz = pr.add("vector", lambda v: v.memset(g8[:], 0.0), deps=[])
        pv7 = {"tp": None, "ev": None, "mm1": None, "hr": None, "tph": None,
               "tph2": None, "evh": None, "mm2": None, "co": None, "sc": None,
               "gx": [None] * NJT, "tps": [None] * NJT, "gsel": None}
        for jt in range(NJT):
            xslot = jt % 2
            gg = pr.add("gpsimd", lambda g_, jt=jt: g_.indirect_dma_start(
                g8[:], None, sc_h, IOff(idxprei[:, jt:jt + 1], 0)),
                ndma=1, stream=s_ix, deps=[lidx, ssc, gzz, pv7["gsel"]])

            def gsel_fn(v, jt=jt):
                pr.fence("vector", v, v.tensor_tensor(
                    g8[:].rearrange("p (o e) -> p o e", o=1),
                    g8[:].rearrange("p (o e) -> p o e", o=1),
                    inde[:].rearrange("p (o e) -> p o e", o=1), ALU.mult))
                return v.tensor_reduce(
                    glist[:, jt:jt + 1],
                    g8[:].rearrange("p (o e) -> p o e", o=1), axis=AXX,
                    op=ALU.add)
            gsel = pr.add("vector", gsel_fn, deps=[gg, indt])
            pv7["gsel"] = gsel
            gx = pr.add("gpsimd", lambda g_, jt=jt, xslot=xslot: g_.indirect_dma_start(
                xg[:, xslot * D:(xslot + 1) * D], None, xlnb_full,
                IOff(idxprei[:, jt:jt + 1], 0)),
                ndma=1, stream=s_ix,
                deps=[lidx, cc1, xgz, pv7["tps"][jt - 2] if jt >= 2 else None,
                      pv7["gx"][jt - 2] if jt >= 2 else None])
            pv7["gx"][jt] = gx
            tp = pr.add("tensor", lambda t, xslot=xslot: [
                t.transpose(ps_tr[:, dc * 128:(dc + 1) * 128],
                            xg[:, xslot * D + dc * 128:xslot * D + (dc + 1) * 128],
                            idb[:])
                for dc in range(8)], deps=[gx, CONST, pv7["ev"]])
            pv7["tp"] = tp
            pv7["tps"][jt] = tp
            ev = pr.add("vector", lambda v: v.tensor_copy(xgT[:], ps_tr[:, 0:1024]),
                        deps=[tp, pv7["mm1"]])
            pv7["ev"] = ev
            for fc in range(8):
                if fc % 4 == 0:
                    lb1 = pr.add("sync", lambda sy, fc=fc: sy.dma_start(
                        b1rep[:], b1_in[0:1, fc * 512:fc * 512 + 2048]
                        .to_broadcast([128, 2048])), ndma=1, stream=s_lx,
                        deps=[pv7["hr"]])
                mm1 = pr.add("tensor", lambda t, fc=fc: [
                    t.matmul(ps_a[:], xgT[:, dc * 128:(dc + 1) * 128],
                             w1b[:, dc * DFF + fc * 512:dc * DFF + (fc + 1) * 512],
                             start=(dc == 0), stop=(dc == 7),
                             skip_group_check=True)
                    for dc in range(8)],
                    deps=[ev, WLOAD, pv7["hr"]])
                pv7["mm1"] = mm1

                def hr_fn(v, fc=fc):
                    pr.fence("vector", v, v.scalar_tensor_tensor(
                        gbuf[:, 0:512], ps_a[:], 1.0,
                        b1rep[:, (fc % 4) * 512:(fc % 4 + 1) * 512],
                        op0=ALU.mult, op1=ALU.add))
                    return v.tensor_scalar(h1b[:, fc * 512:(fc + 1) * 512],
                                           gbuf[:, 0:512], 0.0, None, op0=ALU.max)
                hr = pr.add("vector", hr_fn, deps=[mm1, CONST, lb1, pv7["tph2"]])
                pv7["hr"] = hr
            tph = pr.add("tensor", lambda t: [
                t.transpose(ps_tr[:, i * 128:(i + 1) * 128],
                            h1b[:, i * 128:(i + 1) * 128], idb[:])
                for i in range(8)], deps=[pv7["hr"], pv7["evh"]])
            pv7["tph"] = tph
            evh = pr.add("vector", lambda v: v.tensor_copy(
                h1T[:, 0:1024], ps_tr[:, 0:1024]), deps=[tph, pv7["mm2"]])
            tph2 = pr.add("tensor", lambda t: [
                t.transpose(ps_tr[:, 1024 + i * 128:1024 + (i + 1) * 128],
                            h1b[:, 1024 + i * 128:1024 + (i + 1) * 128], idb[:])
                for i in range(8)] + [
                t.transpose(ps_tr[:, i * 128:(i + 1) * 128],
                            h1b[:, 2048 + i * 128:2048 + (i + 1) * 128], idb[:])
                for i in range(8)], deps=[pv7["hr"], evh, pv7["evh"]])
            evh2 = pr.add("vector", lambda v: [
                v.tensor_copy(h1T[:, 1024:2048], ps_tr[:, 1024:2048]),
                v.tensor_copy(h1T[:, 2048:3072], ps_tr[:, 0:1024])],
                deps=[tph2, pv7["mm2"]])
            tph3 = pr.add("tensor", lambda t: [
                t.transpose(ps_tr[:, 1024 + i * 128:1024 + (i + 1) * 128],
                            h1b[:, 3072 + i * 128:3072 + (i + 1) * 128], idb[:])
                for i in range(8)], deps=[pv7["hr"], evh2])
            pv7["tph2"] = tph3
            evh3 = pr.add("vector", lambda v: v.tensor_copy(
                h1T[:, 3072:4096], ps_tr[:, 1024:2048]), deps=[tph3, pv7["mm2"]])
            pv7["evh"] = evh3
            mm2 = pr.add("tensor", lambda t: [
                t.matmul(ps_b[:, h2 * 512:(h2 + 1) * 512],
                         h1T[:, fs * 128:(fs + 1) * 128],
                         w2b[:, fs * D + h2 * 512:fs * D + (h2 + 1) * 512],
                         start=(fs == 0), stop=(fs == 31), skip_group_check=True)
                for fs in range(32) for h2 in range(2)],
                deps=[evh, evh2, evh3, WLOAD, pv7["co"]])
            pv7["mm2"] = mm2

            def co_fn(v, jt=jt):
                pr.fence("vector", v, v.tensor_tensor(cont[:], ps_b[:, 0:1024],
                                                      b2rep[:], ALU.add))
                return v.tensor_scalar(cont[:], cont[:], glist[:, jt:jt + 1],
                                       None, op0=ALU.mult)
            co = pr.add("vector", co_fn, deps=[mm2, gsel, CONST])
            pv7["co"] = co
            sc_ = pr.add("sync", lambda g_, jt=jt: g_.dma_start(
                cont_loc[jt * 128:(jt + 1) * 128, :], cont[:]), ndma=1,
                stream=s_sl, deps=[co])
            pv7["sc"] = sc_

        zr = pr.add("vector", lambda v: v.memset(gbuf[:], 0.0), deps=[])
        zrs = pr.add("sync", lambda sy: sy.dma_start(
            cont_loc[KTOP:KTOP + 1, :], gbuf[0:1, :]), ndma=1, stream=s_sl,
            deps=[zr])

        cc3 = pr.add("gpsimd", lambda g_: g_.collective_compute(
            "AllGather", ALU.bypass, replica_groups=[list(range(NC))],
            ins=[cont_loc[:, :].opt()], outs=[cont_full[:, :].opt()]),
            deps=[pv7["sc"], pv7["co"], zrs])

        prev_e = None
        for e in range(NEXP):
            def msk_fn(v, e=e):
                sce = sc3[:, :, e:e + 1].rearrange("p g o -> p (g o)")
                pr.fence("vector", v, v.tensor_scalar(
                    mask[:], sce, bst[:, 8 + e:9 + e], None, op0=ALU.is_gt))
                pr.fence("vector", v, v.tensor_tensor_scan(
                    rc[:, 0:GPP], onesb[:, 0:GPP], mask[:], 0.0,
                    ALU.mult, ALU.add))
                return v.nop()
            msk = pr.add("vector", msk_fn, deps=[prev_e or post])
            ocm2 = pr.add("tensor", lambda t: t.matmul(
                ps_sm[:, 0:1], ltri[:], rc[:, GPP - 1:GPP],
                start=True, stop=True, skip_group_check=True), deps=[msk])

            def src_fn(v, e=e):
                pr.fence("vector", v, v.tensor_copy(ocol[:], ps_sm[:, 0:1]))
                pr.fence("vector", v, v.tensor_scalar(
                    srcr[:], rc[:, 0:GPP], 1.0, float(e * (KTOP + 1)) - 1.0,
                    op0=ALU.mult, op1=ALU.add))
                pr.fence("vector", v, v.tensor_tensor(
                    srcr[:], srcr[:], ocol[:].to_broadcast([128, GPP]), ALU.add))
                pr.fence("vector", v, v.tensor_scalar(
                    scown[:], mask[:], 0.5, 1e9, op0=ALU.is_lt, op1=ALU.mult))
                pr.fence("vector", v, v.tensor_tensor(
                    srcr[:], srcr[:], scown[:], ALU.add))
                pr.fence("vector", v, v.tensor_scalar(
                    srcr[:], srcr[:], float(e * (KTOP + 1) + KTOP), None,
                    op0=ALU.min))
                pr.fence("vector", v, v.tensor_copy(
                    srciw[:, e * GPP:(e + 1) * GPP], srcr[:]))
                return v.nop()
            st_ = pr.add("vector", src_fn, deps=[ocm2])
            prev_e = st_
        gbz = pr.add("vector", lambda v: v.memset(gbuf[:], 0.0), deps=[])
        pv9 = {"ro": None, "go": None, "acc": None}
        for tt in range(NTT):
            lxr = pr.add("sync", lambda sy, tt=tt: sy.dma_start(
                obuf[:], x_in[tt * 128:(tt + 1) * 128, :]), ndma=1, stream=s_go,
                deps=[pv9["go"], p1_ln[-1], p1_vr[-1]])
            pv9["acc"] = lxr
            for e in range(NEXP):
                def sel_fn(v, e=e, tt=tt):
                    view = srciw[:, e * GPP:(e + 1) * GPP].rearrange(
                        "p (b t) -> p b t", t=NTT)[:, :, tt:tt + 1].rearrange(
                        "p b o -> p (b o)")
                    pr.fence("vector", v, v.tensor_tensor(
                        vcol8[:], view, inde[:], ALU.mult))
                    pr.fence("vector", v, v.tensor_reduce(
                        vcol[:, 0:1], vcol8[:].rearrange("p (o b) -> p o b", o=1),
                        axis=AXX, op=ALU.add))
                    pr.fence("vector", v, v.tensor_copy(icol[:], vcol[:, 0:1]))
                    return v.nop()
                sel = pr.add("vector", sel_fn, deps=[prev_e, indt, pv9["gei"]
                                                     if "gei" in pv9 else None])
                ge = pr.add("gpsimd", lambda g_: g_.indirect_dma_start(
                    gbuf[:], None, cont_full, IOff(icol[:], 0)), ndma=1,
                    stream=s_ix,
                    deps=[cc3, sel, gbz, pv9["acc"]])
                pv9["gei"] = ge
                ac = pr.add("vector", lambda v: v.tensor_tensor(
                    obuf[:], obuf[:], gbuf[:], ALU.add),
                    deps=[ge, lxr, pv9["ro"]])
                pv9["acc"] = ac
                pv9["ro"] = ac
            go = pr.add("sync", lambda sy, tt=tt: sy.dma_start(
                xout[tt * 128:(tt + 1) * 128, :], obuf[:]), ndma=1, stream=s_go,
                deps=[pv9["acc"]])
            pv9["go"] = go

        if fu:
            return _fuse_tail(nc, pr, blk, fu, pfx, last=True)
        pr.add("gpsimd", lambda g_: g_.nop(), deps=pr.barrier_deps())
        pr.emit(blk)
    return nc


def moe_in_maps(x_full, rw, rb, w1, b1, w2, b2, NC=8, pfx="", with_x=True):
    ltri = np.triu(np.ones((128, 128), np.float32), 1)
    base = {
        "rw": np.ascontiguousarray(rw), "rb": np.asarray(rb, np.float32)[None, :],
        "idb": np.eye(128, dtype=np.float32).astype(BF),
        "ltri": ltri,
        "siota": np.arange(64, dtype=np.float32)[None, :],
        "piota": np.arange(128, dtype=np.float32)[:, None],
        "cval": np.array([[LN_EPS, RMS_EPS, 1.0, 0, 0, 0, 0, 0]], dtype=np.float32),
    }
    base = {pfx + k: v for k, v in base.items()}
    in_maps = []
    for b in range(NC):
        m = dict(base)
        if with_x:
            m[pfx + "x"] = np.ascontiguousarray(x_full[b])
        m[pfx + "w1"] = np.ascontiguousarray(w1[b])
        m[pfx + "b1"] = np.ascontiguousarray(b1[b])[None, :]
        m[pfx + "w2"] = np.ascontiguousarray(w2[b])
        m[pfx + "b2"] = np.ascontiguousarray(b2[b])[None, :]
        m[pfx + "coreid"] = np.array([[b]], dtype=np.float32)
        in_maps.append(m)
    return in_maps


def run_l1_ffn(x_full, rw, rb, w1, b1, w2, b2, T=4096, raw=False):
    NC = x_full.shape[0]
    nc = build_l1_ffn(T, NC)
    in_maps = moe_in_maps(x_full, rw, rb, w1, b1, w2, b2, NC)
    res = run_bass_kernel_spmd(nc, in_maps, core_ids=list(range(NC)))
    if raw:
        return res
    return np.stack([np.asarray(r["xout"]) for r in res.results])



def build_fused(T=4096, NC=8):
    nc = bass.Bass("TRN2", target_bir_lowering=False)
    x0 = nc.dram_tensor("x0", [T, D], F32, kind="ExternalInput").ap()
    x1 = nc.dram_tensor("x1", [T, D], F32).ap()
    x2 = nc.dram_tensor("x2", [T, D], F32).ap()
    x3 = nc.dram_tensor("x3", [T, D], F32).ap()
    x4 = nc.dram_tensor("x4", [T, D], F32, kind="ExternalOutput").ap()
    with ExitStack() as sctx:
        with nc.Block() as blk:
            fu = dict(nc=nc, blk=blk, sem_ctx=sctx)
            b0 = build_l0_attn(T, fu=dict(fu, pfx="a_", x_in=x0, x_out=x1,
                                          bar_prev=None))
            b1 = build_l0_ffn(T, fu=dict(fu, pfx="f_", x_in=x1, x_out=x2,
                                         bar_prev=b0))
            b2 = build_l1_attn(T, fu=dict(fu, pfx="s_", x_in=x2, x_out=x3,
                                          bar_prev=b1))
            build_l1_ffn(T, NC, fu=dict(fu, pfx="m_", x_in=x3, x_out=x4,
                                        bar_prev=b2))
    return nc


def fused_in_maps(x_full, Wq, Wk, Wv, Wo, qnw, knw, wg, wu, wd,
                  xpw, dtw, dtb, alog, dpar, lna, lnb,
                  rw, rb, w1, b1, w2, b2, T=4096, NC=8):
    a = attn_in_maps(x_full, Wq, Wk, Wv, Wo, qnw, knw, T, pfx="a_",
                     with_x=False)
    f = ffn_in_maps(x_full, wg, wu, wd, pfx="f_", with_x=False)
    s = ssm_in_maps(x_full, xpw, dtw, dtb, alog, dpar, lna, lnb, pfx="s_",
                    with_x=False)
    m = moe_in_maps(x_full, rw, rb, w1, b1, w2, b2, NC, pfx="m_",
                    with_x=False)
    out = []
    for b in range(NC):
        d = {}
        d.update(a[b]); d.update(f[b]); d.update(s[b]); d.update(m[b])
        d["x0"] = np.ascontiguousarray(x_full[b])
        out.append(d)
    return out


def kernel_multi(**inputs):
    g = lambda k: np.asarray(inputs[k], dtype=np.float32)
    x = g("L0_attn_x")
    x = run_l0_attn(x, g("L0_attn_Wq"), g("L0_attn_Wk"), g("L0_attn_Wv"),
                    g("L0_attn_Wo"), g("L0_attn_q_norm_w"),
                    g("L0_attn_k_norm_w"), T=4096)
    x = run_l0_ffn(x, g("L0_ffn_w_gate"), g("L0_ffn_w_up"),
                   g("L0_ffn_w_down"), T=4096)
    x = run_l1_attn(x, g("L1_attn_x_proj_w"), g("L1_attn_dt_proj_w"),
                    g("L1_attn_dt_proj_b"), g("L1_attn_A_log"),
                    g("L1_attn_D_param"), g("L1_attn_ln_a"),
                    g("L1_attn_ln_b"), T=4096)
    x = run_l1_ffn(x, g("L1_ffn_router_w"), g("L1_ffn_router_b"),
                   g("L1_ffn_w1"), g("L1_ffn_b1"), g("L1_ffn_w2"),
                   g("L1_ffn_b2"), T=4096)
    return np.ascontiguousarray(x.astype(np.float32))


def kernel(**inputs):
    g = lambda k: np.asarray(inputs[k], dtype=np.float32)
    x = g("L0_attn_x")
    nc = build_fused(4096, 8)
    in_maps = fused_in_maps(
        x, g("L0_attn_Wq"), g("L0_attn_Wk"), g("L0_attn_Wv"),
        g("L0_attn_Wo"), g("L0_attn_q_norm_w"), g("L0_attn_k_norm_w"),
        g("L0_ffn_w_gate"), g("L0_ffn_w_up"), g("L0_ffn_w_down"),
        g("L1_attn_x_proj_w"), g("L1_attn_dt_proj_w"), g("L1_attn_dt_proj_b"),
        g("L1_attn_A_log"), g("L1_attn_D_param"), g("L1_attn_ln_a"),
        g("L1_attn_ln_b"), g("L1_ffn_router_w"), g("L1_ffn_router_b"),
        g("L1_ffn_w1"), g("L1_ffn_b1"), g("L1_ffn_w2"), g("L1_ffn_b2"),
        T=4096, NC=8)
    res = run_bass_kernel_spmd(nc, in_maps, core_ids=list(range(8)))
    out = np.stack([np.asarray(r["x4"]) for r in res.results])
    return np.ascontiguousarray(out.astype(np.float32))


# version 04
# speedup vs baseline: 1.3530x; 1.2342x over previous
import math
import numpy as np
import ml_dtypes
from contextlib import ExitStack

import concourse.bass as bass
import concourse.mybir as mybir
from concourse.bass_utils import run_bass_kernel_spmd
from concourse.alu_op_type import AluOpType as ALU

F32 = mybir.dt.float32
BF16 = mybir.dt.bfloat16
I32 = mybir.dt.int32
AF = mybir.ActivationFunctionType
IOff = bass.IndirectOffsetOnAxis
BF = ml_dtypes.bfloat16
AXX = mybir.AxisListType.X

D = 1024
NH = 16
HD = 64
LN_EPS = 1e-5
RMS_EPS = 1e-6


class Task:
    pass


class DmaStream:
    def __init__(self, ctx, nc, name, R):
        self.sems = [ctx.enter_context(nc.semaphore(f"{name}{i}")) for i in range(R)]
        self.R = R
        self.counts = [0] * R
        self.ntasks = 0


class Prog:
    CQUEUES = ["tensor", "vector", "scalar", "gpsimd"]

    def __init__(self, nc, ctx, pfx=""):
        self.nc = nc
        self.ctx = ctx
        self.pfx = pfx
        self.tasks = []
        self.chans = {e: ctx.enter_context(nc.semaphore(f"{pfx}ch_{e}")) for e in Prog.CQUEUES}
        self.ccounts = {e: 0 for e in Prog.CQUEUES}
        self.hz = {e: ctx.enter_context(nc.semaphore(f"{pfx}hz_{e}")) for e in Prog.CQUEUES}
        self.hzc = {e: 0 for e in Prog.CQUEUES}
        self.streams = []

    def fence(self, ename, eng, inst):
        self.hzc[ename] += 1
        inst.then_inc(self.hz[ename], 1)
        eng.wait_ge(self.hz[ename], self.hzc[ename])

    def stream(self, name, R=2):
        s = DmaStream(self.ctx, self.nc, self.pfx + name, R)
        self.streams.append(s)
        return s

    def add(self, engine, fn, deps=(), ndma=None, stream=None):
        t = Task()
        t.engine = engine
        t.fn = fn
        t.deps = [d for d in deps if d is not None]
        if ndma is not None:
            assert stream is not None, "dma task needs a stream"
            t.kind = "d"
            t.stream = stream
            t.nev = ndma
            t.sem_idx = stream.ntasks % stream.R
            stream.ntasks += 1
            stream.counts[t.sem_idx] += ndma
            t.wait_count = stream.counts[t.sem_idx] * 16
        else:
            assert engine != "sync"
            t.kind = "c"
            t.stream = None
            self.ccounts[engine] += 1
            t.wait_count = self.ccounts[engine]
        self.tasks.append(t)
        return t

    def emit(self, blk, open_waits=()):
        queues = [("sync", None)] + [(e, None) for e in ["gpsimd", "tensor", "vector", "scalar"]]
        for ename, _ in queues:
            if ename == "sync":
                mine = [t for t in self.tasks if t.engine == "sync"]
            else:
                mine = [t for t in self.tasks if t.engine == ename]
            if not mine and not open_waits:
                continue
            section = getattr(blk, ename)

            def make(mine=mine, ename=ename):
                def body(eng):
                    for sem, cnt in open_waits:
                        eng.wait_ge(sem, cnt)
                    for t in mine:
                        waits = {}
                        for d in t.deps:
                            if d.kind == "c":
                                if d.engine == t.engine and t.kind == "c":
                                    continue
                                key = self.chans[d.engine]
                            else:
                                if (t.kind == "d" and d.stream is t.stream
                                        and d.sem_idx == t.sem_idx):
                                    pass
                                key = d.stream.sems[d.sem_idx]
                            waits[key] = max(waits.get(key, 0), d.wait_count)
                        for key, cnt in waits.items():
                            eng.wait_ge(key, cnt)
                        out = t.fn(eng)
                        insts = out if isinstance(out, (list, tuple)) else [out]
                        if t.kind == "d":
                            assert len(insts) == t.nev, (ename, len(insts), t.nev)
                            sem = t.stream.sems[t.sem_idx]
                            for inst in insts:
                                inst.then_inc(sem, 16)
                        else:
                            insts[-1].then_inc(self.chans[t.engine], 1)
                return body
            section(make())

    def barrier_deps(self):
        out = []
        last_c = {}
        for t in self.tasks:
            if t.kind == "c":
                last_c[t.engine] = t
        out.extend(last_c.values())
        for s in self.streams:
            seen = set()
            for t in reversed(self.tasks):
                if t.kind == "d" and t.stream is s and t.sem_idx not in seen:
                    seen.add(t.sem_idx)
                    out.append(t)
                if len(seen) == s.R:
                    break
        return out


def build_l0_attn(T=4096, debug=False, fu=None):
    NTT = T // 128
    NQC = T // 512
    pfx = fu["pfx"] if fu else ""
    nc = fu["nc"] if fu else bass.Bass("TRN2", target_bir_lowering=False)
    DT = lambda name, *a, **k: nc.dram_tensor(pfx + name, *a, **k)

    if fu:
        x_in, xout = fu["x_in"], fu["x_out"]
    else:
        x_in = DT("x", [T, D], F32, kind="ExternalInput").ap()
        xout = DT("xout", [T, D], F32, kind="ExternalOutput").ap()
    wq_in = DT("wq", [D, D], F32, kind="ExternalInput").ap()
    wk_in = DT("wk", [D, D], F32, kind="ExternalInput").ap()
    wv_in = DT("wv", [D, D], F32, kind="ExternalInput").ap()
    wo_in = DT("wo", [D, D], F32, kind="ExternalInput").ap()
    qnw_in = DT("qnw", [1, 512], F32, kind="ExternalInput").ap()
    knw_in = DT("knw", [1, 512], F32, kind="ExternalInput").ap()
    cos_in = DT("cosr", [T, 32], BF16, kind="ExternalInput").ap()
    sin_in = DT("sinr", [T, 32], BF16, kind="ExternalInput").ap()
    idb_in = DT("idb", [128, 128], BF16, kind="ExternalInput").ap()
    idf_in = DT("idf", [128, 128], F32, kind="ExternalInput").ap()
    wedge_in = DT("wedge", [128, 896], BF16, kind="ExternalInput").ap()
    cval_in = DT("cval", [1, 8], F32, kind="ExternalInput").ap()

    dbg = dict(kind="ExternalOutput") if debug else {}
    xlnT_h = DT("xlnT_h", [NTT, 128, D], BF16, **dbg).ap()
    avnf_h = DT("avnf_h", [8, NTT, 128, 128], BF16, **dbg).ap()
    negm_h = DT("negm_h", [32, 128], F32, **dbg).ap()

    with ExitStack() as ctx:
        S = lambda name, shape, dt: ctx.enter_context(nc.sbuf_tensor(pfx + name, shape, dt))
        idb = S("idb_s", [128, 128], BF16)
        idf = S("idf_s", [128, 128], F32)
        wedge = S("wedge_s", [128, 896], BF16)
        cvals = S("cvals", [128, 8], F32)
        wqrep = S("wqrep", [128, 512], F32)
        wkrep = S("wkrep", [128, 512], F32)
        cosr = S("cosr_s", [128, NTT * 32], BF16)
        sinr = S("sinr_s", [128, NTT * 32], BF16)
        xbuf = S("xbuf", [128, D], F32)
        stat = S("stat", [128, 16], F32)
        sqk = S("sqk", [128, 512], F32)
        xlnbf = S("xlnbf", [128, D], BF16)
        stg = S("stg", [128, D], BF16)
        wbuf = S("wbuf", [128, 4096], BF16)
        xts = S("xts", [128, 2 * D], BF16)
        qks = S("qks", [128, 2 * 512], F32)
        ropa = S("ropa", [128, 2 * 256], F32)
        qtok = S("qtok", [128, 512], BF16)
        ktok = S("ktok", [128, 512], BF16)
        rms = S("rms", [128, 32], F32)
        qf = S("qf", [128, 4 * T], BF16)
        kf = S("kf", [128, 4 * T], BF16)
        vg = S("vg", [128, NTT * 520], BF16)
        kst = S("kst", [80, T], BF16)
        qst = S("qst", [80, T], BF16)
        astat = S("astat", [128, 32], F32)
        astT = S("astT", [32, 128], F32)
        pt = S("pt", [128, 2 * 1024], BF16)
        av65 = S("av65", [80, 512], F32)
        avtok = S("avtok", [128, NTT * 512], BF16)
        ztok = S("ztok", [128, NTT * 8], F32)
        rzs = S("rzs", [128, NTT * 8], F32)
        avn = S("avn", [128, 1024], BF16)
        avstg = S("avstg", [128, 1024], BF16)
        avf = S("avf", [128, 2 * D], BF16)
        obuf = S("obuf", [128, 1024], F32)

        P_ = lambda name, shape, dt: ctx.enter_context(nc.psum_tensor(pfx + name, shape, dt))
        ps_tr = P_("ps_tr", [128, 2048], BF16)
        ps_big = P_("ps_big", [128, 2048], F32)
        ps_av = P_("ps_av", [80, 512], F32)
        ps_ms = P_("ps_ms", [128, 512], F32)
        ps_q = ps_big[:, 0:512]
        ps_k = ps_big[:, 512:1024]
        ps_v = ps_big[:, 1024:1536]

        pr = Prog(nc, fu["sem_ctx"] if fu else ctx, pfx)
        s_const = pr.stream("s_const", 1)
        s_lx = pr.stream("s_lx", 2)
        s_stg = pr.stream("s_stg", 2)
        s_w = pr.stream("s_w", 1)
        s_xt = pr.stream("s_xt", 2)
        s_negm = pr.stream("s_negm", 2)
        s_gav = pr.stream("s_gav", 2)
        s_lwo = pr.stream("s_lwo", 1)
        s_lav = pr.stream("s_lav", 2)
        s_lxo = pr.stream("s_lxo", 2)
        s_go = pr.stream("s_go", 2)
        s_kq = pr.stream("s_kq", 2)
        blk = fu["blk"] if fu else ctx.enter_context(nc.Block())

        def t_const(sy):
            out = []
            for a, b in [(idb, idb_in), (idf, idf_in), (wedge, wedge_in)]:
                out.append(sy.dma_start(a[:], b[:]))
            out.append(sy.dma_start(cvals[:], cval_in.to_broadcast([128, 8])))
            out.append(sy.dma_start(wqrep[:], qnw_in.to_broadcast([128, 512])))
            out.append(sy.dma_start(wkrep[:], knw_in.to_broadcast([128, 512])))
            for t_in, t_sb in [(cos_in, cosr), (sin_in, sinr)]:
                out.append(sy.dma_start(
                    t_sb[:].rearrange("p (tt i) -> p tt i", i=32),
                    t_in.rearrange("(tt p) i -> p tt i", p=128)))
            return out
        CONST = pr.add("sync", t_const, ndma=8, stream=s_const)
        vg4 = vg[:].rearrange("p (tt h c) -> p tt h c", h=8, c=65)
        ones_set = pr.add("vector", lambda v: [
            v.memset(kst[64:65, :], 1.0),
            v.memset(vg4[:, :, :, 64:65], 1.0)], deps=[])

        ln_done = [None] * NTT
        vr_done = [None] * NTT
        tp_done = [None] * NTT
        ev_done = [None] * NTT
        st_done = [None] * NTT
        sqs = qks[:, 0:D]
        for tt in range(NTT):
            xb = xbuf[:, 0:D]
            xlb = xlnbf[:, 0:D]
            ld = pr.add("sync", lambda sy, xb=xb, tt=tt: sy.dma_start(
                xb, x_in[tt * 128:(tt + 1) * 128, :]), ndma=1, stream=s_lx,
                deps=[ln_done[tt - 1] if tt >= 1 else None])
            sq_t = pr.add("scalar", lambda s, xb=xb: s.activation(
                sqs[:], xb, AF.Square),
                deps=[ld, vr_done[tt - 1] if tt >= 1 else None])
            def vr_fn(v, xb=xb):
                pr.fence("vector", v, v.reduce_sum(stat[:, 0:1], xb, axis=AXX))
                pr.fence("vector", v, v.reduce_sum(stat[:, 1:2], sqs[:], axis=AXX))
                pr.fence("vector", v, v.tensor_scalar(
                    stat[:, 2:3], stat[:, 0:1], 1.0 / D, None, op0=ALU.mult))
                pr.fence("vector", v, v.tensor_tensor(
                    stat[:, 3:4], stat[:, 2:3], stat[:, 2:3], ALU.mult))
                return v.scalar_tensor_tensor(stat[:, 4:5], stat[:, 1:2], 1.0 / D,
                                              stat[:, 3:4], op0=ALU.mult,
                                              op1=ALU.subtract)
            vr_done[tt] = pr.add("vector", vr_fn,
                deps=[ld, sq_t, ln_done[tt - 1] if tt >= 1 else None])
            def lnp_fn(s):
                pr.fence("scalar", s, s.activation(stat[:, 5:6], stat[:, 4:5],
                                                   AF.Ln, bias=cvals[:, 0:1]))
                pr.fence("scalar", s,
                         s.activation(stat[:, 6:7], stat[:, 5:6], AF.Exp, scale=-0.5))
                pr.fence("scalar", s, s.activation(stat[:, 7:8], stat[:, 2:3],
                                                   AF.Copy, scale=stat[:, 6:7]))
                return s.mul(stat[:, 7:8], stat[:, 7:8], -1.0)
            lnp = pr.add("scalar", lnp_fn, deps=[vr_done[tt], CONST])
            ln_done[tt] = pr.add("vector", lambda v, xb=xb, xlb=xlb: v.scalar_tensor_tensor(
                xlb, xb, stat[:, 6:7], stat[:, 7:8].to_broadcast([128, D]),
                op0=ALU.mult, op1=ALU.add),
                deps=[lnp, tp_done[tt - 1] if tt >= 1 else None])
            half = (tt % 2) * 1024
            tp_done[tt] = pr.add("tensor", lambda t, xlb=xlb, half=half: [
                t.transpose(ps_tr[:, half + dc * 128:half + (dc + 1) * 128],
                            xlb[:, dc * 128:(dc + 1) * 128], idb[:])
                for dc in range(8)],
                deps=[CONST, ln_done[tt], ev_done[tt - 2] if tt >= 2 else None])
            ev_done[tt] = pr.add("vector", lambda v, half=half: v.tensor_copy(
                stg[:], ps_tr[:, half:half + 1024]),
                deps=[tp_done[tt], st_done[tt - 1] if tt >= 1 else None])
            st_done[tt] = pr.add("gpsimd", lambda g, tt=tt: g.dma_start(
                xlnT_h[tt, :, :], stg[:]), ndma=1, stream=s_stg, deps=[ev_done[tt]])

        gav_done = []
        last_evA = None
        mm_all = []
        eBslot = [None] * 4
        AVslot = [None] * 4
        last_red = None
        for g in range(2):
            gofs = g * 512
            prev_grp = pr.barrier_deps() if g == 1 else []
            p1_tail = [ev_done[-1], ev_done[-2]] if g == 0 else []
            wld = [None] * 3
            mmv = {}
            psum_free = {}
            tqk_done = {}
            evq_done = {}
            for wi, w_in in enumerate([wq_in, wk_in, wv_in]):
                wld[wi] = pr.add("gpsimd", lambda gp, w_in=w_in, gofs=gofs: [
                    gp.dma_start(wbuf[:, dc * 512:(dc + 1) * 512],
                                 w_in[dc * 128:(dc + 1) * 128, gofs:gofs + 512])
                    for dc in range(8)], ndma=8, stream=s_w,
                    deps=([mmv[(wi - 1, NTT - 1)]] if wi > 0 else []) + prev_grp
                         + p1_tail)
                for tt in range(NTT):
                    git = len(mm_all)
                    xo = (git % 2) * D
                    lx = pr.add("sync", lambda sy, tt=tt, xo=xo: sy.dma_start(
                        xts[:, xo:xo + D], xlnT_h[tt, :, :]), ndma=1,
                        stream=s_xt,
                        deps=[st_done[tt], mm_all[git - 2] if git >= 2 else None])
                    po = (tt % 2) * 512
                    psd = ps_big[:, po:po + 512]
                    war = [psum_free.get((wi, tt - 2))]
                    if tt < 2:
                        war.append(psum_free.get((wi - 1, NTT - 2 + tt)))
                    mmv[(wi, tt)] = pr.add("tensor", lambda t, psd=psd, xo=xo: [
                        t.matmul(psd, xts[:, xo + dc * 128:xo + (dc + 1) * 128],
                                 wbuf[:, dc * 512:(dc + 1) * 512],
                                 start=(dc == 0), stop=(dc == 7), skip_group_check=True)
                        for dc in range(8)],
                        deps=[wld[wi], lx] + war)
                    mm_all.append(mmv[(wi, tt)])
                    if wi < 2:
                        wrep = wqrep if wi == 0 else wkrep
                        rcol, scol = (16, 0) if wi == 0 else (24, 8)
                        sq2 = pr.add("scalar", lambda s, psd=psd: s.activation(
                            sqk[:], psd, AF.Square),
                            deps=[mmv[(wi, tt)], last_red])
                        red = pr.add("vector", lambda v, scol=scol: v.tensor_reduce(
                            rms[:, scol:scol + 8],
                            sqk[:].rearrange("p (h d) -> p h d", d=HD),
                            axis=AXX, op=ALU.add), deps=[sq2])
                        last_red = red
                        def rst_fn(s, rcol=rcol, scol=scol, wi=wi):
                            pr.fence("scalar", s, s.activation(
                                rms[:, rcol:rcol + 8], rms[:, scol:scol + 8],
                                AF.Ln, scale=1.0 / HD, bias=cvals[:, 1:2]))
                            last = s.activation(rms[:, rcol:rcol + 8],
                                                rms[:, rcol:rcol + 8],
                                                AF.Exp, scale=-0.5)
                            if wi == 0:
                                pr.fence("scalar", s, last)
                                last = s.mul(rms[:, rcol:rcol + 8],
                                             rms[:, rcol:rcol + 8], 0.125)
                            return last
                        rst = pr.add("scalar", rst_fn, deps=[red, CONST])
                        tok = qtok if wi == 0 else ktok
                        cs = cosr[:, tt * 32:(tt + 1) * 32]
                        sn = sinr[:, tt * 32:(tt + 1) * 32]

                        def rope_fn(v, psd=psd, wrep=wrep, rcol=rcol, tok=tok,
                                    cs=cs, sn=sn):
                            rr = rms[:, rcol:rcol + 8]
                            qh = qks[:, 0:512]
                            ii = []
                            ii.append(v.tensor_tensor(
                                qks[:, 512:1024].rearrange("p (h d) -> p h d", d=64),
                                psd.rearrange("p (h d) -> p h d", d=64),
                                rr.rearrange("p (h o) -> p h o", o=1)
                                  .to_broadcast([128, 8, 64]), ALU.mult))
                            ii.append(v.tensor_tensor(qh, qks[:, 512:1024], wrep[:],
                                                      ALU.mult))
                            q3 = qh.rearrange("p (h t i) -> p h t i", t=2, i=32)
                            o3 = tok.rearrange("p (h t i) -> p h t i", t=2, i=32)
                            cs3 = cs.rearrange("p (o i) -> p o i", o=1).to_broadcast([128, 8, 32])
                            sn3 = sn.rearrange("p (o i) -> p o i", o=1).to_broadcast([128, 8, 32])
                            a = ropa[:, 0:256].rearrange("p (h i) -> p h i", i=32)
                            b = ropa[:, 256:512].rearrange("p (h i) -> p h i", i=32)
                            ii.append(v.tensor_tensor(a, q3[:, :, 0, :], cs3, ALU.mult))
                            ii.append(v.tensor_tensor(b, q3[:, :, 1, :], sn3, ALU.mult))
                            ii.append(v.tensor_tensor(o3[:, :, 0, :], a, b, ALU.subtract))
                            ii.append(v.tensor_tensor(a, q3[:, :, 1, :], cs3, ALU.mult))
                            ii.append(v.tensor_tensor(b, q3[:, :, 0, :], sn3, ALU.mult))
                            ii.append(v.tensor_tensor(o3[:, :, 1, :], a, b, ALU.add))
                            return ii
                        rp = pr.add("vector", rope_fn,
                                    deps=[rst, mmv[(wi, tt)], CONST,
                                          tqk_done.get((wi, tt - 1))])
                        psum_free[(wi, tt)] = rp
                        half2 = (tt % 2) * 1024
                        tqk_done[(wi, tt)] = pr.add("tensor", lambda t, tok=tok,
                                                    half2=half2: [
                            t.transpose(ps_tr[:, half2 + s4 * 128:half2 + (s4 + 1) * 128],
                                        tok[:, s4 * 128:(s4 + 1) * 128], idb[:])
                            for s4 in range(4)],
                            deps=[rp, evq_done.get((wi, tt - 2))])
                        dstf = qf if wi == 0 else kf
                        evq_done[(wi, tt)] = pr.add("vector", lambda v, dstf=dstf,
                                                    half2=half2, tt=tt: [
                            v.tensor_copy(dstf[:, rt * T + tt * 128:rt * T + (tt + 1) * 128],
                                          ps_tr[:, half2 + rt * 128:half2 + (rt + 1) * 128])
                            for rt in range(4)],
                            deps=[tqk_done[(wi, tt)]])
                    else:
                        psum_free[(wi, tt)] = pr.add(
                            "vector", lambda v, tt=tt, psd=psd: v.tensor_copy(
                                vg4[:, tt, :, 0:64],
                                psd.rearrange("p (h d) -> p h d", d=64)),
                            deps=[mmv[(wi, tt)], ones_set])

            qkv_done = pr.barrier_deps()

            last_mB = None
            psms_slot = [None] * 4
            last_stats = None
            for hg in range(8):
                rt, sub = hg // 2, hg % 2
                p0 = sub * 64
                stg_k = pr.add("sync", lambda sy, rt=rt, p0=p0: sy.dma_start(
                    kst[0:64, :], kf[p0:p0 + 64, rt * T:(rt + 1) * T]),
                    ndma=1, stream=s_kq,
                    deps=qkv_done + [ones_set, last_mB])
                stg_q = pr.add("sync", lambda sy, rt=rt, p0=p0: sy.dma_start(
                    qst[0:64, :], qf[p0:p0 + 64, rt * T:(rt + 1) * T]),
                    ndma=1, stream=s_kq,
                    deps=qkv_done + [last_mB])
                rmx = [None] * NTT
                for qt in range(NTT):
                    qtend = (qt + 1) * 128
                    w0 = max(0, qtend - 512)
                    wid = qtend - w0
                    sl = 2 + (qt % 2)
                    mA = pr.add("tensor", lambda t, qt=qt, w0=w0, wid=wid, sl=sl:
                        t.matmul(
                            ps_big[:, sl * 512:sl * 512 + wid],
                            qst[0:64, qt * 128:(qt + 1) * 128],
                            kst[0:64, w0:w0 + wid], start=True, stop=True,
                            skip_group_check=True),
                        deps=[stg_k, stg_q, rmx[qt - 2] if qt >= 2 else None,
                              eBslot[sl]])
                    rmx[qt] = pr.add("vector", lambda v, qt=qt, wid=wid, sl=sl:
                        v.reduce_max(
                            astat[:, qt:qt + 1], ps_big[:, sl * 512:sl * 512 + wid],
                            axis=AXX, negate=True),
                        deps=[mA, last_stats])
                eBslot[2] = rmx[NTT - 2]
                eBslot[3] = rmx[NTT - 1]
                tst = pr.add("tensor", lambda t: t.transpose(
                    ps_ms[0:NTT, 0:128] if NTT < 128 else ps_ms[:, 0:128],
                    astat[:, 0:NTT], idf[:]),
                    deps=[CONST, psms_slot[0]] + rmx)
                est = pr.add("vector", lambda v: v.tensor_copy(
                    astT[0:NTT, :], ps_ms[0:NTT, 0:128]), deps=[tst])
                psms_slot[0] = est
                gss = pr.add("gpsimd", lambda g_: g_.dma_start(
                    negm_h[0:NTT, :], astT[0:NTT, :]), ndma=1, stream=s_negm,
                    deps=[est])
                gmr = pr.add("gpsimd", lambda g_: g_.dma_start(
                    qst[64:65, 0:T].rearrange("p (a b) -> p a b", b=128),
                    negm_h[0:NTT, :].rearrange("(o a) b -> o a b", o=1)),
                    ndma=1, stream=s_negm, deps=[gss, stg_q])
                last_stats = gmr

                ktc = 0
                for qc in range(NQC):
                    nkt = (qc + 1) * 4
                    pend = []

                    def emit_av(ki, sl, eB_t, wt, nkt=nkt, hg=hg, qc=qc):
                        av = pr.add("tensor", lambda t, ki=ki, sl=sl: t.matmul(
                            ps_av[0:65, :], vg4[:, ki, hg, :],
                            pt[:, sl * 512:(sl + 1) * 512],
                            start=(ki == 0), stop=(ki == nkt - 1),
                            skip_group_check=True),
                            deps=[eB_t, wt] +
                                 ([last_evA] if ki == 0 else []))
                        AVslot[sl] = av
                        return av

                    for ki in range(nkt):
                        sl = ktc % 4
                        ktc += 1
                        mB = pr.add("tensor", lambda t, ki=ki, sl=sl, qc=qc:
                            t.matmul(
                                ps_big[:, sl * 512:(sl + 1) * 512],
                                kst[0:65, ki * 128:(ki + 1) * 128],
                                qst[0:65, qc * 512:(qc + 1) * 512],
                                start=True, stop=True, skip_group_check=True),
                            deps=[gmr, eBslot[sl]])
                        last_mB = mB
                        eB_t = pr.add("scalar", lambda s, sl=sl: s.activation(
                            pt[:, sl * 512:(sl + 1) * 512],
                            ps_big[:, sl * 512:(sl + 1) * 512], AF.Exp),
                            deps=[mB, AVslot[sl]])
                        eBslot[sl] = eB_t
                        o = ki * 128 - qc * 512
                        wt = None
                        if 0 <= o <= 384:
                            wt = pr.add("vector", lambda v, sl=sl, o=o:
                                v.tensor_tensor(
                                    pt[:, sl * 512:(sl + 1) * 512],
                                    pt[:, sl * 512:(sl + 1) * 512],
                                    wedge[:, 384 - o:896 - o], ALU.mult),
                                deps=[eB_t])
                        pend.append((ki, sl, eB_t, wt))
                        if len(pend) >= 2:
                            emit_av(*pend.pop(0))
                    while pend:
                        emit_av(*pend.pop(0))
                    evA4 = None
                    for s4 in range(4):
                        evA4 = pr.add("vector", lambda v, s4=s4: v.tensor_copy(
                            av65[0:65, s4 * 128:(s4 + 1) * 128],
                            ps_av[0:65, s4 * 128:(s4 + 1) * 128]),
                            deps=[AVslot[(ktc - 1) % 4]] if s4 == 0 else [])
                        tA2 = pr.add("tensor", lambda t, s4=s4: t.transpose(
                            ps_ms[:, s4 * 128:s4 * 128 + 65],
                            av65[0:65, s4 * 128:(s4 + 1) * 128],
                            idf[0:65, 0:65]), deps=[evA4, psms_slot[s4]])
                        tt_abs = qc * 4 + s4
                        evA2 = pr.add("vector", lambda v, tt_abs=tt_abs, hg=hg,
                                      s4=s4: [
                            v.tensor_copy(
                                avtok[:, tt_abs * 512 + hg * 64:tt_abs * 512 + (hg + 1) * 64],
                                ps_ms[:, s4 * 128:s4 * 128 + 64]),
                            v.tensor_copy(ztok[:, tt_abs * 8 + hg:tt_abs * 8 + hg + 1],
                                          ps_ms[:, s4 * 128 + 64:s4 * 128 + 65])],
                            deps=[tA2])
                        psms_slot[s4] = evA2

            grp_done = pr.barrier_deps()
            def rzall_fn(v):
                pr.fence("vector", v,
                         v.reciprocal(rzs[:, 0:NTT * 8], ztok[:, 0:NTT * 8]))
                return v.nop()
            rzall = pr.add("vector", rzall_fn, deps=grp_done)
            t3s = [None] * NTT
            ev3s = [None] * NTT
            gavs = [None] * NTT
            for tt in range(NTT):
                ao = (tt % 2) * 512
                rz = pr.add("vector", lambda v, tt=tt, ao=ao: v.tensor_tensor(
                    avn[:, ao:ao + 512].rearrange("p (h d) -> p h d", d=64),
                    avtok[:, tt * 512:(tt + 1) * 512]
                      .rearrange("p (h d) -> p h d", d=64),
                    rzs[:, tt * 8:(tt + 1) * 8]
                       .rearrange("p (h o) -> p h o", o=1)
                       .to_broadcast([128, 8, 64]),
                    ALU.mult),
                    deps=[rzall, t3s[tt - 2] if tt >= 2 else None])
                t3s[tt] = pr.add("tensor", lambda t, ao=ao: [
                    t.transpose(ps_tr[:, ao + s4 * 128:ao + (s4 + 1) * 128],
                                avn[:, ao + s4 * 128:ao + (s4 + 1) * 128], idb[:])
                    for s4 in range(4)],
                    deps=[rz, CONST, ev3s[tt - 2] if tt >= 2 else None])
                ev3s[tt] = pr.add("vector", lambda v, ao=ao: v.tensor_copy(
                    avstg[:, ao:ao + 512], ps_tr[:, ao:ao + 512]),
                    deps=[t3s[tt], gavs[tt - 2] if tt >= 2 else None])
                gavs[tt] = pr.add("gpsimd", lambda g_, tt=tt, g=g, ao=ao: [
                    g_.dma_start(avnf_h[g * 4 + b4, tt, :, :],
                                 avstg[:, ao + b4 * 128:ao + (b4 + 1) * 128])
                    for b4 in range(4)], ndma=4, stream=s_gav, deps=[ev3s[tt]])
                gav_done.append(gavs[tt])

        p4mo = [None] * (2 * NTT)
        p4ro = [None] * (2 * NTT)
        p4go = [None] * (2 * NTT)
        for h2 in range(2):
            lwo = pr.add("gpsimd", lambda gp, h2=h2: [
                gp.dma_start(wbuf[:, dc * 512:(dc + 1) * 512],
                             wo_in[dc * 128:(dc + 1) * 128, h2 * 512:(h2 + 1) * 512])
                for dc in range(8)], ndma=8, stream=s_w,
                deps=[mm_all[-1], gav_done[-1], gav_done[-2]] +
                     ([p4mo[NTT - 1]] if h2 == 1 else []))
            for tt in range(NTT):
                it = h2 * NTT + tt
                buf = it % 2
                lav = pr.add("sync", lambda sy, tt=tt, buf=buf: [
                    sy.dma_start(avf[:, buf * D + dc * 128:buf * D + (dc + 1) * 128],
                                 avnf_h[dc, tt, :, :]) for dc in range(8)], ndma=8,
                    stream=s_lav,
                    deps=[gav_done[-1], gav_done[-2],
                          p4mo[it - 2] if it >= 2 else None])
                mo = pr.add("tensor", lambda t, buf=buf: [
                    t.matmul(ps_big[:, buf * 512:buf * 512 + 512],
                             avf[:, buf * D + dc * 128:buf * D + (dc + 1) * 128],
                             wbuf[:, dc * 512:(dc + 1) * 512],
                             start=(dc == 0), stop=(dc == 7), skip_group_check=True)
                    for dc in range(8)],
                    deps=[lwo, lav, eBslot[buf],
                          p4ro[it - 2] if it >= 2 else None])
                p4mo[it] = mo
                lxo = pr.add("sync", lambda sy, tt=tt, buf=buf, h2=h2: sy.dma_start(
                    xbuf[:, buf * 512:(buf + 1) * 512],
                    x_in[tt * 128:(tt + 1) * 128, h2 * 512:(h2 + 1) * 512]), ndma=1,
                    stream=s_lxo,
                    deps=[ln_done[-1], vr_done[-1],
                          p4ro[it - 2] if it >= 2 else None])
                ro = pr.add("vector", lambda v, buf=buf: v.tensor_tensor(
                    obuf[:, buf * 512:buf * 512 + 512],
                    xbuf[:, buf * 512:(buf + 1) * 512],
                    ps_big[:, buf * 512:buf * 512 + 512], ALU.add),
                    deps=[mo, lxo, p4go[it - 2] if it >= 2 else None])
                p4ro[it] = ro
                go = pr.add("sync", lambda sy, tt=tt, h2=h2, buf=buf: sy.dma_start(
                    xout[tt * 128:(tt + 1) * 128, h2 * 512:(h2 + 1) * 512],
                    obuf[:, buf * 512:buf * 512 + 512]), ndma=1, stream=s_go,
                    deps=[ro])
                p4go[it] = go

        if fu:
            return _fuse_tail(nc, pr, blk, fu, pfx)
        pr.add("gpsimd", lambda g_: g_.nop(), deps=pr.barrier_deps())
        pr.emit(blk)

    return nc


def _fuse_tail(nc, pr, blk, fu, pfx, last=False):
    ow = [fu["bar_prev"]] if fu.get("bar_prev") else []
    if last:
        pr.add("gpsimd", lambda g_: g_.nop(), deps=pr.barrier_deps())
        pr.emit(blk, open_waits=ow)
        return None
    bar = fu["sem_ctx"].enter_context(nc.semaphore(pfx + "bar"))

    def jfn(g_):
        g_.nop().then_inc(bar, 1)
        return g_.nop()
    pr.add("gpsimd", jfn, deps=pr.barrier_deps())
    pr.emit(blk, open_waits=ow)
    return (bar, 1)


def host_attn_consts(T=4096):
    inv = 1.0 / (10000.0 ** (np.arange(0, HD, 2, dtype=np.float32) / HD))
    t = np.arange(T, dtype=np.float32)
    fr = np.outer(t, inv)
    cos, sin = np.cos(fr).astype(np.float32), np.sin(fr).astype(np.float32)
    idf = np.eye(128, dtype=np.float32)
    idb = np.eye(128, dtype=np.float32).astype(BF)
    wedge = np.zeros((128, 896), dtype=np.float32)
    k = np.arange(128)[:, None]
    u = np.arange(896)[None, :]
    wedge[(u - 384) >= k] = 1.0
    return cos, sin, idf, idb, wedge.astype(BF)


def attn_in_maps(x_full, Wq, Wk, Wv, Wo, qnw, knw, T=4096, pfx="", with_x=True):
    cos, sin, idf, idb, wedge = host_attn_consts(T)
    base = {
        "wq": np.ascontiguousarray(Wq), "wk": np.ascontiguousarray(Wk),
        "wv": np.ascontiguousarray(Wv), "wo": np.ascontiguousarray(Wo),
        "qnw": np.tile(np.asarray(qnw, np.float32), 8)[None, :],
        "knw": np.tile(np.asarray(knw, np.float32), 8)[None, :],
        "cosr": cos.astype(BF), "sinr": sin.astype(BF), "idf": idf, "idb": idb, "wedge": wedge,
        "cval": np.array([[LN_EPS, RMS_EPS, 1.0, 0, 0, 0, 0, 0]], dtype=np.float32),
    }
    base = {pfx + k: v for k, v in base.items()}
    in_maps = []
    for b in range(x_full.shape[0]):
        m = dict(base)
        if with_x:
            m[pfx + "x"] = np.ascontiguousarray(x_full[b])
        in_maps.append(m)
    return in_maps


def run_l0_attn(x_full, Wq, Wk, Wv, Wo, qnw, knw, T=4096, debug=False, raw=False):
    nc = build_l0_attn(T, debug=debug)
    in_maps = attn_in_maps(x_full, Wq, Wk, Wv, Wo, qnw, knw, T)
    res = run_bass_kernel_spmd(nc, in_maps, core_ids=list(range(len(in_maps))))
    if raw:
        return res
    return np.stack([np.asarray(r["xout"]) for r in res.results])



FI = 2816


def emit_quant(pr, src_ap, dst_ap, W, scr, deps, name=""):
    G = W // 16
    m_ = scr["m"][:, 0:G]
    s_ = scr["m"][:, G:2 * G]
    r_ = scr["m"][:, 2 * G:3 * G]
    s2_ = scr["m"][:, 3 * G:4 * G]
    ax = scr["ax"][:, 0:W]
    acc = scr["acc"][:, 0:W]
    ac2 = scr["ac2"][:, 0:W]
    sg = scr["sg"][:, 0:W]

    t_abs = pr.add("scalar", lambda s: s.activation(ax, src_ap, AF.Abs), deps=deps)
    t_sgn = pr.add("scalar", lambda s: s.activation(sg, src_ap, AF.Sign), deps=deps)

    def t_m_fn(v):
        pr.fence("vector", v, v.reduce_max(
            m_, src_ap.rearrange("p (g k) -> p g k", k=16), axis=AXX,
            apply_absolute_value=True))
        pr.fence("vector", v, v.tensor_scalar(s_, m_, 1.0 / 6.0, 1e-12,
                                              op0=ALU.mult, op1=ALU.max))
        pr.fence("vector", v, v.reciprocal(r_, s_))
        return v.tensor_copy(s2_, s_)
    t_m = pr.add("vector", t_m_fn, deps=deps)

    def t_q_fn(v):
        ii = []
        r3 = r_.rearrange("p (g o) -> p g o", o=1).to_broadcast([128, G, 16])
        ax3 = ax.rearrange("p (g k) -> p g k", k=16)
        ii.append(v.tensor_tensor(ax3, ax3, r3, ALU.mult))
        u = ax
        ii.append(v.tensor_scalar(acc, u, 0.25, None, op0=ALU.is_gt))
        ii.append(v.scalar_tensor_tensor(acc, u, 0.75, acc, op0=ALU.is_gt, op1=ALU.add))
        ii.append(v.scalar_tensor_tensor(acc, u, 1.25, acc, op0=ALU.is_gt, op1=ALU.add))
        ii.append(v.scalar_tensor_tensor(acc, u, 1.75, acc, op0=ALU.is_gt, op1=ALU.add))
        ii.append(v.tensor_scalar(ac2, u, 2.5, None, op0=ALU.is_gt))
        ii.append(v.scalar_tensor_tensor(ac2, u, 3.5, ac2, op0=ALU.is_gt, op1=ALU.add))
        ii.append(v.scalar_tensor_tensor(acc, ac2, 2.0, acc, op0=ALU.mult, op1=ALU.add))
        ii.append(v.tensor_scalar(ac2, u, 5.0, 4.0, op0=ALU.is_gt, op1=ALU.mult))
        ii.append(v.tensor_tensor(acc, acc, ac2, ALU.add))
        ii.append(v.tensor_scalar(acc, acc, 0.5, None, op0=ALU.mult))
        ii.append(v.tensor_tensor(acc, acc, sg, ALU.mult))
        s23 = s2_.rearrange("p (g o) -> p g o", o=1).to_broadcast([128, G, 16])
        acc3 = acc.rearrange("p (g k) -> p g k", k=16)
        ii.append(v.tensor_tensor(dst_ap.rearrange("p (g k) -> p g k", k=16),
                                  acc3, s23, ALU.mult))
        return ii
    return pr.add("vector", t_q_fn, deps=[t_abs, t_sgn, t_m])


def build_l0_ffn(T=4096, debug=False, fu=None):
    NTT = T // 128
    FCH = [(i * 512, min(512, FI - i * 512)) for i in range((FI + 511) // 512)]
    pfx = fu["pfx"] if fu else ""
    nc = fu["nc"] if fu else bass.Bass("TRN2", target_bir_lowering=False)
    DT = lambda name, *a, **k: nc.dram_tensor(pfx + name, *a, **k)

    if fu:
        x_in, xout = fu["x_in"], fu["x_out"]
    else:
        x_in = DT("x", [T, D], F32, kind="ExternalInput").ap()
        xout = DT("xout", [T, D], F32, kind="ExternalOutput").ap()
    wg_in = DT("wg", [FI, D], F32, kind="ExternalInput").ap()
    wu_in = DT("wu", [FI, D], F32, kind="ExternalInput").ap()
    wd_in = DT("wd", [D, FI], F32, kind="ExternalInput").ap()
    idb_in = DT("idb", [128, 128], BF16, kind="ExternalInput").ap()
    cval_in = DT("cval", [1, 8], F32, kind="ExternalInput").ap()
    dbg = dict(kind="ExternalOutput") if debug else {}
    xln_h = DT("xln_h", [T, D], F32, **dbg).ap()
    xq_h = DT("xq_h", [T, D], BF16, **dbg).ap() if debug else None
    wgu_h = DT("wgu_h", [128, 8 * 2 * FI], BF16, **dbg).ap() if debug else None

    with ExitStack() as ctx:
        S = lambda name, shape, dt: ctx.enter_context(nc.sbuf_tensor(pfx + name, shape, dt))
        idb = S("idb_s", [128, 128], BF16)
        cvals = S("cvals", [128, 8], F32)
        wguT = S("wguT", [128, 8 * 2 * FI], BF16)
        wdT = S("wdT", [128, 22 * D], BF16)
        wraw = S("wraw", [128, 1408], F32)
        wqrow = S("wqrow", [128, 1408], BF16)
        xbuf = S("xbuf", [128, D], F32)
        xln = S("xln", [128, D], F32)
        xln2 = S("xln2", [128, D], F32)
        sq1 = S("sq1", [128, D], F32)
        stat = S("stat", [128, 16], F32)
        xq = S("xq", [128, D], BF16)
        xqT = S("xqT", [128, D], BF16)
        hbuf = S("hbuf", [128, 512], F32)
        hq = S("hq", [128, 512], BF16)
        hqT = S("hqT", [128, 512], BF16)
        obuf = S("obuf", [128, D], F32)
        scr = {
            "ax": S("q_ax", [128, 1408], F32),
            "acc": S("q_acc", [128, 1408], F32),
            "ac2": S("q_ac2", [128, 1408], F32),
            "sg": S("q_sg", [128, 1408], F32),
            "m": S("q_m", [128, 4 * 88], F32),
        }

        P_ = lambda name, shape, dt: ctx.enter_context(nc.psum_tensor(pfx + name, shape, dt))
        ps_tr = P_("ps_tr", [128, 2048], BF16)
        ps_gu = P_("ps_gu", [128, 1024], F32)
        ps_dn = P_("ps_dn", [128, 1024], F32)

        pr = Prog(nc, fu["sem_ctx"] if fu else ctx, pfx)
        s_const = pr.stream("s_const", 1)
        s_wr = pr.stream("s_wr", 2)
        s_lx = pr.stream("s_lx", 2)
        s_sx = pr.stream("s_sx", 2)
        s_go = pr.stream("s_go", 2)
        blk = fu["blk"] if fu else ctx.enter_context(nc.Block())

        CONST = pr.add("sync", lambda sy: [
            sy.dma_start(idb[:], idb_in[:]),
            sy.dma_start(cvals[:], cval_in.to_broadcast([128, 8]))],
            ndma=2, stream=s_const)

        prev_q = None
        prev_ev = [None, None]
        pi = 0

        def wchunk(w_in, r0, nr, w_cols0, w_ncols, dests):
            nonlocal prev_q, prev_ev, pi
            half = (pi % 2) * 1024
            ldw = pr.add("sync", lambda sy: sy.dma_start(
                wraw[:, 0:w_ncols], w_in[r0:r0 + nr, w_cols0:w_cols0 + w_ncols]),
                ndma=1, stream=s_wr, deps=[prev_q])
            tq = emit_quant(pr, wraw[:, 0:w_ncols], wqrow[:, 0:w_ncols], w_ncols,
                            scr, deps=[ldw, prev_ev[0], prev_ev[1]])
            prev_q = tq
            nb = w_ncols // 128
            done = []
            for fb in range(0, nb, 8):
                nfs = min(8, nb - fb)
                ttr = pr.add("tensor", lambda t, half=half, fb=fb, nfs=nfs: [
                    t.transpose(ps_tr[:, half + i * 128:half + (i + 1) * 128],
                                wqrow[:, (fb + i) * 128:(fb + i + 1) * 128], idb[:])
                    for i in range(nfs)],
                    deps=[tq, CONST, done[-1] if done else prev_ev[pi % 2]])
                ev = pr.add("vector", lambda v, half=half, fb=fb, nfs=nfs,
                            dests=dests: [
                    v.tensor_copy(dests[fb + i][0][:, dests[fb + i][1]:
                                                   dests[fb + i][1] + 128],
                                  ps_tr[:, half + i * 128:half + (i + 1) * 128])
                    for i in range(nfs)], deps=[ttr])
                done.append(ev)
            prev_ev[pi % 2] = done[-1]
            pi += 1

        for which, w_in in [(0, wg_in), (1, wu_in)]:
            for rt in range(FI // 128):
                r0 = rt * 128
                fc, off = r0 // 512, r0 % 512
                fcw = min(512, FI - fc * 512)
                dests = [(wguT, dc * 2 * FI + fc * 1024 + which * fcw + off)
                         for dc in range(8)]
                wchunk(w_in, r0, 128, 0, D, dests)
        for rt in range(8):
            for ch in range(2):
                dests = [(wdT, (ch * 11 + i) * D + rt * 128) for i in range(11)]
                wchunk(wd_in, rt * 128, 128, ch * 1408, 1408, dests)
        if debug:
            pr.add("gpsimd", lambda g_: g_.dma_start(wgu_h[:, :], wguT[:, :]),
                   ndma=1, stream=s_sx, deps=pr.barrier_deps())
        wdone = pr.barrier_deps()

        p1_ln = [None] * NTT
        p1_vr = [None] * NTT
        p1_sx = [None] * NTT
        for tt in range(NTT):
            ld = pr.add("sync", lambda sy, tt=tt: sy.dma_start(
                xbuf[:], x_in[tt * 128:(tt + 1) * 128, :]), ndma=1, stream=s_lx,
                deps=[p1_ln[tt - 1] if tt >= 1 else None] + (wdone if tt == 0 else []))
            sq_t = pr.add("scalar", lambda s: s.activation(
                sq1[:], xbuf[:], AF.Square),
                deps=[ld, p1_vr[tt - 1] if tt >= 1 else None])

            def vr_fn(v):
                pr.fence("vector", v, v.reduce_sum(stat[:, 0:1], xbuf[:], axis=AXX))
                pr.fence("vector", v, v.reduce_sum(stat[:, 1:2], sq1[:],
                                                   axis=AXX))
                pr.fence("vector", v, v.tensor_scalar(stat[:, 2:3], stat[:, 0:1],
                                                      1.0 / D, None, op0=ALU.mult))
                pr.fence("vector", v, v.tensor_tensor(stat[:, 3:4], stat[:, 2:3],
                                                      stat[:, 2:3], ALU.mult))
                return v.scalar_tensor_tensor(stat[:, 4:5], stat[:, 1:2], 1.0 / D,
                                              stat[:, 3:4], op0=ALU.mult,
                                              op1=ALU.subtract)
            vr = pr.add("vector", vr_fn, deps=[ld, sq_t])
            p1_vr[tt] = vr

            def lnp_fn(s):
                pr.fence("scalar", s, s.activation(stat[:, 5:6], stat[:, 4:5],
                                                   AF.Ln, bias=cvals[:, 0:1]))
                pr.fence("scalar", s, s.activation(stat[:, 6:7], stat[:, 5:6],
                                                   AF.Exp, scale=-0.5))
                pr.fence("scalar", s, s.activation(stat[:, 7:8], stat[:, 2:3],
                                                   AF.Copy, scale=stat[:, 6:7]))
                return s.mul(stat[:, 7:8], stat[:, 7:8], -1.0)
            lnp = pr.add("scalar", lnp_fn, deps=[vr, CONST])
            ln = pr.add("vector", lambda v: v.scalar_tensor_tensor(
                xln[:], xbuf[:], stat[:, 6:7], stat[:, 7:8].to_broadcast([128, D]),
                op0=ALU.mult, op1=ALU.add),
                deps=[lnp, p1_sx[tt - 1] if tt >= 1 else None])
            p1_ln[tt] = ln
            p1_sx[tt] = pr.add("gpsimd", lambda g_, tt=tt: g_.dma_start(
                xln_h[tt * 128:(tt + 1) * 128, :], xln[:]), ndma=1, stream=s_sx,
                deps=[ln])

        prev = {"q": None, "mm": None, "sil": None, "hq": None, "tph": None,
                "evh": None, "dn": None, "ro": None, "go": None, "tpx": None,
                "evx": None, "lnld": None}
        for tt in range(NTT):
            lnld = pr.add("sync", lambda sy, tt=tt: sy.dma_start(
                xln2[:], xln_h[tt * 128:(tt + 1) * 128, :]), ndma=1, stream=s_lx,
                deps=[p1_sx[tt], prev["q"]])
            prev["lnld"] = lnld
            tq = emit_quant(pr, xln2[:], xq[:], D, scr,
                            deps=[lnld, prev["tpx"]] +
                                 ([p1_vr[-1], p1_ln[-1]] if tt == 0 else []))
            prev["q"] = tq
            tpx = pr.add("tensor", lambda t: [
                t.transpose(ps_tr[:, dc * 128:(dc + 1) * 128],
                            xq[:, dc * 128:(dc + 1) * 128], idb[:])
                for dc in range(8)], deps=[tq, CONST, prev["evx"]])
            prev["tpx"] = tpx
            evx = pr.add("vector", lambda v: v.tensor_copy(xqT[:], ps_tr[:, 0:1024]),
                         deps=[tpx, prev["mm"]])
            prev["evx"] = evx
            if debug:
                pr.add("gpsimd", lambda g_, tt=tt: g_.dma_start(
                    xq_h[tt * 128:(tt + 1) * 128, :], xq[:]), ndma=1, stream=s_sx,
                    deps=[tq, prev["q"]])

            for fci, (f0, fn_) in enumerate(FCH):
                mm = pr.add("tensor", lambda t, f0=f0, fn_=fn_: [
                    t.matmul(ps_gu[:, w * 512:w * 512 + fn_],
                             xqT[:, dc * 128:(dc + 1) * 128],
                             wguT[:, dc * 2 * FI + (f0 // 512) * 1024 + w * fn_:
                                  dc * 2 * FI + (f0 // 512) * 1024 + (w + 1) * fn_],
                             start=(dc == 0), stop=(dc == 7), skip_group_check=True)
                    for w in range(2) for dc in range(8)],
                    deps=[evx, prev["sil"], prev["hq"]])
                prev["mm"] = mm
                sil = pr.add("scalar", lambda s, fn_=fn_: s.activation(
                    hbuf[:, 0:fn_], ps_gu[:, 0:fn_], AF.Silu), deps=[mm, prev["hq"]])
                prev["sil"] = sil
                hm = pr.add("vector", lambda v, fn_=fn_: v.tensor_tensor(
                    hbuf[:, 0:fn_], hbuf[:, 0:fn_], ps_gu[:, 512:512 + fn_],
                    ALU.mult), deps=[sil, mm])
                thq = emit_quant(pr, hbuf[:, 0:fn_], hq[:, 0:fn_], fn_, scr,
                                 deps=[hm, prev["tph"]])
                prev["hq"] = thq
                nfs = fn_ // 128
                tph = pr.add("tensor", lambda t, nfs=nfs: [
                    t.transpose(ps_tr[:, 1024 + i * 128:1024 + (i + 1) * 128],
                                hq[:, i * 128:(i + 1) * 128], idb[:])
                    for i in range(nfs)], deps=[thq, prev["evh"]])
                prev["tph"] = tph
                evh = pr.add("vector", lambda v, nfs=nfs: v.tensor_copy(
                    hqT[:, 0:nfs * 128], ps_tr[:, 1024:1024 + nfs * 128]),
                    deps=[tph, prev["dn"]])
                prev["evh"] = evh
                dn = pr.add("tensor", lambda t, f0=f0, nfs=nfs, fci=fci: [
                    t.matmul(ps_dn[:, h2 * 512:(h2 + 1) * 512],
                             hqT[:, i * 128:(i + 1) * 128],
                             wdT[:, (f0 // 128 + i) * D + h2 * 512:
                                  (f0 // 128 + i) * D + (h2 + 1) * 512],
                             start=(fci == 0 and i == 0),
                             stop=(fci == len(FCH) - 1 and i == nfs - 1
                                   and h2 == 1),
                             skip_group_check=True)
                    for i in range(nfs) for h2 in range(2)],
                    deps=[evh, prev["ro"]])
                prev["dn"] = dn
            lxr = pr.add("sync", lambda sy, tt=tt: sy.dma_start(
                xbuf[:], x_in[tt * 128:(tt + 1) * 128, :]), ndma=1, stream=s_go,
                deps=[prev["dn"]] + ([p1_ln[-1], p1_vr[-1],
                                      pr.tasks[pr.tasks.index(prev["dn"])]]
                                     if tt == 0 else []))
            ro = pr.add("vector", lambda v: v.tensor_tensor(
                obuf[:], xbuf[:], ps_dn[:, 0:1024], ALU.add),
                deps=[prev["dn"], lxr, prev["go"]])
            prev["ro"] = ro
            go = pr.add("sync", lambda sy, tt=tt: sy.dma_start(
                xout[tt * 128:(tt + 1) * 128, :], obuf[:]), ndma=1, stream=s_go,
                deps=[ro])
            prev["go"] = go

        if fu:
            return _fuse_tail(nc, pr, blk, fu, pfx)
        pr.add("gpsimd", lambda g_: g_.nop(), deps=pr.barrier_deps())
        pr.emit(blk)
    return nc


def ffn_in_maps(x_full, wg, wu, wd, pfx="", with_x=True):
    base = {
        "wg": np.ascontiguousarray(wg), "wu": np.ascontiguousarray(wu),
        "wd": np.ascontiguousarray(wd),
        "idb": np.eye(128, dtype=np.float32).astype(BF),
        "cval": np.array([[LN_EPS, RMS_EPS, 1.0, 0, 0, 0, 0, 0]], dtype=np.float32),
    }
    base = {pfx + k: v for k, v in base.items()}
    in_maps = []
    for b in range(x_full.shape[0]):
        m = dict(base)
        if with_x:
            m[pfx + "x"] = np.ascontiguousarray(x_full[b])
        in_maps.append(m)
    return in_maps


def run_l0_ffn(x_full, wg, wu, wd, T=4096, raw=False, debug=False):
    nc = build_l0_ffn(T, debug=debug)
    in_maps = ffn_in_maps(x_full, wg, wu, wd)
    res = run_bass_kernel_spmd(nc, in_maps, core_ids=list(range(len(in_maps))))
    if raw:
        return res
    return np.stack([np.asarray(r["xout"]) for r in res.results])



RN, NN = 64, 16


def build_l1_attn(T=4096, debug=False, fu=None):
    NTT = T // 128
    NCH = T // 512
    Q = 512
    pfx = fu["pfx"] if fu else ""
    nc = fu["nc"] if fu else bass.Bass("TRN2", target_bir_lowering=False)
    DT = lambda name, *a, **k: nc.dram_tensor(pfx + name, *a, **k)

    if fu:
        x_in, xout = fu["x_in"], fu["x_out"]
    else:
        x_in = DT("x", [T, D], F32, kind="ExternalInput").ap()
        xout = DT("xout", [T, D], F32, kind="ExternalOutput").ap()
    xpw_in = DT("xpw", [96, D], F32, kind="ExternalInput").ap()
    dtw_in = DT("dtw", [D, RN], F32, kind="ExternalInput").ap()
    dtb_in = DT("dtb", [D], F32, kind="ExternalInput").ap()
    alog_in = DT("alog", [D, NN], F32, kind="ExternalInput").ap()
    dpar_in = DT("dpar", [D], F32, kind="ExternalInput").ap()
    lna_in = DT("lna", [1, D], F32, kind="ExternalInput").ap()
    lnb_in = DT("lnb", [1, D], F32, kind="ExternalInput").ap()
    idb_in = DT("idb", [128, 128], BF16, kind="ExternalInput").ap()
    idf_in = DT("idf", [128, 128], F32, kind="ExternalInput").ap()
    cval_in = DT("cval", [1, 8], F32, kind="ExternalInput").ap()
    dbg = dict(kind="ExternalOutput") if debug else {}
    xln_h = DT("xln_h", [T, D], F32, **dbg).ap()
    dbc_h = DT("dbc_h", [32, T], F32, **dbg).ap()
    delta_h = DT("delta_h", [128, 8 * T], F32, **dbg).ap() if debug else None
    y_h = DT("y_h", [128, 8 * T], F32, **dbg).ap() if debug else None

    with ExitStack() as ctx:
        S = lambda name, shape, dt: ctx.enter_context(nc.sbuf_tensor(pfx + name, shape, dt))
        idb = S("idb_s", [128, 128], BF16)
        idf = S("idf_s", [128, 128], F32)
        cvals = S("cvals", [128, 8], F32)
        xpwT = S("xpwT", [128, 8 * 96], BF16)
        dtwT = S("dtwT", [64, 8 * 128], BF16)
        wstgb = S("wstgb", [128, 128], BF16)
        a_sb = S("a_sb", [128, 8 * NN], F32)
        db_sb = S("db_sb", [128, 8], F32)
        dp_sb = S("dp_sb", [128, 8], F32)
        lnar = S("lnar", [128, D], F32)
        lnbr = S("lnbr", [128, D], F32)
        xbuf = S("xbuf", [128, D], F32)
        xln = S("xln", [128, D], F32)
        sq1 = S("sq1", [128, D], F32)
        stat = S("stat", [128, 16], F32)
        xfT = S("xfT", [128, 8 * 512], F32)
        xbT = S("xbT", [128, 8 * 512], BF16)
        dbcB = S("dbcB", [64, 512], BF16)
        dbcS = S("dbcS", [32, 512], F32)
        delta = S("delta", [128, 8 * 512], F32)
        spl = S("spl", [128, 512], F32)
        spr = S("spr", [128, 512], F32)
        zc = S("zc", [128, 8 * 512], F32)
        dAb = S("dAb", [128, 8 * 512], F32)
        bxb = S("bxb", [128, 8 * 512], F32)
        hb = S("hb", [128, 8 * 512], F32)
        yac = S("yac", [128, 8 * 512], F32)
        yv = S("yv", [128, 512], F32)
        bmr = S("bmr", [128, 2 * 512], F32)
        cmr = S("cmr", [128, 2 * 512], F32)
        state = S("state", [128, 8 * NN], F32)
        ytok = S("ytok", [128, D], F32)
        obuf = S("obuf", [128, D], F32)

        P_ = lambda name, shape, dt: ctx.enter_context(nc.psum_tensor(pfx + name, shape, dt))
        ps_trf = P_("ps_trf", [128, 1024], F32)
        ps_trb = P_("ps_trb", [128, 128], BF16)
        ps_dbc = P_("ps_dbc", [96, 512], F32)
        ps_dt = P_("ps_dt", [128, 512], F32)

        pr = Prog(nc, fu["sem_ctx"] if fu else ctx, pfx)
        s_const = pr.stream("s_const", 1)
        s_w = pr.stream("s_w", 2)
        s_lx = pr.stream("s_lx", 2)
        s_sx = pr.stream("s_sx", 2)
        s_dbc = pr.stream("s_dbc", 2)
        s_bc = pr.stream("s_bc", 4)
        s_go = pr.stream("s_go", 2)
        blk = fu["blk"] if fu else ctx.enter_context(nc.Block())

        CONST = pr.add("sync", lambda sy: [
            sy.dma_start(idb[:], idb_in[:]),
            sy.dma_start(idf[:], idf_in[:]),
            sy.dma_start(cvals[:], cval_in.to_broadcast([128, 8])),
            sy.dma_start(a_sb[:].rearrange("p (dc n) -> p dc n", n=NN),
                         alog_in.rearrange("(dc p) n -> p dc n", p=128)),
            sy.dma_start(lnar[:], lna_in.to_broadcast([128, D])),
            sy.dma_start(lnbr[:], lnb_in.to_broadcast([128, D]))],
            ndma=6, stream=s_const)
        CONST2 = pr.add("gpsimd", lambda g_: [
            g_.dma_start(db_sb[:], dtb_in.rearrange("(dc p) -> p dc", p=128),
                         allow_slow_non_contiguous=True),
            g_.dma_start(dp_sb[:], dpar_in.rearrange("(dc p) -> p dc", p=128),
                         allow_slow_non_contiguous=True)],
            ndma=2, stream=s_bc)
        tA1 = pr.add("scalar", lambda s: s.activation(a_sb[:], a_sb[:], AF.Exp),
                     deps=[CONST])
        tA2 = pr.add("vector", lambda v: v.tensor_scalar(
            a_sb[:], a_sb[:], -1.0, None, op0=ALU.mult), deps=[tA1])

        prev_ev = None
        for dc in range(8):
            lw = pr.add("gpsimd", lambda g_, dc=dc: g_.dma_start(
                wstgb[0:96, :], xpw_in[:, dc * 128:(dc + 1) * 128]), ndma=1,
                stream=s_w, deps=[prev_ev])
            tt_ = pr.add("tensor", lambda t: t.transpose(
                ps_trb[:, 0:96], wstgb[0:96, :], idb[0:96, 0:96]),
                deps=[lw, CONST, prev_ev])
            ev = pr.add("vector", lambda v, dc=dc: v.tensor_copy(
                xpwT[:, dc * 96:(dc + 1) * 96], ps_trb[:, 0:96]), deps=[tt_])
            prev_ev = ev
        for dc in range(8):
            lw = pr.add("gpsimd", lambda g_, dc=dc: g_.dma_start(
                wstgb[:, 0:RN], dtw_in[dc * 128:(dc + 1) * 128, :]), ndma=1,
                stream=s_w, deps=[prev_ev])
            tt_ = pr.add("tensor", lambda t: t.transpose(
                ps_trb[0:RN, 0:128], wstgb[:, 0:RN], idb[:]),
                deps=[lw, CONST, prev_ev])
            ev = pr.add("vector", lambda v, dc=dc: v.tensor_copy(
                dtwT[0:RN, dc * 128:(dc + 1) * 128], ps_trb[0:RN, 0:128]),
                deps=[tt_])
            prev_ev = ev
        wdone = pr.barrier_deps()

        p1_ln = [None] * NTT
        p1_vr = [None] * NTT
        p1_sx = [None] * NTT
        for tt in range(NTT):
            ld = pr.add("sync", lambda sy, tt=tt: sy.dma_start(
                xbuf[:], x_in[tt * 128:(tt + 1) * 128, :]), ndma=1, stream=s_lx,
                deps=[p1_ln[tt - 1] if tt >= 1 else None])
            sq_t = pr.add("scalar", lambda s: s.activation(
                sq1[:], xbuf[:], AF.Square),
                deps=[ld, p1_vr[tt - 1] if tt >= 1 else None] +
                     (wdone if tt == 0 else []))

            def vr_fn(v):
                pr.fence("vector", v, v.reduce_sum(stat[:, 0:1], xbuf[:], axis=AXX))
                pr.fence("vector", v, v.reduce_sum(stat[:, 1:2], sq1[:], axis=AXX))
                pr.fence("vector", v, v.tensor_scalar(stat[:, 2:3], stat[:, 0:1],
                                                      1.0 / D, None, op0=ALU.mult))
                pr.fence("vector", v, v.tensor_tensor(stat[:, 3:4], stat[:, 2:3],
                                                      stat[:, 2:3], ALU.mult))
                return v.scalar_tensor_tensor(stat[:, 4:5], stat[:, 1:2], 1.0 / D,
                                              stat[:, 3:4], op0=ALU.mult,
                                              op1=ALU.subtract)
            vr = pr.add("vector", vr_fn, deps=[ld, sq_t])
            p1_vr[tt] = vr

            def lnp_fn(s):
                pr.fence("scalar", s, s.activation(stat[:, 5:6], stat[:, 4:5],
                                                   AF.Ln, bias=cvals[:, 0:1]))
                pr.fence("scalar", s, s.activation(stat[:, 6:7], stat[:, 5:6],
                                                   AF.Exp, scale=-0.5))
                pr.fence("scalar", s, s.activation(stat[:, 7:8], stat[:, 2:3],
                                                   AF.Copy, scale=stat[:, 6:7]))
                return s.mul(stat[:, 7:8], stat[:, 7:8], -1.0)
            lnp = pr.add("scalar", lnp_fn, deps=[vr, CONST])
            ln = pr.add("vector", lambda v: v.scalar_tensor_tensor(
                xln[:], xbuf[:], stat[:, 6:7], stat[:, 7:8].to_broadcast([128, D]),
                op0=ALU.mult, op1=ALU.add),
                deps=[lnp, p1_sx[tt - 1] if tt >= 1 else None])
            p1_ln[tt] = ln
            p1_sx[tt] = pr.add("gpsimd", lambda g_, tt=tt: g_.dma_start(
                xln_h[tt * 128:(tt + 1) * 128, :], xln[:]), ndma=1, stream=s_sx,
                deps=[ln])

        st0 = pr.add("vector", lambda v: v.memset(state[:], 0.0), deps=[])

        pv = {k: None for k in ["tpq", "evx", "mmdbc", "evb", "evs", "gdbc",
                                "mmdt", "spv", "zt", "dA", "bx", "scn", "yn",
                                "stu", "yD", "lxq", "ro", "go", "lnv", "sqy"]}
        pv["stu"] = st0
        for c in range(NCH):
            c0 = c * 512
            evx_list = []
            for s4 in range(4):
                tt = c * 4 + s4
                lxq = pr.add("sync", lambda sy, tt=tt: sy.dma_start(
                    xln[:], xln_h[tt * 128:(tt + 1) * 128, :]), ndma=1, stream=s_lx,
                    deps=[p1_sx[tt], pv["tpq"]] +
                         ([p1_ln[-1], p1_vr[-1], p1_sx[-1], p1_sx[-2]]
                          if c == 0 and s4 == 0 else []))
                pv["lxq"] = lxq
                tpq = pr.add("tensor", lambda t: [
                    t.transpose(ps_trf[:, i * 128:(i + 1) * 128],
                                xln[:, i * 128:(i + 1) * 128], idf[:])
                    for i in range(8)],
                    deps=[lxq, CONST, pv["evx"], pv.get("evy")])
                pv["tpq"] = tpq

                def evx_fn(v, s4=s4):
                    ii = []
                    for i in range(8):
                        ii.append(v.tensor_copy(
                            xfT[:, i * 512 + s4 * 128:i * 512 + (s4 + 1) * 128],
                            ps_trf[:, i * 128:(i + 1) * 128]))
                        ii.append(v.tensor_copy(
                            xbT[:, i * 512 + s4 * 128:i * 512 + (s4 + 1) * 128],
                            ps_trf[:, i * 128:(i + 1) * 128]))
                    return ii
                evx = pr.add("vector", evx_fn,
                             deps=[tpq, pv["zt"], pv["yD"], pv["mmdbc"]])
                pv["evx"] = evx
                evx_list.append(evx)
            mmdbc = pr.add("tensor", lambda t: [
                t.matmul(ps_dbc[0:96, :], xpwT[:, dc * 96:(dc + 1) * 96],
                         xbT[:, dc * 512:(dc + 1) * 512],
                         start=(dc == 0), stop=(dc == 7), skip_group_check=True)
                for dc in range(8)],
                deps=evx_list + [pv["evb"], pv["evs"]] + ([] if c else pr.barrier_deps()[:1]))
            pv["mmdbc"] = mmdbc
            evb = pr.add("vector", lambda v: v.tensor_copy(
                dbcB[0:64, :], ps_dbc[0:64, :]), deps=[mmdbc, pv["mmdt"]])
            pv["evb"] = evb
            evs = pr.add("vector", lambda v: v.tensor_copy(
                dbcS[0:32, :], ps_dbc[64:96, :]), deps=[mmdbc, pv["gdbc"]])
            pv["evs"] = evs
            gdbc = pr.add("gpsimd", lambda g_, c0=c0: g_.dma_start(
                dbc_h[:, c0:c0 + 512], dbcS[0:32, :]), ndma=1, stream=s_dbc,
                deps=[evs])
            pv["gdbc"] = gdbc
            spv_l = None
            for dc in range(8):
                mmdt = pr.add("tensor", lambda t, dc=dc: t.matmul(
                    ps_dt[:], dtwT[0:RN, dc * 128:(dc + 1) * 128], dbcB[0:RN, :],
                    start=True, stop=True, skip_group_check=True),
                    deps=[evb, spv_l])
                pv["mmdt"] = mmdt

                def sp_fn(s, dc=dc):
                    s.activation(spl[:], ps_dt[:], AF.Abs, bias=db_sb[:, dc:dc + 1])
                    s.activation(spl[:], spl[:], AF.Exp, scale=-1.0)
                    s.activation(spl[:], spl[:], AF.Ln, bias=cvals[:, 2:3])
                    return s.activation(spr[:], ps_dt[:], AF.Relu,
                                        bias=db_sb[:, dc:dc + 1])
                sp = pr.add("scalar", sp_fn, deps=[mmdt, CONST2, spv_l])
                spv = pr.add("vector", lambda v, dc=dc: v.tensor_tensor(
                    delta[:, dc * 512:(dc + 1) * 512], spl[:], spr[:], ALU.add),
                    deps=[sp])
                spv_l = spv
            pv["spv"] = spv_l
            zt = pr.add("vector", lambda v: v.tensor_tensor(
                zc[:], delta[:], xfT[:], ALU.mult), deps=[spv_l, pv["bx"]])
            pv["zt"] = zt
            for n in range(NN):
                nb = n % 2
                bm = pr.add("gpsimd", lambda g_, n=n, c0=c0, nb=nb: [
                    g_.dma_start(bmr[:, nb * 512:(nb + 1) * 512],
                                 dbc_h[n:n + 1, c0:c0 + 512].to_broadcast(
                                     [128, 512])),
                    g_.dma_start(cmr[:, nb * 512:(nb + 1) * 512],
                                 dbc_h[16 + n:17 + n, c0:c0 + 512].to_broadcast(
                                     [128, 512]))],
                    ndma=2, stream=s_bc,
                    deps=[gdbc, pv["bx"], pv["yn"]])

                def dA_fn(s, n=n):
                    return [s.activation(dAb[:, dc * 512:(dc + 1) * 512],
                                         delta[:, dc * 512:(dc + 1) * 512],
                                         AF.Exp, scale=a_sb[:, dc * NN + n:
                                                           dc * NN + n + 1])
                            for dc in range(8)]
                dA = pr.add("scalar", dA_fn, deps=[pv["spv"], pv["scn"], tA2])
                pv["dA"] = dA

                def bx_fn(v, nb=nb):
                    b3 = bmr[:, nb * 512:(nb + 1) * 512]
                    return [v.tensor_tensor(bxb[:, dc * 512:(dc + 1) * 512],
                                            zc[:, dc * 512:(dc + 1) * 512],
                                            b3, ALU.mult)
                            for dc in range(8)]
                bx = pr.add("vector", bx_fn, deps=[zt, bm, pv["scn"]])
                pv["bx"] = bx

                def scn_fn(v, n=n):
                    for dc in range(7):
                        v.tensor_tensor_scan(
                            hb[:, dc * 512:(dc + 1) * 512],
                            dAb[:, dc * 512:(dc + 1) * 512],
                            bxb[:, dc * 512:(dc + 1) * 512],
                            state[:, dc * NN + n:dc * NN + n + 1],
                            ALU.mult, ALU.add)
                    pr.fence("vector", v, v.tensor_tensor_scan(
                        hb[:, 7 * 512:8 * 512], dAb[:, 7 * 512:8 * 512],
                        bxb[:, 7 * 512:8 * 512],
                        state[:, 7 * NN + n:7 * NN + n + 1],
                        ALU.mult, ALU.add))
                    return v.nop()
                scn = pr.add("vector", scn_fn,
                             deps=[dA, bx, pv["yn"], pv["stu"]])
                pv["scn"] = scn

                def stu_fn(v, n=n):
                    for dc in range(7):
                        v.tensor_copy(state[:, dc * NN + n:dc * NN + n + 1],
                                      hb[:, dc * 512 + 511:dc * 512 + 512])
                    pr.fence("vector", v, v.tensor_copy(
                        state[:, 7 * NN + n:7 * NN + n + 1],
                        hb[:, 7 * 512 + 511:7 * 512 + 512]))
                    return v.engine_nop() if hasattr(v, "engine_nop") else v.nop()
                stu = pr.add("vector", stu_fn, deps=[scn])
                pv["stu"] = stu

                def yn_fn(v, n=n, nb=nb):
                    c3 = cmr[:, nb * 512:(nb + 1) * 512]
                    ii = []
                    for dc in range(8):
                        sl = slice(dc * 512, (dc + 1) * 512)
                        if n == 0:
                            ii.append(v.tensor_tensor(yac[:, sl], hb[:, sl], c3,
                                                      ALU.mult))
                        else:
                            ii.append(v.tensor_tensor(yv[:], hb[:, sl], c3,
                                                      ALU.mult))
                            ii.append(v.tensor_tensor(yac[:, sl], yac[:, sl],
                                                      yv[:], ALU.add))
                    return ii
                yn = pr.add("vector", yn_fn, deps=[scn, bm])
                pv["yn"] = yn
            yD = pr.add("vector", lambda v: [
                v.scalar_tensor_tensor(yac[:, dc * 512:(dc + 1) * 512],
                                       xfT[:, dc * 512:(dc + 1) * 512],
                                       dp_sb[:, dc:dc + 1],
                                       yac[:, dc * 512:(dc + 1) * 512],
                                       op0=ALU.mult, op1=ALU.add)
                for dc in range(8)], deps=[pv["yn"], CONST2])
            pv["yD"] = yD
            if debug:
                pr.add("gpsimd", lambda g_, c0=c0: g_.dma_start(
                    delta_h[:, :].rearrange("p (dc t) -> p dc t", t=T)[:, :, c0:c0 + 512],
                    delta[:].rearrange("p (dc t) -> p dc t", t=512)),
                    ndma=1, stream=s_sx, deps=[pv["spv"]])
                pr.add("gpsimd", lambda g_, c0=c0: g_.dma_start(
                    y_h[:, :].rearrange("p (dc t) -> p dc t", t=T)[:, :, c0:c0 + 512],
                    yac[:].rearrange("p (dc t) -> p dc t", t=512)),
                    ndma=1, stream=s_sx, deps=[yD])
            for s4 in range(4):
                tt = c * 4 + s4
                tpy = pr.add("tensor", lambda t, s4=s4: [
                    t.transpose(ps_trf[:, i * 128:(i + 1) * 128],
                                yac[:, i * 512 + s4 * 128:i * 512 + (s4 + 1) * 128],
                                idf[:]) for i in range(8)],
                    deps=[yD, pv["lnv"], pv["evx"], pv.get("evy")])
                pv["tpq"] = tpy
                evy = pr.add("vector", lambda v: v.tensor_copy(
                    ytok[:], ps_trf[:, 0:1024]), deps=[tpy, pv["ro"]])
                pv["evy"] = evy
                sqy = pr.add("scalar", lambda s: s.activation(
                    sq1[:], ytok[:], AF.Square), deps=[evy, pv["lnv"]])

                def yst_fn(v):
                    pr.fence("vector", v, v.reduce_sum(stat[:, 0:1], ytok[:],
                                                       axis=AXX))
                    pr.fence("vector", v, v.reduce_sum(stat[:, 1:2], sq1[:],
                                                       axis=AXX))
                    pr.fence("vector", v, v.tensor_scalar(
                        stat[:, 2:3], stat[:, 0:1], 1.0 / D, None, op0=ALU.mult))
                    pr.fence("vector", v, v.tensor_tensor(
                        stat[:, 3:4], stat[:, 2:3], stat[:, 0:1], ALU.mult))
                    pr.fence("vector", v, v.scalar_tensor_tensor(
                        stat[:, 4:5], stat[:, 1:2], 1.0, stat[:, 3:4],
                        op0=ALU.mult, op1=ALU.subtract))
                    return v.tensor_scalar(stat[:, 4:5], stat[:, 4:5],
                                           1.0 / (D - 1), None, op0=ALU.mult)
                yst = pr.add("vector", yst_fn, deps=[evy, sqy])

                def ylnp_fn(s):
                    pr.fence("scalar", s, s.activation(stat[:, 5:6], stat[:, 4:5],
                                                       AF.Ln))
                    return s.activation(stat[:, 6:7], stat[:, 5:6],
                                        AF.Exp, scale=0.5)
                ylnp = pr.add("scalar", ylnp_fn, deps=[yst])

                def yrec_fn(v):
                    pr.fence("vector", v, v.tensor_scalar(
                        stat[:, 6:7], stat[:, 6:7], 1e-12, None, op0=ALU.add))
                    pr.fence("vector", v, v.reciprocal(stat[:, 7:8], stat[:, 6:7]))
                    pr.fence("vector", v, v.tensor_tensor(
                        stat[:, 8:9], stat[:, 2:3], stat[:, 7:8], ALU.mult))
                    pr.fence("vector", v, v.tensor_scalar(
                        stat[:, 8:9], stat[:, 8:9], -1.0, None, op0=ALU.mult))
                    return v.nop()
                yrec = pr.add("vector", yrec_fn, deps=[ylnp])
                lnv = pr.add("vector", lambda v: v.scalar_tensor_tensor(
                    ytok[:], ytok[:], stat[:, 7:8],
                    stat[:, 8:9].to_broadcast([128, D]),
                    op0=ALU.mult, op1=ALU.add), deps=[yrec])
                pv["lnv"] = lnv
                lxr = pr.add("sync", lambda sy, tt=tt: sy.dma_start(
                    xbuf[:], x_in[tt * 128:(tt + 1) * 128, :]), ndma=1,
                    stream=s_go, deps=[pv["ro"], lnv] +
                    ([p1_ln[-1], p1_vr[-1]] if c == 0 and s4 == 0 else []))

                def out_fn(v):
                    ii = []
                    ii.append(v.tensor_tensor(obuf[:], ytok[:], lnar[:], ALU.mult))
                    ii.append(v.tensor_tensor(obuf[:], obuf[:], lnbr[:], ALU.add))
                    ii.append(v.tensor_tensor(obuf[:], obuf[:], xbuf[:], ALU.add))
                    return ii
                ro = pr.add("vector", out_fn, deps=[lnv, lxr, pv["go"], CONST])
                pv["ro"] = ro
                go = pr.add("sync", lambda sy, tt=tt: sy.dma_start(
                    xout[tt * 128:(tt + 1) * 128, :], obuf[:]), ndma=1,
                    stream=s_go, deps=[ro])
                pv["go"] = go

        if fu:
            return _fuse_tail(nc, pr, blk, fu, pfx)
        pr.add("gpsimd", lambda g_: g_.nop(), deps=pr.barrier_deps())
        pr.emit(blk)
    return nc


def ssm_in_maps(x_full, xpw, dtw, dtb, alog, dpar, lna, lnb, pfx="", with_x=True):
    base = {
        "xpw": np.ascontiguousarray(xpw), "dtw": np.ascontiguousarray(dtw),
        "dtb": np.ascontiguousarray(dtb), "alog": np.ascontiguousarray(alog),
        "dpar": np.ascontiguousarray(dpar),
        "lna": np.ascontiguousarray(lna)[None, :],
        "lnb": np.ascontiguousarray(lnb)[None, :],
        "idb": np.eye(128, dtype=np.float32).astype(BF),
        "idf": np.eye(128, dtype=np.float32),
        "cval": np.array([[LN_EPS, RMS_EPS, 1.0, 0, 0, 0, 0, 0]], dtype=np.float32),
    }
    base = {pfx + k: v for k, v in base.items()}
    in_maps = []
    for b in range(x_full.shape[0]):
        m = dict(base)
        if with_x:
            m[pfx + "x"] = np.ascontiguousarray(x_full[b])
        in_maps.append(m)
    return in_maps


def run_l1_attn(x_full, xpw, dtw, dtb, alog, dpar, lna, lnb, T=4096, raw=False, debug=False):
    nc = build_l1_attn(T, debug=debug)
    in_maps = ssm_in_maps(x_full, xpw, dtw, dtb, alog, dpar, lna, lnb)
    res = run_bass_kernel_spmd(nc, in_maps, core_ids=list(range(len(in_maps))))
    if raw:
        return res
    return np.stack([np.asarray(r["xout"]) for r in res.results])


NEXP, KTOP, DFF = 8, 2048, 4096


def build_l1_ffn(T=4096, NC=8, fu=None):
    NTT = T // 128
    TOT = T * NC
    GPP = TOT // 128
    NJT = KTOP // 128
    pfx = fu["pfx"] if fu else ""
    nc = fu["nc"] if fu else bass.Bass("TRN2", target_bir_lowering=False)
    DT = lambda name, *a, **k: nc.dram_tensor(pfx + name, *a, **k)

    if fu:
        x_in, xout = fu["x_in"], fu["x_out"]
    else:
        x_in = DT("x", [T, D], F32, kind="ExternalInput").ap()
        xout = DT("xout", [T, D], F32, kind="ExternalOutput").ap()
    rw_in = DT("rw", [D, NEXP], F32, kind="ExternalInput").ap()
    rb_in = DT("rb", [1, NEXP], F32, kind="ExternalInput").ap()
    w1_in = DT("w1", [D, DFF], F32, kind="ExternalInput").ap()
    b1_in = DT("b1", [1, DFF], F32, kind="ExternalInput").ap()
    w2_in = DT("w2", [DFF, D], F32, kind="ExternalInput").ap()
    b2_in = DT("b2", [1, D], F32, kind="ExternalInput").ap()
    idb_in = DT("idb", [128, 128], BF16, kind="ExternalInput").ap()
    ltri_in = DT("ltri", [128, 128], F32, kind="ExternalInput").ap()
    siota_in = DT("siota", [1, 64], F32, kind="ExternalInput").ap()
    piota_in = DT("piota", [128, 1], F32, kind="ExternalInput").ap()
    cval_in = DT("cval", [1, 8], F32, kind="ExternalInput").ap()
    coreid_in = DT("coreid", [1, 1], F32, kind="ExternalInput").ap()

    xlnb_loc = DT("xlnb_loc", [T, D], BF16).ap()
    xlnb_full = DT("xlnb_full", [TOT, D], BF16).ap()
    lg_loc = DT("lg_loc", [T, NEXP], F32).ap()
    lg_full = DT("lg_full", [TOT, NEXP], F32).ap()
    sc_h = DT("sc_h", [TOT, NEXP], F32).ap()
    idx_h = DT("idx_h", [KTOP + 128], I32).ap()
    cont_loc = DT("cont_loc", [KTOP + 1, D], BF16).ap()
    cont_full = DT("cont_full", [NC * (KTOP + 1), D], BF16).ap()

    with ExitStack() as ctx:
        S = lambda name, shape, dt: ctx.enter_context(nc.sbuf_tensor(pfx + name, shape, dt))
        idb = S("idb_s", [128, 128], BF16)
        ltri = S("ltri_s", [128, 128], F32)
        siota = S("siota_s", [128, 64], F32)
        piota = S("piota_s", [128, 1], F32)
        cvals = S("cvals", [128, 8], F32)
        brep = S("brep", [128, NEXP], F32)
        coreid = S("coreid_s", [128, 1], F32)
        rwb = S("rwb", [128, 8 * NEXP], BF16)
        inde = S("inde", [128, NEXP], F32)
        xbuf = S("xbuf", [128, D], F32)
        xlnbf = S("xlnbf", [128, D], BF16)
        stat = S("stat", [128, 16], F32)
        lgt = S("lgt", [128, NEXP], F32)
        sc_sb = S("sc_sb", [128, GPP * NEXP], F32)
        junk = S("junk", [128, GPP], F32)
        bst = S("bst", [128, 64], F32)
        csml = S("csml", [1, NEXP], F32)
        rc = S("rc", [128, GPP], F32)
        onesb = S("onesb", [128, max(GPP, 128)], F32)
        mask = S("mask", [128, GPP], F32)
        scown = S("scown", [128, GPP], F32)
        hiown = S("hiown", [128, 1], F32)
        pos = S("pos", [128, 64], F32)
        gid = S("gid", [128, 64], F32)
        slot = S("slot", [128, 64], F32)
        gidi = S("gidi", [128, 64], I32)
        sloti = S("sloti", [128, 64], I32)
        bsti = S("bsti", [128, 16], I32)
        ocol = S("ocol", [128, 1], F32)
        idxpre = S("idxpre", [128, NJT], F32)
        idxprei = S("idxprei", [128, NJT], I32)
        glist = S("glist", [128, NJT], F32)
        g8 = S("g8", [128, NEXP], F32)
        xg = S("xg", [128, 2 * D], BF16)
        xgT = S("xgT", [128, D], BF16)
        w1b = S("w1b", [128, 8 * DFF], BF16)
        w2b = S("w2b", [128, 32 * D], BF16)
        b1rep = S("b1rep", [128, 1024], F32)
        b2rep = S("b2rep", [128, D], BF16)
        h1b = S("h1b", [128, DFF], BF16)
        h1T = S("h1T", [128, DFF], BF16)
        cont = S("cont", [128, D], BF16)
        srcr = S("srcr", [128, GPP], F32)
        srciw = S("srciw", [128, NEXP * GPP], F32)
        vcol8 = S("vcol8", [128, NEXP], F32)
        icol = S("icol", [128, 1], I32)
        gbuf = S("gbuf", [128, 4 * D], BF16)
        icolall = S("icolall", [128, NEXP * NTT], I32)
        obuf = S("obuf", [128, 2 * D], F32)
        vcol = S("vcol", [128, 2], F32)

        P_ = lambda name, shape, dt: ctx.enter_context(nc.psum_tensor(pfx + name, shape, dt))
        ps_tr = P_("ps_tr", [128, 2048], BF16)
        ps_a = P_("ps_a", [128, 512], F32)
        ps_b = P_("ps_b", [128, 1024], F32)
        ps_sm = P_("ps_sm", [128, 8], F32)

        pr = Prog(nc, fu["sem_ctx"] if fu else ctx, pfx)
        s_const = pr.stream("s_const", 1)
        s_lx = pr.stream("s_lx", 2)
        s_sx = pr.stream("s_sx", 2)
        s_sl = pr.stream("s_sl", 2)
        s_w = pr.stream("s_w", 1)
        s_ix = pr.stream("s_ix", 4)
        s_go = pr.stream("s_go", 2)
        blk = fu["blk"] if fu else ctx.enter_context(nc.Block())

        CONST = pr.add("sync", lambda sy: [
            sy.dma_start(idb[:], idb_in[:]),
            sy.dma_start(ltri[:], ltri_in[:]),
            sy.dma_start(siota[:], siota_in.to_broadcast([128, 64])),
            sy.dma_start(piota[:], piota_in[:]),
            sy.dma_start(cvals[:], cval_in.to_broadcast([128, 8])),
            sy.dma_start(brep[:], rb_in.to_broadcast([128, NEXP])),
            sy.dma_start(coreid[:], coreid_in.to_broadcast([128, 1]))],
            ndma=7, stream=s_const)
        WLOAD = pr.add("gpsimd", lambda g_: [
            g_.dma_start(rwb[:].rearrange("p (dc e) -> p dc e", e=NEXP),
                         rw_in.rearrange("(dc p) e -> p dc e", p=128))] + [
            g_.dma_start(w1b[:, dc * DFF:(dc + 1) * DFF],
                         w1_in[dc * 128:(dc + 1) * 128, :]) for dc in range(8)] + [
            g_.dma_start(w2b[:, fs * D:(fs + 1) * D],
                         w2_in[fs * 128:(fs + 1) * 128, :]) for fs in range(32)] + [
            g_.dma_start(b2rep[:], b2_in.to_broadcast([128, D]))],
            ndma=42, stream=s_w)
        indt = pr.add("vector", lambda v: v.scalar_tensor_tensor(
            inde[:], siota[:, 0:8], 1.0, coreid[:].to_broadcast([128, 8]),
            op0=ALU.mult, op1=ALU.is_equal), deps=[CONST])

        p1_ln = [None] * NTT
        p1_vr = [None] * NTT
        p1_sx = [None] * NTT
        for tt in range(NTT):
            ld = pr.add("sync", lambda sy, tt=tt: sy.dma_start(
                xbuf[:], x_in[tt * 128:(tt + 1) * 128, :]), ndma=1, stream=s_lx,
                deps=[p1_ln[tt - 1] if tt >= 1 else None])
            sq_t = pr.add("scalar", lambda s: s.activation(
                b1rep[:, 0:D], xbuf[:], AF.Square),
                deps=[ld, p1_vr[tt - 1] if tt >= 1 else None])

            def vr_fn(v):
                pr.fence("vector", v, v.reduce_sum(stat[:, 0:1], xbuf[:], axis=AXX))
                pr.fence("vector", v, v.reduce_sum(stat[:, 1:2], b1rep[:, 0:D], axis=AXX))
                pr.fence("vector", v, v.tensor_scalar(stat[:, 2:3], stat[:, 0:1],
                                                      1.0 / D, None, op0=ALU.mult))
                pr.fence("vector", v, v.tensor_tensor(stat[:, 3:4], stat[:, 2:3],
                                                      stat[:, 2:3], ALU.mult))
                return v.scalar_tensor_tensor(stat[:, 4:5], stat[:, 1:2], 1.0 / D,
                                              stat[:, 3:4], op0=ALU.mult,
                                              op1=ALU.subtract)
            vr = pr.add("vector", vr_fn, deps=[ld, sq_t])
            p1_vr[tt] = vr

            def lnp_fn(s):
                pr.fence("scalar", s, s.activation(stat[:, 5:6], stat[:, 4:5],
                                                   AF.Ln, bias=cvals[:, 0:1]))
                pr.fence("scalar", s, s.activation(stat[:, 6:7], stat[:, 5:6],
                                                   AF.Exp, scale=-0.5))
                pr.fence("scalar", s, s.activation(stat[:, 7:8], stat[:, 2:3],
                                                   AF.Copy, scale=stat[:, 6:7]))
                return s.mul(stat[:, 7:8], stat[:, 7:8], -1.0)
            lnp = pr.add("scalar", lnp_fn, deps=[vr, CONST])
            ln = pr.add("vector", lambda v: v.scalar_tensor_tensor(
                xlnbf[:], xbuf[:], stat[:, 6:7], stat[:, 7:8].to_broadcast([128, D]),
                op0=ALU.mult, op1=ALU.add),
                deps=[lnp, p1_sx[tt - 1] if tt >= 1 else None])
            p1_ln[tt] = ln
            p1_sx[tt] = pr.add("sync", lambda g_, tt=tt: g_.dma_start(
                xlnb_loc[tt * 128:(tt + 1) * 128, :], xlnbf[:]), ndma=1, stream=s_sx,
                deps=[ln])

        prev_tp = None
        prev_ev = None
        prev_sl = None
        prev_mm = None
        for tt in range(NTT):
            llx = pr.add("sync", lambda sy, tt=tt: sy.dma_start(
                xlnbf[:], xlnb_loc[tt * 128:(tt + 1) * 128, :]), ndma=1, stream=s_lx,
                deps=[p1_sx[tt], prev_tp] +
                     ([p1_ln[-1], p1_vr[-1], p1_sx[-1], p1_sx[-2]] if tt == 0 else []))
            tp = pr.add("tensor", lambda t: [
                t.transpose(ps_tr[:, dc * 128:(dc + 1) * 128],
                            xlnbf[:, dc * 128:(dc + 1) * 128], idb[:])
                for dc in range(8)], deps=[llx, CONST, prev_ev])
            prev_tp = tp
            ev = pr.add("vector", lambda v: v.tensor_copy(h1T[:, 0:1024],
                                                          ps_tr[:, 0:1024]),
                        deps=[tp, prev_mm])
            prev_ev = ev
            mm = pr.add("tensor", lambda t: [
                t.matmul(ps_sm[:], h1T[:, dc * 128:(dc + 1) * 128],
                         rwb[:, dc * NEXP:(dc + 1) * NEXP],
                         start=(dc == 0), stop=(dc == 7), skip_group_check=True)
                for dc in range(8)], deps=[ev, WLOAD, prev_sl])
            prev_mm = mm
            evl = pr.add("vector", lambda v: v.tensor_copy(lgt[:], ps_sm[:]),
                         deps=[mm])
            sl = pr.add("sync", lambda g_, tt=tt: g_.dma_start(
                lg_loc[tt * 128:(tt + 1) * 128, :], lgt[:]), ndma=1, stream=s_sl,
                deps=[evl])
            prev_sl = sl

        cc1 = pr.add("gpsimd", lambda g_: g_.collective_compute(
            "AllGather", ALU.bypass, replica_groups=[list(range(NC))],
            ins=[xlnb_loc[:, :].opt()], outs=[xlnb_full[:, :].opt()]),
            deps=[p1_sx[-1], p1_sx[-2], p1_ln[-1]])
        cc2 = pr.add("gpsimd", lambda g_: g_.collective_compute(
            "AllGather", ALU.bypass, replica_groups=[list(range(NC))],
            ins=[lg_loc[:, :].opt()], outs=[lg_full[:, :].opt()]),
            deps=[prev_sl, cc1])

        def lsc_fn(sy):
            out = []
            gq = GPP // 4
            for q4 in range(4):
                out.append(sy.dma_start(
                    sc_sb[:, q4 * gq * NEXP:(q4 + 1) * gq * NEXP]
                    .rearrange("p (g e) -> p g e", e=NEXP),
                    lg_full.rearrange("(g p) e -> p g e", p=128)
                    [:, q4 * gq:(q4 + 1) * gq, :]))
            return out
        lsc = pr.add("sync", lsc_fn, ndma=4, stream=s_lx, deps=[cc2])
        sc3 = sc_sb[:].rearrange("p (g e) -> p g e", e=NEXP)

        def sm1_fn(v):
            brep3 = brep[:].rearrange("p (o e) -> p o e", o=1).to_broadcast(
                [128, GPP, NEXP])
            pr.fence("vector", v, v.tensor_tensor(sc3, sc3, brep3, ALU.add))
            pr.fence("vector", v, v.tensor_reduce(
                rc[:, 0:GPP], sc3, axis=AXX, op=ALU.max))
            rcb = rc[:, 0:GPP].rearrange("p (g o) -> p g o", o=1).to_broadcast(
                [128, GPP, NEXP])
            return v.tensor_tensor(sc3, sc3, rcb, ALU.subtract)
        sm1 = pr.add("vector", sm1_fn, deps=[lsc, CONST])
        sm2 = pr.add("scalar", lambda s: s.activation(sc_sb[:], sc_sb[:], AF.Exp),
                     deps=[sm1])

        def sm3_fn(v):
            pr.fence("vector", v, v.tensor_reduce(
                rc[:, 0:GPP], sc3, axis=AXX, op=ALU.add))
            pr.fence("vector", v, v.reciprocal(rc[:, 0:GPP], rc[:, 0:GPP]))
            rcb = rc[:, 0:GPP].rearrange("p (g o) -> p g o", o=1).to_broadcast(
                [128, GPP, NEXP])
            return v.tensor_tensor(sc3, sc3, rcb, ALU.mult)
        sm3 = pr.add("vector", sm3_fn, deps=[sm2])
        def ssc_fn(g_):
            out = []
            gq = GPP // 4
            for q4 in range(4):
                out.append(g_.dma_start(
                    sc_h.rearrange("(g p) e -> p g e", p=128)
                    [:, q4 * gq:(q4 + 1) * gq, :],
                    sc_sb[:, q4 * gq * NEXP:(q4 + 1) * gq * NEXP]
                    .rearrange("p (g e) -> p g e", e=NEXP)))
            return out
        ssc = pr.add("sync", ssc_fn, ndma=4, stream=s_sl, deps=[sm3])

        blo, bhi = bst[:, 0:8], bst[:, 8:16]
        bm_, bsel = bst[:, 16:24], bst[:, 24:32]
        bcnt, bprt = bst[:, 32:40], bst[:, 40:48]
        bseli, bselni = bsti[:, 0:8], bsti[:, 8:16]
        init = pr.add("vector", lambda v: [
            v.memset(blo, 0.0), v.memset(bhi, 1.0), v.memset(onesb[:], 1.0)],
            deps=[sm3])
        prev_it = init
        for it in range(42):
            def itm_fn(v):
                pr.fence("vector", v, v.tensor_tensor(bm_, blo, bhi, ALU.add))
                pr.fence("vector", v, v.tensor_scalar(bm_, bm_, 0.5, None,
                                                      op0=ALU.mult))
                for e in range(NEXP):
                    sce = sc3[:, :, e:e + 1].rearrange("p g o -> p (g o)")
                    pr.fence("vector", v, v.tensor_scalar(
                        junk[:], sce, bm_[:, e:e + 1], 0.0, op0=ALU.is_gt,
                        op1=ALU.add, accum_out=bprt[:, e:e + 1]))
                return v.nop()
            itm = pr.add("vector", itm_fn, deps=[prev_it])
            summ = pr.add("tensor", lambda t: t.matmul(
                ps_sm[0:1, 0:8], onesb[:, 0:1], bprt,
                start=True, stop=True, skip_group_check=True), deps=[itm, init])
            evc = pr.add("vector", lambda v: v.tensor_copy(
                csml[0:1, :], ps_sm[0:1, 0:8]), deps=[summ])
            rep = pr.add("tensor", lambda t: t.matmul(
                ps_sm[:, 0:8], onesb[0:1, 0:128], csml[0:1, :],
                start=True, stop=True, skip_group_check=True), deps=[evc])

            def upd_fn(v):
                pr.fence("vector", v, v.tensor_copy(bcnt, ps_sm[:, 0:8]))
                pr.fence("vector", v, v.tensor_scalar(bseli, bcnt, float(KTOP),
                                                      None, op0=ALU.is_le))
                pr.fence("vector", v, v.tensor_scalar(bselni, bcnt, float(KTOP),
                                                      None, op0=ALU.is_gt))
                pr.fence("vector", v, v.select(bhi, bseli, bm_, bhi))
                pr.fence("vector", v, v.select(blo, bselni, bm_, blo))
                return v.nop()
            upd = pr.add("vector", upd_fn, deps=[rep])
            prev_it = upd

        def own_fn(v):
            for e in range(NEXP):
                sce = sc3[:, :, e:e + 1].rearrange("p g o -> p (g o)")
                if e == 0:
                    pr.fence("vector", v, v.tensor_scalar(
                        scown[:], sce, inde[:, 0:1], None, op0=ALU.mult))
                else:
                    pr.fence("vector", v, v.scalar_tensor_tensor(
                        scown[:], sce, inde[:, e:e + 1], scown[:],
                        op0=ALU.mult, op1=ALU.add))
            pr.fence("vector", v, v.tensor_tensor(bm_, bhi, inde[:], ALU.mult))
            pr.fence("vector", v, v.tensor_reduce(
                hiown[:], bm_.rearrange("p (o e) -> p o e", o=1),
                axis=AXX, op=ALU.add))
            pr.fence("vector", v, v.tensor_scalar(
                mask[:], scown[:], hiown[:, 0:1], None, op0=ALU.is_gt))
            pr.fence("vector", v, v.tensor_tensor_scan(
                rc[:, 0:GPP], onesb[:, 0:GPP], mask[:], 0.0, ALU.mult, ALU.add))
            return v.nop()
        own = pr.add("vector", own_fn, deps=[prev_it, indt])
        ocm = pr.add("tensor", lambda t: t.matmul(
            ps_sm[:, 0:1], ltri[:], rc[:, GPP - 1:GPP],
            start=True, stop=True, skip_group_check=True), deps=[own])

        def pos_fn(v):
            pr.fence("vector", v, v.tensor_copy(ocol[:], ps_sm[:, 0:1]))
            for s5 in range(64):
                pr.fence("vector", v, v.tensor_scalar(
                    junk[:], rc[:, 0:GPP], float(s5), 0.0, op0=ALU.is_le,
                    op1=ALU.add, accum_out=pos[:, s5:s5 + 1]))
            pr.fence("vector", v, v.scalar_tensor_tensor(
                gid[:], pos[:], 128.0, piota[:].to_broadcast([128, 64]),
                op0=ALU.mult, op1=ALU.add))
            pr.fence("vector", v, v.tensor_scalar(
                slot[:], pos[:], float(GPP) - 0.5, 1e9, op0=ALU.is_gt,
                op1=ALU.mult))
            pr.fence("vector", v, v.tensor_tensor(gid[:], gid[:], slot[:],
                                                  ALU.add))
            pr.fence("vector", v, v.scalar_tensor_tensor(
                slot[:], slot[:], 1.0, siota[:, 0:64], op0=ALU.mult, op1=ALU.add))
            pr.fence("vector", v, v.tensor_tensor(
                slot[:], slot[:], ocol[:].to_broadcast([128, 64]), ALU.add))
            pr.fence("vector", v, v.tensor_scalar(
                gid[:], gid[:], float(TOT - 1), None, op0=ALU.min))
            pr.fence("vector", v, v.tensor_scalar(
                slot[:], slot[:], float(KTOP), None, op0=ALU.min))
            pr.fence("vector", v, v.tensor_copy(gidi[:], gid[:]))
            pr.fence("vector", v, v.tensor_copy(sloti[:], slot[:]))
            pr.fence("vector", v, v.memset(idxpre[:], 0.0))
            pr.fence("vector", v, v.tensor_copy(idxprei[:], idxpre[:]))
            return v.nop()
        post = pr.add("vector", pos_fn, deps=[ocm, CONST])
        pre = pr.add("sync", lambda g_: [g_.dma_start(
            idx_h[jt * 128:(jt + 1) * 128].rearrange("(p o) -> p o", o=1),
            idxprei[:, jt:jt + 1]) for jt in range(NJT)],
            ndma=NJT, stream=s_sl, deps=[post])
        scts = []
        for s5 in range(64):
            sct = pr.add("gpsimd", lambda g_, s5=s5: g_.indirect_dma_start(
                idx_h.rearrange("(a b) -> a b", b=1),
                IOff(sloti[:, s5:s5 + 1], 0), gidi[:, s5:s5 + 1], None),
                ndma=1, stream=s_ix, deps=[post, pre])
            scts.append(sct)
        lidx = pr.add("sync", lambda sy: [sy.dma_start(
            idxprei[:, jt:jt + 1],
            idx_h[jt * 128:(jt + 1) * 128].rearrange("(p o) -> p o", o=1))
            for jt in range(NJT)], ndma=NJT, stream=s_lx, deps=scts)

        xgz = pr.add("vector", lambda v: v.memset(xg[:], 0.0), deps=[])
        gzz = pr.add("vector", lambda v: v.memset(g8[:], 0.0), deps=[])
        pv7 = {"tp": None, "ev": None, "mm1": None, "hr": None, "tph": None,
               "tph2": None, "evh": None, "mm2": None, "co": None, "sc": None,
               "gx": [None] * NJT, "tps": [None] * NJT, "gsel": None}
        for jt in range(NJT):
            xslot = jt % 2
            gg = pr.add("gpsimd", lambda g_, jt=jt: g_.indirect_dma_start(
                g8[:], None, sc_h, IOff(idxprei[:, jt:jt + 1], 0)),
                ndma=1, stream=s_ix, deps=[lidx, ssc, gzz, pv7["gsel"]])

            def gsel_fn(v, jt=jt):
                pr.fence("vector", v, v.tensor_tensor(
                    g8[:].rearrange("p (o e) -> p o e", o=1),
                    g8[:].rearrange("p (o e) -> p o e", o=1),
                    inde[:].rearrange("p (o e) -> p o e", o=1), ALU.mult))
                return v.tensor_reduce(
                    glist[:, jt:jt + 1],
                    g8[:].rearrange("p (o e) -> p o e", o=1), axis=AXX,
                    op=ALU.add)
            gsel = pr.add("vector", gsel_fn, deps=[gg, indt])
            pv7["gsel"] = gsel
            gx = pr.add("gpsimd", lambda g_, jt=jt, xslot=xslot: g_.indirect_dma_start(
                xg[:, xslot * D:(xslot + 1) * D], None, xlnb_full,
                IOff(idxprei[:, jt:jt + 1], 0)),
                ndma=1, stream=s_ix,
                deps=[lidx, cc1, xgz, pv7["tps"][jt - 2] if jt >= 2 else None,
                      pv7["gx"][jt - 2] if jt >= 2 else None])
            pv7["gx"][jt] = gx
            tp = pr.add("tensor", lambda t, xslot=xslot: [
                t.transpose(ps_tr[:, dc * 128:(dc + 1) * 128],
                            xg[:, xslot * D + dc * 128:xslot * D + (dc + 1) * 128],
                            idb[:])
                for dc in range(8)], deps=[gx, CONST, pv7["ev"]])
            pv7["tp"] = tp
            pv7["tps"][jt] = tp
            ev = pr.add("vector", lambda v: v.tensor_copy(xgT[:], ps_tr[:, 0:1024]),
                        deps=[tp, pv7["mm1"]])
            pv7["ev"] = ev
            for fc in range(8):
                if fc % 2 == 0:
                    lb1 = pr.add("sync", lambda sy, fc=fc: sy.dma_start(
                        b1rep[:], b1_in[0:1, fc * 512:fc * 512 + 1024]
                        .to_broadcast([128, 1024])), ndma=1, stream=s_lx,
                        deps=[pv7["hr"], p1_vr[-1]])
                mm1 = pr.add("tensor", lambda t, fc=fc: [
                    t.matmul(ps_a[:], xgT[:, dc * 128:(dc + 1) * 128],
                             w1b[:, dc * DFF + fc * 512:dc * DFF + (fc + 1) * 512],
                             start=(dc == 0), stop=(dc == 7),
                             skip_group_check=True)
                    for dc in range(8)],
                    deps=[ev, WLOAD, pv7["hr"]])
                pv7["mm1"] = mm1

                def hr_fn(v, fc=fc):
                    pr.fence("vector", v, v.scalar_tensor_tensor(
                        xbuf[:, 0:512], ps_a[:], 1.0,
                        b1rep[:, (fc % 2) * 512:(fc % 2 + 1) * 512],
                        op0=ALU.mult, op1=ALU.add))
                    return v.tensor_scalar(h1b[:, fc * 512:(fc + 1) * 512],
                                           xbuf[:, 0:512], 0.0, None, op0=ALU.max)
                hr = pr.add("vector", hr_fn, deps=[mm1, CONST, lb1, pv7["tph2"]])
                pv7["hr"] = hr
            tph = pr.add("tensor", lambda t: [
                t.transpose(ps_tr[:, i * 128:(i + 1) * 128],
                            h1b[:, i * 128:(i + 1) * 128], idb[:])
                for i in range(8)], deps=[pv7["hr"], pv7["evh"]])
            pv7["tph"] = tph
            evh = pr.add("vector", lambda v: v.tensor_copy(
                h1T[:, 0:1024], ps_tr[:, 0:1024]), deps=[tph, pv7["mm2"]])
            tph2 = pr.add("tensor", lambda t: [
                t.transpose(ps_tr[:, 1024 + i * 128:1024 + (i + 1) * 128],
                            h1b[:, 1024 + i * 128:1024 + (i + 1) * 128], idb[:])
                for i in range(8)] + [
                t.transpose(ps_tr[:, i * 128:(i + 1) * 128],
                            h1b[:, 2048 + i * 128:2048 + (i + 1) * 128], idb[:])
                for i in range(8)], deps=[pv7["hr"], evh, pv7["evh"]])
            evh2 = pr.add("vector", lambda v: [
                v.tensor_copy(h1T[:, 1024:2048], ps_tr[:, 1024:2048]),
                v.tensor_copy(h1T[:, 2048:3072], ps_tr[:, 0:1024])],
                deps=[tph2, pv7["mm2"]])
            tph3 = pr.add("tensor", lambda t: [
                t.transpose(ps_tr[:, 1024 + i * 128:1024 + (i + 1) * 128],
                            h1b[:, 3072 + i * 128:3072 + (i + 1) * 128], idb[:])
                for i in range(8)], deps=[pv7["hr"], evh2])
            pv7["tph2"] = tph3
            evh3 = pr.add("vector", lambda v: v.tensor_copy(
                h1T[:, 3072:4096], ps_tr[:, 1024:2048]), deps=[tph3, pv7["mm2"]])
            pv7["evh"] = evh3
            mm2 = pr.add("tensor", lambda t: [
                t.matmul(ps_b[:, h2 * 512:(h2 + 1) * 512],
                         h1T[:, fs * 128:(fs + 1) * 128],
                         w2b[:, fs * D + h2 * 512:fs * D + (h2 + 1) * 512],
                         start=(fs == 0), stop=(fs == 31), skip_group_check=True)
                for fs in range(32) for h2 in range(2)],
                deps=[evh, evh2, evh3, WLOAD, pv7["co"]])
            pv7["mm2"] = mm2

            def co_fn(v, jt=jt):
                pr.fence("vector", v, v.tensor_tensor(cont[:], ps_b[:, 0:1024],
                                                      b2rep[:], ALU.add))
                return v.tensor_scalar(cont[:], cont[:], glist[:, jt:jt + 1],
                                       None, op0=ALU.mult)
            co = pr.add("vector", co_fn, deps=[mm2, gsel, CONST])
            pv7["co"] = co
            sc_ = pr.add("sync", lambda g_, jt=jt: g_.dma_start(
                cont_loc[jt * 128:(jt + 1) * 128, :], cont[:]), ndma=1,
                stream=s_sl, deps=[co])
            pv7["sc"] = sc_

        zr = pr.add("vector", lambda v: v.memset(gbuf[:], 0.0), deps=[])
        zrs = pr.add("sync", lambda sy: sy.dma_start(
            cont_loc[KTOP:KTOP + 1, :], gbuf[0:1, 0:D]), ndma=1, stream=s_sl,
            deps=[zr])

        cc3 = pr.add("gpsimd", lambda g_: g_.collective_compute(
            "AllGather", ALU.bypass, replica_groups=[list(range(NC))],
            ins=[cont_loc[:, :].opt()], outs=[cont_full[:, :].opt()]),
            deps=[pv7["sc"], pv7["co"], zrs])

        prev_e = None
        for e in range(NEXP):
            def msk_fn(v, e=e):
                sce = sc3[:, :, e:e + 1].rearrange("p g o -> p (g o)")
                pr.fence("vector", v, v.tensor_scalar(
                    mask[:], sce, bst[:, 8 + e:9 + e], None, op0=ALU.is_gt))
                pr.fence("vector", v, v.tensor_tensor_scan(
                    rc[:, 0:GPP], onesb[:, 0:GPP], mask[:], 0.0,
                    ALU.mult, ALU.add))
                return v.nop()
            msk = pr.add("vector", msk_fn, deps=[prev_e or post])
            ocm2 = pr.add("tensor", lambda t: t.matmul(
                ps_sm[:, 0:1], ltri[:], rc[:, GPP - 1:GPP],
                start=True, stop=True, skip_group_check=True), deps=[msk])

            def src_fn(v, e=e):
                pr.fence("vector", v, v.tensor_copy(ocol[:], ps_sm[:, 0:1]))
                pr.fence("vector", v, v.tensor_scalar(
                    srcr[:], rc[:, 0:GPP], 1.0, float(e * (KTOP + 1)) - 1.0,
                    op0=ALU.mult, op1=ALU.add))
                pr.fence("vector", v, v.tensor_tensor(
                    srcr[:], srcr[:], ocol[:].to_broadcast([128, GPP]), ALU.add))
                pr.fence("vector", v, v.tensor_scalar(
                    scown[:], mask[:], 0.5, 1e9, op0=ALU.is_lt, op1=ALU.mult))
                pr.fence("vector", v, v.tensor_tensor(
                    srcr[:], srcr[:], scown[:], ALU.add))
                pr.fence("vector", v, v.tensor_scalar(
                    srcr[:], srcr[:], float(e * (KTOP + 1) + KTOP), None,
                    op0=ALU.min))
                pr.fence("vector", v, v.tensor_copy(
                    srciw[:, e * GPP:(e + 1) * GPP], srcr[:]))
                return v.nop()
            st_ = pr.add("vector", src_fn, deps=[ocm2])
            prev_e = st_
        def icola_fn(v):
            j3 = junk[:].rearrange("p (t b) -> p t b", b=NC)
            i3 = inde[:].rearrange("p (o e2) -> p o e2", o=1) \
                .to_broadcast([128, NTT, NC])
            for e in range(NEXP):
                view = srciw[:, e * GPP:(e + 1) * GPP].rearrange(
                    "p (b t) -> p t b", t=NTT)
                pr.fence("vector", v, v.tensor_tensor(j3, view, i3, ALU.mult))
                pr.fence("vector", v, v.tensor_reduce(
                    srcr[:, e * NTT:(e + 1) * NTT], j3, axis=AXX, op=ALU.add))
            return v.tensor_copy(icolall[:], srcr[:, 0:NEXP * NTT])
        icola = pr.add("vector", icola_fn, deps=[prev_e, indt])
        gbz = pr.add("vector", lambda v: v.memset(gbuf[:], 0.0), deps=[])
        go9 = [None] * NTT
        aclast = [None] * 4
        gi = 0
        for tt in range(NTT):
            ob = (tt % 2) * D
            lxr = pr.add("sync", lambda sy, tt=tt, ob=ob: sy.dma_start(
                obuf[:, ob:ob + D], x_in[tt * 128:(tt + 1) * 128, :]), ndma=1,
                stream=s_go,
                deps=[go9[tt - 2] if tt >= 2 else None, p1_ln[-1], p1_vr[-1]])
            acc = lxr
            for e in range(NEXP):
                sl = gi % 4
                gi += 1
                ge = pr.add("gpsimd", lambda g_, sl=sl, e=e, tt=tt:
                    g_.indirect_dma_start(
                        gbuf[:, sl * D:(sl + 1) * D], None, cont_full,
                        IOff(icolall[:, e * NTT + tt:e * NTT + tt + 1], 0)),
                    ndma=1, stream=s_ix,
                    deps=[cc3, icola, gbz, aclast[sl]])
                ac = pr.add("vector", lambda v, sl=sl, ob=ob: v.tensor_tensor(
                    obuf[:, ob:ob + D], obuf[:, ob:ob + D],
                    gbuf[:, sl * D:(sl + 1) * D], ALU.add),
                    deps=[ge, acc])
                aclast[sl] = ac
                acc = ac
            go9[tt] = pr.add("sync", lambda sy, tt=tt, ob=ob: sy.dma_start(
                xout[tt * 128:(tt + 1) * 128, :], obuf[:, ob:ob + D]), ndma=1,
                stream=s_go, deps=[acc])

        if fu:
            return _fuse_tail(nc, pr, blk, fu, pfx, last=True)
        pr.add("gpsimd", lambda g_: g_.nop(), deps=pr.barrier_deps())
        pr.emit(blk)
    return nc


def moe_in_maps(x_full, rw, rb, w1, b1, w2, b2, NC=8, pfx="", with_x=True):
    ltri = np.triu(np.ones((128, 128), np.float32), 1)
    base = {
        "rw": np.ascontiguousarray(rw), "rb": np.asarray(rb, np.float32)[None, :],
        "idb": np.eye(128, dtype=np.float32).astype(BF),
        "ltri": ltri,
        "siota": np.arange(64, dtype=np.float32)[None, :],
        "piota": np.arange(128, dtype=np.float32)[:, None],
        "cval": np.array([[LN_EPS, RMS_EPS, 1.0, 0, 0, 0, 0, 0]], dtype=np.float32),
    }
    base = {pfx + k: v for k, v in base.items()}
    in_maps = []
    for b in range(NC):
        m = dict(base)
        if with_x:
            m[pfx + "x"] = np.ascontiguousarray(x_full[b])
        m[pfx + "w1"] = np.ascontiguousarray(w1[b])
        m[pfx + "b1"] = np.ascontiguousarray(b1[b])[None, :]
        m[pfx + "w2"] = np.ascontiguousarray(w2[b])
        m[pfx + "b2"] = np.ascontiguousarray(b2[b])[None, :]
        m[pfx + "coreid"] = np.array([[b]], dtype=np.float32)
        in_maps.append(m)
    return in_maps


def run_l1_ffn(x_full, rw, rb, w1, b1, w2, b2, T=4096, raw=False):
    NC = x_full.shape[0]
    nc = build_l1_ffn(T, NC)
    in_maps = moe_in_maps(x_full, rw, rb, w1, b1, w2, b2, NC)
    res = run_bass_kernel_spmd(nc, in_maps, core_ids=list(range(NC)))
    if raw:
        return res
    return np.stack([np.asarray(r["xout"]) for r in res.results])



def build_fused(T=4096, NC=8):
    nc = bass.Bass("TRN2", target_bir_lowering=False)
    x0 = nc.dram_tensor("x0", [T, D], F32, kind="ExternalInput").ap()
    x1 = nc.dram_tensor("x1", [T, D], F32).ap()
    x2 = nc.dram_tensor("x2", [T, D], F32).ap()
    x3 = nc.dram_tensor("x3", [T, D], F32).ap()
    x4 = nc.dram_tensor("x4", [T, D], F32, kind="ExternalOutput").ap()
    with ExitStack() as sctx:
        with nc.Block() as blk:
            fu = dict(nc=nc, blk=blk, sem_ctx=sctx)
            b0 = build_l0_attn(T, fu=dict(fu, pfx="a_", x_in=x0, x_out=x1,
                                          bar_prev=None))
            b1 = build_l0_ffn(T, fu=dict(fu, pfx="f_", x_in=x1, x_out=x2,
                                         bar_prev=b0))
            b2 = build_l1_attn(T, fu=dict(fu, pfx="s_", x_in=x2, x_out=x3,
                                          bar_prev=b1))
            build_l1_ffn(T, NC, fu=dict(fu, pfx="m_", x_in=x3, x_out=x4,
                                        bar_prev=b2))
    return nc


def fused_in_maps(x_full, Wq, Wk, Wv, Wo, qnw, knw, wg, wu, wd,
                  xpw, dtw, dtb, alog, dpar, lna, lnb,
                  rw, rb, w1, b1, w2, b2, T=4096, NC=8):
    a = attn_in_maps(x_full, Wq, Wk, Wv, Wo, qnw, knw, T, pfx="a_",
                     with_x=False)
    f = ffn_in_maps(x_full, wg, wu, wd, pfx="f_", with_x=False)
    s = ssm_in_maps(x_full, xpw, dtw, dtb, alog, dpar, lna, lnb, pfx="s_",
                    with_x=False)
    m = moe_in_maps(x_full, rw, rb, w1, b1, w2, b2, NC, pfx="m_",
                    with_x=False)
    out = []
    for b in range(NC):
        d = {}
        d.update(a[b]); d.update(f[b]); d.update(s[b]); d.update(m[b])
        d["x0"] = np.ascontiguousarray(x_full[b])
        out.append(d)
    return out


def kernel_multi(**inputs):
    g = lambda k: np.asarray(inputs[k], dtype=np.float32)
    x = g("L0_attn_x")
    x = run_l0_attn(x, g("L0_attn_Wq"), g("L0_attn_Wk"), g("L0_attn_Wv"),
                    g("L0_attn_Wo"), g("L0_attn_q_norm_w"),
                    g("L0_attn_k_norm_w"), T=4096)
    x = run_l0_ffn(x, g("L0_ffn_w_gate"), g("L0_ffn_w_up"),
                   g("L0_ffn_w_down"), T=4096)
    x = run_l1_attn(x, g("L1_attn_x_proj_w"), g("L1_attn_dt_proj_w"),
                    g("L1_attn_dt_proj_b"), g("L1_attn_A_log"),
                    g("L1_attn_D_param"), g("L1_attn_ln_a"),
                    g("L1_attn_ln_b"), T=4096)
    x = run_l1_ffn(x, g("L1_ffn_router_w"), g("L1_ffn_router_b"),
                   g("L1_ffn_w1"), g("L1_ffn_b1"), g("L1_ffn_w2"),
                   g("L1_ffn_b2"), T=4096)
    return np.ascontiguousarray(x.astype(np.float32))


def kernel(**inputs):
    g = lambda k: np.asarray(inputs[k], dtype=np.float32)
    x = g("L0_attn_x")
    nc = build_fused(4096, 8)
    in_maps = fused_in_maps(
        x, g("L0_attn_Wq"), g("L0_attn_Wk"), g("L0_attn_Wv"),
        g("L0_attn_Wo"), g("L0_attn_q_norm_w"), g("L0_attn_k_norm_w"),
        g("L0_ffn_w_gate"), g("L0_ffn_w_up"), g("L0_ffn_w_down"),
        g("L1_attn_x_proj_w"), g("L1_attn_dt_proj_w"), g("L1_attn_dt_proj_b"),
        g("L1_attn_A_log"), g("L1_attn_D_param"), g("L1_attn_ln_a"),
        g("L1_attn_ln_b"), g("L1_ffn_router_w"), g("L1_ffn_router_b"),
        g("L1_ffn_w1"), g("L1_ffn_b1"), g("L1_ffn_w2"), g("L1_ffn_b2"),
        T=4096, NC=8)
    res = run_bass_kernel_spmd(nc, in_maps, core_ids=list(range(8)))
    out = np.stack([np.asarray(r["x4"]) for r in res.results])
    return np.ascontiguousarray(out.astype(np.float32))
